# version 03
# baseline (speedup 1.0000x reference)
import math
import os
from contextlib import ExitStack

os.environ.setdefault("GAUGE_NTFF_IGNORE_DMA", "1")

import numpy as np
import ml_dtypes

import concourse.bass as bass
import concourse.mybir as mybir
from concourse.bass_utils import run_bass_kernel_spmd

F32 = mybir.dt.float32
BF16 = mybir.dt.bfloat16
AF = mybir.ActivationFunctionType
ALU = mybir.AluOpType
AX = mybir.AxisListType.X

S = 4096
HID = 2048
NHI = 16
NH = 16
DH = 128
KVC = 512
NKV = 4
QC = 1536
NQC = 12
RD = 64
HF = 32
EXP = 16
I1 = 1408
NI = 11
ISZ = 2816
NIS = 22
TB = 512
NTB = 8
QBW = 512
NQB = 8
TC = 512
NCH = 8
NT = 32
INVSQ = 1.0 / math.sqrt(128.0)
LN_EPS = 1e-5
NB = 8


class Sem:
    __slots__ = ("h", "v", "name")

    def __init__(self, h, name):
        self.h = h
        self.v = 0
        self.name = name

    def inc(self, instr, n=1):
        instr.then_inc(self.h, n)
        self.v += n
        return self.v


class Bld:
    def __init__(self, nc, st):
        self.nc = nc
        self.st = st
        self.hw = {}
        self.all_sems = []

    def sem(self, name):
        s = Sem(self.st.enter_context(self.nc.semaphore(name)), name)
        self.all_sems.append(s)
        return s

    def W(self, eng, engname, sem, val):
        if val <= 0:
            return
        key = (engname, sem.name)
        if self.hw.get(key, -1) >= val:
            return
        eng.wait_ge(sem.h, val)
        self.hw[key] = val


def rope_tables():
    inv_freq = 1.0 / 10000.0 ** (np.arange(0, RD, 2, dtype=np.float64) / RD)
    t = np.arange(S, dtype=np.float64)
    fr = t[None, :] * inv_freq[:, None]
    return (np.cos(fr).astype(ml_dtypes.bfloat16),
            np.sin(fr).astype(ml_dtypes.bfloat16))


def build(phases=("p1a", "p1b", "p2", "p3", "p4"), debug=False, p4_chunks=NCH, p4_units=None):
    nc = bass.Bass("TRN2", target_bir_lowering=False)
    dk = "ExternalOutput" if debug else "Internal"

    x_in = nc.dram_tensor("L0_attn_hidden_states", [S, HID], F32, kind="ExternalInput").ap()
    W_DKV = nc.dram_tensor("L0_attn_W_DKV", [HID, KVC], F32, kind="ExternalInput").ap()
    b_DKV = nc.dram_tensor("L0_attn_b_DKV", [KVC], F32, kind="ExternalInput").ap()
    W_UK = nc.dram_tensor("L0_attn_W_UK", [KVC, HID], F32, kind="ExternalInput").ap()
    b_UK = nc.dram_tensor("L0_attn_b_UK", [HID], F32, kind="ExternalInput").ap()
    W_UV = nc.dram_tensor("L0_attn_W_UV", [KVC, HID], F32, kind="ExternalInput").ap()
    b_UV = nc.dram_tensor("L0_attn_b_UV", [HID], F32, kind="ExternalInput").ap()
    W_DQ = nc.dram_tensor("L0_attn_W_DQ", [HID, QC], F32, kind="ExternalInput").ap()
    b_DQ = nc.dram_tensor("L0_attn_b_DQ", [QC], F32, kind="ExternalInput").ap()
    W_UQ = nc.dram_tensor("L0_attn_W_UQ", [QC, HID], F32, kind="ExternalInput").ap()
    b_UQ = nc.dram_tensor("L0_attn_b_UQ", [HID], F32, kind="ExternalInput").ap()
    W_QR = nc.dram_tensor("L0_attn_W_QR", [DH, RD], F32, kind="ExternalInput").ap()
    W_KR = nc.dram_tensor("L0_attn_W_KR", [DH, RD], F32, kind="ExternalInput").ap()
    W_O = nc.dram_tensor("L0_attn_W_O", [HID, HID], F32, kind="ExternalInput").ap()
    b_O = nc.dram_tensor("L0_attn_b_O", [HID], F32, kind="ExternalInput").ap()
    gate_w = nc.dram_tensor("L0_ffn_gate_w", [EXP, HID], F32, kind="ExternalInput").ap()
    w1 = nc.dram_tensor("L0_ffn_w1", [EXP, 2 * I1, HID], F32, kind="ExternalInput").ap()
    w2 = nc.dram_tensor("L0_ffn_w2", [EXP, HID, I1], F32, kind="ExternalInput").ap()
    sh_w1 = nc.dram_tensor("L0_ffn_shared_w1", [2 * ISZ, HID], F32, kind="ExternalInput").ap()
    sh_w2 = nc.dram_tensor("L0_ffn_shared_w2", [HID, ISZ], F32, kind="ExternalInput").ap()

    y = nc.dram_tensor("y", [S, HID], F32, kind="ExternalOutput").ap()

    xs = nc.dram_tensor("xs", [NHI, S, 128], BF16, kind="Internal").ap()
    ksp = nc.dram_tensor("ksp", [NH, 128, S], BF16, kind=dk).ap()
    qsp = nc.dram_tensor("qsp", [NH, 128, S], BF16, kind=dk).ap()
    vsp = nc.dram_tensor("vsp", [S, NH, 128], BF16, kind=dk).ap()
    csp = nc.dram_tensor("csp", [NH, 128, S], BF16, kind=dk).ap()
    x1sp = nc.dram_tensor("x1sp", [NHI, 128, S], F32, kind=dk).ap()
    w1s = nc.dram_tensor("w1s", [EXP, NHI, 2 * I1, 128], BF16, kind="Internal").ap()
    w2s = nc.dram_tensor("w2s", [EXP, NI, HID, 128], BF16, kind="Internal").ap()
    sh1s = nc.dram_tensor("sh1s", [NHI, 2 * ISZ, 128], BF16, kind="Internal").ap()
    sh2s = nc.dram_tensor("sh2s", [NIS, HID, 128], BF16, kind="Internal").ap()
    gws = nc.dram_tensor("gws", [EXP, HID], BF16, kind="Internal").ap()
    cmbd = nc.dram_tensor("cmbd", [EXP, TC], BF16, kind=dk).ap()
    sdbg = nc.dram_tensor("sdbg", [NT, 128, 512], F32, kind=dk).ap() if debug else None
    pdbg = nc.dram_tensor("pdbg", [NT, 128, 512], BF16, kind=dk).ap() if debug else None

    cos_np, sin_np = rope_tables()
    cosT = nc.inline_tensor(np.ascontiguousarray(cos_np), "cosT").ap()
    sinT = nc.inline_tensor(np.ascontiguousarray(sin_np), "sinT").ap()
    identb_t = nc.inline_tensor(np.eye(128, dtype=ml_dtypes.bfloat16), "identb_t").ap()
    identf_t = nc.inline_tensor(np.eye(128, dtype=np.float32), "identf_t").ap()
    ones_row_t = nc.inline_tensor(np.ones((1, 128), dtype=ml_dtypes.bfloat16), "ones_row_t").ap()
    ones_rowf_t = nc.inline_tensor(np.ones((1, 128), dtype=np.float32), "ones_rowf_t").ap()
    ones_mat_t = nc.inline_tensor(np.ones((128, 128), dtype=ml_dtypes.bfloat16), "ones_mat_t").ap()
    onesw_t = nc.inline_tensor(np.ones((1, S), dtype=ml_dtypes.bfloat16), "onesw_t").ap()

    with ExitStack() as st:
        blk = st.enter_context(nc.Block())
        bld = Bld(nc, st)
        W = bld.W
        pe = bld.sem("pe")
        act = bld.sem("act")
        dve = bld.sem("dve")
        s_const = bld.sem("s_const")
        s_xs = bld.sem("s_xs")
        s_scr = bld.sem("s_scr")
        s_out = bld.sem("s_out")
        s_xfm = [bld.sem("s_xfm0"), bld.sem("s_xfm1")]
        s_kq = [bld.sem("s_kq0"), bld.sem("s_kq1")]
        s_w1 = [bld.sem("s_w10"), bld.sem("s_w11")]
        s_w2 = [bld.sem("s_w20"), bld.sem("s_w21")]

        e = st.enter_context
        identb = e(nc.sbuf_tensor("identb", [128, 128], BF16))
        identf = e(nc.sbuf_tensor("identf", [128, 128], F32))
        ones_row = e(nc.sbuf_tensor("ones_row", [1, 128], BF16))
        ones_rowf = e(nc.sbuf_tensor("ones_rowf", [1, 128], F32))
        ones_mat = e(nc.sbuf_tensor("ones_mat", [128, 128], BF16))
        onesw = e(nc.sbuf_tensor("onesw", [1, S], BF16))

        @blk.sync
        def _(sy):
            s_const.inc(sy.dma_start(identb[:, :], identb_t[:, :]), 16)
            s_const.inc(sy.dma_start(identf[:, :], identf_t[:, :]), 16)
            s_const.inc(sy.dma_start(ones_row[:, :], ones_row_t[:, :]), 16)
            s_const.inc(sy.dma_start(ones_rowf[:, :], ones_rowf_t[:, :]), 16)
            s_const.inc(sy.dma_start(ones_mat[:, :], ones_mat_t[:, :]), 16)
            s_const.inc(sy.dma_start(onesw[:, :], onesw_t[:, :]), 16)
        consts_qs = s_const.v

        def barrier():
            tgt = [(sm, sm.v) for sm in bld.all_sems]

            def waits(engobj, ename):
                for sm, v_ in tgt:
                    W(engobj, ename, sm, v_)

            @blk.tensor
            def _(t):
                waits(t, "t")

            @blk.scalar
            def _(s):
                waits(s, "s")

            @blk.vector
            def _(v):
                waits(v, "v")

            @blk.gpsimd
            def _(g):
                waits(g, "g")

            @blk.sync
            def _(sy):
                waits(sy, "sy")

        @blk.gpsimd
        def _(g):
            for hi in range(NHI):
                for tcn in range(8):
                    s_xs.inc(g.dma_start(
                        xs[hi, tcn * 512:(tcn + 1) * 512, :],
                        x_in[tcn * 512:(tcn + 1) * 512, hi * 128:(hi + 1) * 128]), 16)
        xs_done_qg = s_xs.v

        def p1(is_q):
            NC_ = NQC if is_q else NKV
            Wd, Wu, Wr = (W_DQ, W_UQ, W_QR) if is_q else (W_DKV, W_UK, W_KR)
            bd_d, bu_d = (b_DQ, b_UQ) if is_q else (b_DKV, b_UK)
            sp = qsp if is_q else ksp
            tag = "q" if is_q else "k"
            with ExitStack() as ps:
                e2 = ps.enter_context
                cosb = e2(nc.sbuf_tensor(f"cosb{tag}", [32, S], BF16))
                sinb = e2(nc.sbuf_tensor(f"sinb{tag}", [32, S], BF16))
                xfm = [e2(nc.sbuf_tensor(f"xfm{tag}{i}", [128, NHI * TB], BF16)) for i in range(2)]
                psA = e2(nc.psum_tensor(f"psA{tag}", [128, 512], F32))
                psB = e2(nc.psum_tensor(f"psB{tag}", [128, 512], F32))
                psC = e2(nc.psum_tensor(f"psC{tag}", [128, 512], F32))
                psD = e2(nc.psum_tensor(f"psD{tag}", [128, 512], F32))
                psF = [e2(nc.psum_tensor(f"psF{tag}{i}", [128, 512], F32)) for i in range(2)]
                wd = e2(nc.sbuf_tensor(f"wd{tag}", [128, NHI * NC_ * 128], BF16))
                wu = e2(nc.sbuf_tensor(f"wu{tag}", [128, NC_ * HID], BF16))
                wr = e2(nc.sbuf_tensor(f"wr{tag}", [128, RD], BF16))
                bd = e2(nc.sbuf_tensor(f"bd{tag}", [128, NC_], F32))
                bu = e2(nc.sbuf_tensor(f"bu{tag}", [128, NH], F32))
                cc = e2(nc.sbuf_tensor(f"cc{tag}", [128, NC_ * TB], BF16))
                kfull = [e2(nc.sbuf_tensor(f"kf{tag}{i}", [128, TB], BF16)) for i in range(2)]
                kst = [e2(nc.sbuf_tensor(f"kst{tag}{i}", [128, TB], BF16)) for i in range(2)]
                rt1 = e2(nc.sbuf_tensor(f"rt1{tag}", [32, TB], F32))
                rt2 = e2(nc.sbuf_tensor(f"rt2{tag}", [32, TB], F32))
                if not is_q:
                    wuv = e2(nc.sbuf_tensor("wuv", [128, NKV * HID], BF16))
                    buvr = e2(nc.sbuf_tensor("buvr", [1, HID], F32))
                    buvb = e2(nc.sbuf_tensor("buvb", [128, HID], F32))
                    vwide = [e2(nc.sbuf_tensor(f"vw{i}", [128, HID], BF16)) for i in range(2)]

                s_kst = [bld.sem(f"s_kst{tag}0"), bld.sem(f"s_kst{tag}1")]
                s_ldw = bld.sem(f"s_ldw{tag}")
                s_bias = bld.sem(f"s_bias{tag}")
                s_tab = bld.sem(f"s_tab{tag}")
                s_vw = [bld.sem(f"s_vw{tag}0"), bld.sem(f"s_vw{tag}1")] if not is_q else None

                @blk.gpsimd
                def _(g):
                    for hi in range(NHI):
                        s_ldw.inc(g.dma_start(
                            wd[:, hi * NC_ * 128:(hi + 1) * NC_ * 128],
                            Wd[hi * 128:(hi + 1) * 128, :]), 16)
                    for ci in range(NC_):
                        s_ldw.inc(g.dma_start(wu[:, ci * HID:(ci + 1) * HID],
                                              Wu[ci * 128:(ci + 1) * 128, :]), 16)
                    s_ldw.inc(g.dma_start(wr[:, :], Wr[:, :]), 16)
                    if not is_q:
                        for ci in range(NKV):
                            s_ldw.inc(g.dma_start(wuv[:, ci * HID:(ci + 1) * HID],
                                                  W_UV[ci * 128:(ci + 1) * 128, :]), 16)
                ld_qg = s_ldw.v

                @blk.scalar
                def _(s):
                    for ci in range(NC_):
                        s_bias.inc(s.dma_start(bd[:, ci:ci + 1],
                                               bd_d[ci * 128:(ci + 1) * 128][:, None]), 16)
                    for h in range(NH):
                        s_bias.inc(s.dma_start(bu[:, h:h + 1],
                                               bu_d[h * 128:(h + 1) * 128][:, None]), 16)
                    if not is_q:
                        s_bias.inc(s.dma_start(buvr[:, :], b_UV[None, :]), 16)
                ld_qa = s_bias.v

                @blk.sync
                def _(sy):
                    s_tab.inc(sy.dma_start(cosb[:, :], cosT[:, :]), 16)
                    s_tab.inc(sy.dma_start(sinb[:, :], sinT[:, :]), 16)
                tab_qs = s_tab.v

                @blk.vector
                def _(v):
                    for i in range(2):
                        dve.inc(v.memset(kst[i][0:32, :], 0.0))
                        if not is_q:
                            W(v, "v", dve, dve.v)
                            dve.inc(v.memset(kst[i][0:1, :], 1.0))
                kst_init = dve.v

                buv_dve = 0
                if not is_q:
                    bb_dve = []
                    for i in range(4):
                        p_ = psA if i % 2 == 0 else psB

                        @blk.tensor
                        def _(t):
                            W(t, "t", s_bias, ld_qa)
                            W(t, "t", s_const, consts_qs)
                            if i >= 2:
                                W(t, "t", dve, bb_dve[i - 2])
                            pe.inc(t.matmul(p_[:, :], ones_rowf[:, :],
                                            buvr[0:1, i * 512:(i + 1) * 512],
                                            start=True, stop=True))
                        mmv = pe.v

                        @blk.vector
                        def _(v):
                            W(v, "v", pe, mmv)
                            dve.inc(v.tensor_copy(buvb[:, i * 512:(i + 1) * 512], p_[:, :]))
                            bb_dve.append(dve.v)
                    buv_dve = dve.v

                xfm_fill = [0, 0]

                def issue_xfm(sy, tb):
                    slot = tb % 2
                    for hi in range(NHI):
                        s_xfm[slot].inc(sy.dma_start(
                            xfm[slot][:, hi * TB:(hi + 1) * TB],
                            xs[hi, tb * TB:(tb + 1) * TB, :], transpose=True), 16)
                    xfm_fill[slot] = s_xfm[slot].v

                @blk.sync
                def _(sy):
                    W(sy, "sy", s_xs, xs_done_qg)
                    issue_xfm(sy, 0)

                psAB_free = [0, 0]
                psCD_free = [("x", 0), ("x", 0)]
                psF_free = [0, 0]
                kf_free = [(0, 0), (0, 0)]
                kst_sp = [0, 0]
                vw_sp = [0, 0]
                cc_done_prev = 0

                for tb in range(NTB):
                    slot = tb % 2

                    @blk.sync
                    def _(sy):
                        if tb + 1 < NTB:
                            W(sy, "sy", pe, cc_done_prev)
                            issue_xfm(sy, tb + 1)

                    cc_mm = [0] * NC_
                    cc_ev = [0] * NC_
                    for ci in range(NC_):
                        p_ = psA if ci % 2 == 0 else psB

                        @blk.tensor
                        def _(t):
                            W(t, "t", s_xfm[slot], xfm_fill[slot])
                            W(t, "t", s_ldw, ld_qg)
                            W(t, "t", dve, buv_dve)
                            W(t, "t", act, psAB_free[ci % 2])
                            ins = None
                            for hi in range(NHI):
                                ins = t.matmul(
                                    p_[:, :],
                                    wd[:, (hi * NC_ + ci) * 128:(hi * NC_ + ci + 1) * 128],
                                    xfm[slot][:, hi * TB:(hi + 1) * TB],
                                    start=(hi == 0), stop=(hi == NHI - 1))
                            cc_mm[ci] = pe.inc(ins)

                        @blk.scalar
                        def _(s):
                            W(s, "s", pe, cc_mm[ci])
                            W(s, "s", s_bias, ld_qa)
                            cc_ev[ci] = act.inc(s.activation(
                                cc[:, ci * TB:(ci + 1) * TB], p_[:, :],
                                AF.Identity, bias=bd[:, ci:ci + 1]))
                        psAB_free[ci % 2] = cc_ev[ci]
                    cc_done_prev = cc_mm[NC_ - 1]
                    cc_all_ev = cc_ev[NC_ - 1]

                    for h in range(NH):
                        hp = h % 2
                        p_ = psC if hp == 0 else psD

                        @blk.tensor
                        def _(t):
                            W(t, "t", act, cc_all_ev)
                            eng, val = psCD_free[hp]
                            if eng == "act":
                                W(t, "t", act, val)
                            elif eng == "dve":
                                W(t, "t", dve, val)
                            ins = None
                            for ci in range(NC_):
                                ins = t.matmul(
                                    p_[:, :],
                                    wu[:, ci * HID + h * 128:ci * HID + (h + 1) * 128],
                                    cc[:, ci * TB:(ci + 1) * TB],
                                    start=(ci == 0), stop=(ci == NC_ - 1))
                            pe.inc(ins)
                        k_mm = pe.v

                        @blk.scalar
                        def _(s):
                            W(s, "s", pe, k_mm)
                            W(s, "s", pe, kf_free[hp][0])
                            W(s, "s", dve, kf_free[hp][1])
                            act.inc(s.activation(kfull[hp][:, :], p_[:, :],
                                                 AF.Identity, bias=bu[:, h:h + 1]))
                        kf_ev = act.v
                        psCD_free[hp] = ("act", kf_ev)

                        pf = psF[hp]

                        @blk.tensor
                        def _(t):
                            W(t, "t", act, kf_ev)
                            W(t, "t", dve, psF_free[hp])
                            pe.inc(t.matmul(pf[0:RD, :], wr[:, :], kfull[hp][:, :],
                                            start=True, stop=True))
                        kr_mm = pe.v

                        @blk.vector
                        def _(v):
                            W(v, "v", pe, kr_mm)
                            W(v, "v", dve, dve.v)
                            W(v, "v", s_tab, tab_qs)
                            W(v, "v", s_kst[hp], kst_sp[hp])
                            W(v, "v", dve, kst_init)
                            dve.inc(v.tensor_mul(rt1[:, :], pf[0:HF, :],
                                                 cosb[:, tb * TB:(tb + 1) * TB]))
                            dve.inc(v.tensor_mul(rt2[:, :], pf[HF:RD, :],
                                                 sinb[:, tb * TB:(tb + 1) * TB]))
                            W(v, "v", dve, dve.v)
                            dve.inc(v.tensor_sub(kst[hp][32:64, :], rt1[:, :], rt2[:, :]))
                            W(v, "v", act, kf_ev)
                            dve.inc(v.tensor_copy(kst[hp][64:128, :], kfull[hp][64:128, :]))
                        rope_done = dve.v
                        psF_free[hp] = rope_done
                        kf_free[hp] = (kr_mm, rope_done)

                        @blk.sync
                        def _(sy):
                            W(sy, "sy", dve, rope_done)
                            s_kst[hp].inc(sy.dma_start(
                                sp[h][:, tb * TB:(tb + 1) * TB], kst[hp][:, :]), 16)
                        kst_sp[hp] = s_kst[hp].v

                    if not is_q:
                        for kt in range(4):
                            vslot = (tb * 4 + kt) % 2
                            for h in range(NH):
                                hp2 = (kt * NH + h) % 2
                                p_ = psC if hp2 == 0 else psD

                                @blk.tensor
                                def _(t):
                                    eng, val = psCD_free[hp2]
                                    if eng == "act":
                                        W(t, "t", act, val)
                                    elif eng == "dve":
                                        W(t, "t", dve, val)
                                    ins = None
                                    for ci in range(NKV):
                                        ins = t.matmul(
                                            p_[:, 0:128],
                                            cc[:, ci * TB + kt * 128:ci * TB + (kt + 1) * 128],
                                            wuv[:, ci * HID + h * 128:ci * HID + (h + 1) * 128],
                                            start=(ci == 0), stop=(ci == NKV - 1))
                                    pe.inc(ins)
                                v_mm = pe.v

                                @blk.vector
                                def _(v):
                                    W(v, "v", pe, v_mm)
                                    W(v, "v", dve, buv_dve)
                                    if h == 0:
                                        W(v, "v", s_vw[vslot], vw_sp[vslot])
                                    dve.inc(v.tensor_add(
                                        vwide[vslot][:, h * 128:(h + 1) * 128],
                                        p_[:, 0:128], buvb[:, h * 128:(h + 1) * 128]))
                                psCD_free[hp2] = ("dve", dve.v)
                            vw_ready = dve.v

                            @blk.sync
                            def _(sy):
                                W(sy, "sy", dve, vw_ready)
                                s_vw[vslot].inc(sy.dma_start(
                                    vsp[tb * TB + kt * 128:tb * TB + (kt + 1) * 128, :, :]
                                    .rearrange("t h d -> t (h d)"),
                                    vwide[vslot][:, :]), 16)
                            vw_sp[vslot] = s_vw[vslot].v

        if "p1a" in phases:
            p1(False)
            barrier()
        if "p1b" in phases:
            p1(True)
            barrier()

        ffn_scratch_qg = 0
        if "p4" in phases:
            @blk.gpsimd
            def _(g):
                s_scr.inc(g.dma_start(gws[:, :], gate_w[:, :]), 16)
                for ee in range(EXP):
                    for j in range(NI):
                        for part in range(2):
                            src_base = (0 if part == 0 else I1) + j * 128
                            for hg in range(2):
                                s_scr.inc(g.dma_start(
                                    w1s[ee, hg * 8:(hg + 1) * 8,
                                        j * 256 + part * 128:j * 256 + (part + 1) * 128, :],
                                    w1[ee, src_base:src_base + 128,
                                       hg * 1024:(hg + 1) * 1024]
                                    .rearrange("o (hi p) -> hi o p", p=128)), 16)
                for ee in range(EXP):
                    for ii in range(NI):
                        for hg in range(2):
                            s_scr.inc(g.dma_start(
                                w2s[ee, ii, hg * 1024:(hg + 1) * 1024, :],
                                w2[ee, hg * 1024:(hg + 1) * 1024, ii * 128:(ii + 1) * 128]), 16)
                for j in range(NIS):
                    for part in range(2):
                        src_base = (0 if part == 0 else ISZ) + j * 128
                        for hg in range(2):
                            s_scr.inc(g.dma_start(
                                sh1s[hg * 8:(hg + 1) * 8,
                                     j * 256 + part * 128:j * 256 + (part + 1) * 128, :],
                                sh_w1[src_base:src_base + 128, hg * 1024:(hg + 1) * 1024]
                                .rearrange("o (hi p) -> hi o p", p=128)), 16)
                for ii in range(NIS):
                    for hg in range(2):
                        s_scr.inc(g.dma_start(
                            sh2s[ii, hg * 1024:(hg + 1) * 1024, :],
                            sh_w2[hg * 1024:(hg + 1) * 1024, ii * 128:(ii + 1) * 128]), 16)
            ffn_scratch_qg = s_scr.v

        if "p2" in phases:
            NQH = NQB * NT
            GTOT = NH * NQH
            CMARGIN = 1.02
            with ExitStack() as ps:
                e2 = ps.enter_context
                kb = [e2(nc.sbuf_tensor(f"kb{i}", [128, S], BF16)) for i in range(2)]
                qbb = [e2(nc.sbuf_tensor(f"qbb{i}", [128, S], BF16)) for i in range(2)]
                vb = [e2(nc.sbuf_tensor(f"vb{i}", [128, NT * 128], BF16)) for i in range(2)]
                ssb = [e2(nc.sbuf_tensor(f"ssb{i}", [128, QBW], F32)) for i in range(4)]
                prb = [e2(nc.sbuf_tensor(f"prb{i}", [128, QBW], BF16)) for i in range(4)]
                ctxs = [e2(nc.sbuf_tensor(f"ctxs{i}", [128, QBW], BF16)) for i in range(2)]
                rs_sb = e2(nc.sbuf_tensor("rs_sb", [128, QBW], F32))
                se_sb = e2(nc.sbuf_tensor("se_sb", [128, QBW], F32))
                sc_sb = e2(nc.sbuf_tensor("sc_sb", [128, QBW], F32))
                sqs = e2(nc.sbuf_tensor("sqs", [128, 2 * S], BF16))
                nrm = e2(nc.sbuf_tensor("nrm", [1, 2 * S], F32))
                nsc = e2(nc.sbuf_tensor("nsc", [1, 8], F32))
                psS = [e2(nc.psum_tensor(f"psS{i}", [128, 512], F32)) for i in range(4)]
                psC = e2(nc.psum_tensor("psC2", [128, 512], F32))
                psE = e2(nc.psum_tensor("psE2", [128, 512], F32))
                psN = [e2(nc.psum_tensor(f"psN{i}", [1, 512], F32)) for i in range(2)]
                s_ctxw = [bld.sem("s_ctxw0"), bld.sem("s_ctxw1")]
                s_dbg = bld.sem("s_dbg") if debug else None
                dbg_wm = {}

                kq_fill = [0, 0]

                def load_head(sy, h):
                    slot = h % 2
                    s_kq[slot].inc(sy.dma_start(kb[slot][:, :], ksp[h][:, :]), 16)
                    s_kq[slot].inc(sy.dma_start(qbb[slot][:, :], qsp[h][:, :]), 16)
                    for kt in range(NT):
                        s_kq[slot].inc(sy.dma_start(
                            vb[slot][:, kt * 128:(kt + 1) * 128],
                            vsp[kt * 128:(kt + 1) * 128, h, :]), 16)
                    kq_fill[slot] = s_kq[slot].v

                st_mm = {}
                copy_done = {}
                exp_pred = {}
                pv_done = {}
                sq_done = {}
                c_done = {}
                row0_done = {}
                prep_norm_pe = {}
                prod_done = {}
                psN_free = [0, 0]
                state = {"psCE_free": 0}
                ctxs_sp = [0, 0]

                def prep_s1(h):
                    slot = h % 2

                    @blk.scalar
                    def _(s):
                        W(s, "s", s_kq[slot], kq_fill[slot])
                        if h >= 1:
                            W(s, "s", pe, prep_norm_pe[h - 1])
                        act.inc(s.activation(sqs[:, 0:S], kb[slot][:, :], AF.Square))
                        act.inc(s.activation(sqs[:, S:2 * S], qbb[slot][:, :], AF.Square))
                    sq_done[h] = act.v

                def prep_norm(h):
                    for i in range(16):
                        pn = psN[i % 2]

                        @blk.tensor
                        def _(t):
                            W(t, "t", act, sq_done[h])
                            W(t, "t", s_const, consts_qs)
                            W(t, "t", dve, psN_free[i % 2])
                            pe.inc(t.matmul(pn[0:1, :], ones_mat[:, 0:1],
                                            sqs[:, i * 512:(i + 1) * 512],
                                            start=True, stop=True))
                        nm = pe.v

                        @blk.vector
                        def _(v):
                            W(v, "v", pe, nm)
                            dve.inc(v.tensor_copy(nrm[0:1, i * 512:(i + 1) * 512],
                                                  pn[0:1, :]))
                        psN_free[i % 2] = dve.v
                    prep_norm_pe[h] = pe.v

                    @blk.vector
                    def _(v):
                        if h >= 1:
                            W(v, "v", act, c_done[h - 1])
                        dve.inc(v.reduce_max(nsc[0:1, 0:1], nrm[0:1, 0:S], axis=AX))
                        dve.inc(v.reduce_max(nsc[0:1, 1:2], nrm[0:1, S:2 * S], axis=AX))
                        W(v, "v", dve, dve.v)
                        dve.inc(v.tensor_mul(nsc[0:1, 2:3], nsc[0:1, 0:1],
                                             nsc[0:1, 1:2]))
                    prod_done[h] = dve.v

                def prep_s2(h):
                    @blk.scalar
                    def _(s):
                        W(s, "s", dve, prod_done[h])
                        act.inc(s.activation(nsc[0:1, 3:4], nsc[0:1, 2:3], AF.Ln))
                        W(s, "s", act, act.v)
                        act.inc(s.activation(nsc[0:1, 4:5], nsc[0:1, 3:4], AF.Exp,
                                             scale=0.5))
                    c_done[h] = act.v

                def prep_row0(h):
                    slot = h % 2

                    @blk.vector
                    def _(v):
                        W(v, "v", act, c_done[h])
                        W(v, "v", s_kq[slot], kq_fill[slot])
                        dve.inc(v.tensor_scalar(nsc[0:1, 5:6], nsc[0:1, 4:5], -CMARGIN,
                                                None, ALU.mult))
                        W(v, "v", dve, dve.v)
                        dve.inc(v.tensor_scalar(qbb[slot][0:1, :], onesw[0:1, :],
                                                nsc[0:1, 5:6], None, ALU.mult))
                    row0_done[h] = dve.v

                def emit_pvs(t, g0):
                    W(t, "t", act, exp_pred[g0 - 1])
                    for j in range(g0 - 4, g0):
                        jslot = (j // NQH) % 2
                        jkt = j % NT
                        if jkt == 0:
                            W(t, "t", dve, state["psCE_free"])
                        t.matmul(psC[:, :], vb[jslot][:, jkt * 128:(jkt + 1) * 128],
                                 prb[j % 4][:, :],
                                 start=(jkt == 0), stop=(jkt == NT - 1))
                        ins = t.matmul(psE[:, :], ones_mat[:, :], prb[j % 4][:, :],
                                       start=(jkt == 0), stop=(jkt == NT - 1))
                        if jkt == NT - 1:
                            pv_done[j] = pe.inc(ins)

                def emit_finalize(gqb):
                    cslot = gqb % 2
                    hh = gqb // NQB
                    q0f = (gqb % NQB) * QBW

                    @blk.vector
                    def _(v):
                        W(v, "v", pe, pv_done[gqb * NT + NT - 1])
                        W(v, "v", s_ctxw[cslot], ctxs_sp[cslot])
                        dve.inc(v.tensor_copy(se_sb[:, :], psE[:, :]))
                        dve.inc(v.tensor_copy(sc_sb[:, :], psC[:, :]))
                        W(v, "v", dve, dve.v)
                        dve.inc(v.reciprocal(rs_sb[:, :], se_sb[:, :]))
                        W(v, "v", dve, dve.v)
                        dve.inc(v.tensor_mul(ctxs[cslot][:, :], sc_sb[:, :],
                                             rs_sb[:, :]))
                    state["psCE_free"] = dve.v - 2
                    ctx_ready = dve.v

                    @blk.sync
                    def _(sy):
                        W(sy, "sy", dve, ctx_ready)
                        s_ctxw[cslot].inc(sy.dma_start(
                            csp[hh][:, q0f:q0f + QBW], ctxs[cslot][:, :]), 16)
                    ctxs_sp[cslot] = s_ctxw[cslot].v

                for h in range(NH):
                    slot = h % 2
                    if h == 0:
                        @blk.sync
                        def _(sy):
                            load_head(sy, 0)
                        prep_s1(0)
                        prep_norm(0)
                        prep_s2(0)
                        prep_row0(0)
                    for qb in range(NQB):
                        base = h * NQH + qb * NT
                        q0 = qb * QBW
                        a0 = act.v
                        for kt in range(NT):
                            exp_pred[base + kt] = a0 + 4 * (kt // 4 + 1)
                        for kq4 in range(8):
                            g0 = base + kq4 * 4
                            if kq4 == 1 and (h * NQB + qb) >= 1:
                                emit_finalize(h * NQB + qb - 1)

                            @blk.tensor
                            def _(t):
                                if g0 - 4 >= 0:
                                    emit_pvs(t, g0)
                                W(t, "t", dve, row0_done[h])
                                W(t, "t", s_kq[slot], kq_fill[slot])
                                if g0 - 1 >= 0:
                                    W(t, "t", dve, copy_done[g0 - 1])
                                ins = None
                                for x in range(4):
                                    gk = g0 + x
                                    kt_ = gk % NT
                                    ins = t.matmul(psS[gk % 4][:, :],
                                                   kb[slot][:, kt_ * 128:(kt_ + 1) * 128],
                                                   qbb[slot][:, q0:q0 + QBW],
                                                   start=True, stop=True)
                                pe.inc(ins)
                            smv = pe.v
                            st_mm[g0] = smv

                            @blk.vector
                            def _(v):
                                W(v, "v", pe, smv)
                                if debug and s_dbg.v:
                                    W(v, "v", s_dbg, s_dbg.v)
                                ins = None
                                for x in range(4):
                                    gk = g0 + x
                                    ins = v.tensor_copy(ssb[gk % 4][:, :],
                                                        psS[gk % 4][:, :])
                                dve.inc(ins)
                            for x in range(4):
                                copy_done[g0 + x] = dve.v

                            if debug and h == 0 and qb == 0:
                                @blk.gpsimd
                                def _(g):
                                    W(g, "g", dve, copy_done[g0])
                                    for x in range(4):
                                        gk = g0 + x
                                        s_dbg.inc(g.dma_start(sdbg[gk % NT],
                                                              ssb[gk % 4][:, :]), 16)
                                    W(g, "g", act, exp_pred[g0 + 3] - 4)
                                    if kq4 >= 1:
                                        for x in range(4):
                                            gk = g0 - 4 + x
                                            s_dbg.inc(g.dma_start(pdbg[gk % NT],
                                                                  prb[gk % 4][:, :]), 16)
                                dbg_wm[g0] = s_dbg.v

                            if qb == 0 and kq4 == 0 and h + 1 < NH:
                                @blk.sync
                                def _(sy):
                                    if h >= 1:
                                        W(sy, "sy", pe, pv_done[h * NQH - 1])
                                        W(sy, "sy", act, sq_done[h - 1])
                                        W(sy, "sy", dve, row0_done[h - 1])
                                    load_head(sy, h + 1)

                        @blk.scalar
                        def _(s):
                            for kq4 in range(8):
                                g0 = base + kq4 * 4
                                W(s, "s", dve, copy_done[g0 + 3])
                                if debug and g0 in dbg_wm:
                                    W(s, "s", s_dbg, dbg_wm[g0])
                                for x in range(4):
                                    gk = g0 + x
                                    act.inc(s.activation(prb[gk % 4][:, :],
                                                         ssb[gk % 4][:, :],
                                                         AF.Exp, scale=INVSQ))
                        assert act.v == a0 + NT

                        if h + 1 < NH:
                            if qb == 1:
                                prep_s1(h + 1)
                            elif qb == 2:
                                prep_norm(h + 1)
                            elif qb == 3:
                                prep_s2(h + 1)
                                prep_row0(h + 1)

                @blk.tensor
                def _(t):
                    emit_pvs(t, GTOT)
                emit_finalize(NH * NQB - 1)

            barrier()

        if "p3" in phases:
            with ExitStack() as ps:
                e2 = ps.enter_context
                wo = e2(nc.sbuf_tensor("wo", [128, NHI * HID], BF16))
                bor = e2(nc.sbuf_tensor("bor", [1, HID], F32))
                bob = e2(nc.sbuf_tensor("bob", [128, HID], F32))
                ctxt = [e2(nc.sbuf_tensor(f"ctxt{i}", [128, NH * 128], BF16)) for i in range(2)]
                xres = [e2(nc.sbuf_tensor(f"xres{i}", [128, HID], F32)) for i in range(2)]
                xr = e2(nc.sbuf_tensor("xr", [128, HID], F32))
                ctr = e2(nc.sbuf_tensor("ctr", [128, HID], F32))
                sqb = e2(nc.sbuf_tensor("sqb", [128, HID], F32))
                mcol = e2(nc.sbuf_tensor("mcol", [128, 1], F32))
                vcol = e2(nc.sbuf_tensor("vcol", [128, 1], F32))
                rstd = e2(nc.sbuf_tensor("rstd", [128, 1], F32))
                x1f = [e2(nc.sbuf_tensor(f"x1f{i}", [128, 128], F32)) for i in range(2)]
                psA = e2(nc.psum_tensor("psA3", [128, 512], F32))
                psB = e2(nc.psum_tensor("psB3", [128, 512], F32))
                psT = e2(nc.psum_tensor("psT3", [128, 128], F32))
                s_ctxt = [bld.sem("s_ctxt_0"), bld.sem("s_ctxt_1")]
                s_x1w = [bld.sem("s_x1w_0"), bld.sem("s_x1w_1")]
                s_wo = bld.sem("s_wo")
                s_bo = bld.sem("s_bo")
                s_xres = [bld.sem("s_xres0"), bld.sem("s_xres1")]

                @blk.gpsimd
                def _(g):
                    for hi in range(NHI):
                        s_wo.inc(g.dma_start(wo[:, hi * HID:(hi + 1) * HID],
                                             W_O[hi * 128:(hi + 1) * 128, :]), 16)
                wo_qg = s_wo.v

                @blk.scalar
                def _(s):
                    s_bo.inc(s.dma_start(bor[:, :], b_O[None, :]), 16)
                bo_qa = s_bo.v

                bo_dve = []
                for i in range(4):
                    p_ = psA if i % 2 == 0 else psB

                    @blk.tensor
                    def _(t):
                        W(t, "t", s_bo, bo_qa)
                        W(t, "t", s_const, consts_qs)
                        if i >= 2:
                            W(t, "t", dve, bo_dve[i - 2])
                        pe.inc(t.matmul(p_[:, :], ones_rowf[:, :],
                                        bor[0:1, i * 512:(i + 1) * 512],
                                        start=True, stop=True))
                    mmv = pe.v

                    @blk.vector
                    def _(v):
                        W(v, "v", pe, mmv)
                        dve.inc(v.tensor_copy(bob[:, i * 512:(i + 1) * 512], p_[:, :]))
                        bo_dve.append(dve.v)
                bob_done = dve.v

                ctxt_fill = [0, 0]

                def load_ctxt(sy, tt):
                    slot = tt % 2
                    for h in range(NH):
                        s_ctxt[slot].inc(sy.dma_start(
                            ctxt[slot][:, h * 128:(h + 1) * 128],
                            csp[h][:, tt * 128:(tt + 1) * 128]), 16)
                    ctxt_fill[slot] = s_ctxt[slot].v

                @blk.sync
                def _(sy):
                    load_ctxt(sy, 0)

                xres_fill = [0, 0]

                @blk.scalar
                def _(s):
                    s_xres[0].inc(s.dma_start(xres[0][:, :], x_in[0:128, :]), 16)
                xres_fill[0] = s_xres[0].v

                psAB_free = [0, 0]
                ctxt_last_pe = [0, 0]
                xres_last_dve = [0, 0]
                x1w_sp = [0, 0]
                ln_ctr_free_pe = 0

                for tt in range(NT):
                    slot = tt % 2

                    @blk.sync
                    def _(sy):
                        if tt + 1 < NT:
                            W(sy, "sy", pe, ctxt_last_pe[(tt + 1) % 2])
                            load_ctxt(sy, tt + 1)

                    @blk.scalar
                    def _(s):
                        if tt + 1 < NT:
                            W(s, "s", dve, xres_last_dve[(tt + 1) % 2])
                            s_xres[(tt + 1) % 2].inc(s.dma_start(
                                xres[(tt + 1) % 2][:, :],
                                x_in[(tt + 1) * 128:(tt + 2) * 128, :]), 16)
                            xres_fill[(tt + 1) % 2] = s_xres[(tt + 1) % 2].v

                    for oc in range(4):
                        p_ = psA if oc % 2 == 0 else psB

                        @blk.tensor
                        def _(t):
                            W(t, "t", s_ctxt[slot], ctxt_fill[slot])
                            W(t, "t", s_wo, wo_qg)
                            W(t, "t", dve, bob_done)
                            W(t, "t", dve, psAB_free[oc % 2])
                            ins = None
                            for hh in range(NH):
                                ins = t.matmul(
                                    p_[:, :],
                                    ctxt[slot][:, hh * 128:(hh + 1) * 128],
                                    wo[:, hh * HID + oc * 512:hh * HID + (oc + 1) * 512],
                                    start=(hh == 0), stop=(hh == NH - 1))
                            pe.inc(ins)
                        mm_oc = pe.v
                        ctxt_last_pe[slot] = mm_oc

                        @blk.vector
                        def _(v):
                            W(v, "v", pe, mm_oc)
                            W(v, "v", dve, dve.v)
                            W(v, "v", dve, bob_done)
                            W(v, "v", s_xres[slot], xres_fill[slot])
                            dve.inc(v.tensor_add(xr[:, oc * 512:(oc + 1) * 512], p_[:, :],
                                                 bob[:, oc * 512:(oc + 1) * 512]))
                            W(v, "v", dve, dve.v)
                            dve.inc(v.tensor_add(xr[:, oc * 512:(oc + 1) * 512],
                                                 xr[:, oc * 512:(oc + 1) * 512],
                                                 xres[slot][:, oc * 512:(oc + 1) * 512]))
                        psAB_free[oc % 2] = dve.v
                    xres_last_dve[slot] = dve.v

                    @blk.vector
                    def _(v):
                        W(v, "v", dve, dve.v)
                        W(v, "v", pe, ln_ctr_free_pe)
                        dve.inc(v.reduce_sum(mcol[:, :], xr[:, :], axis=AX))
                        W(v, "v", dve, dve.v)
                        dve.inc(v.tensor_scalar(mcol[:, :], mcol[:, :],
                                                1.0 / HID, None, ALU.mult))
                        W(v, "v", dve, dve.v)
                        dve.inc(v.tensor_scalar(ctr[:, :], xr[:, :], mcol[:, 0:1],
                                                None, ALU.subtract))
                        W(v, "v", dve, dve.v)
                        dve.inc(v.tensor_mul(sqb[:, :], ctr[:, :], ctr[:, :]))
                        W(v, "v", dve, dve.v)
                        dve.inc(v.reduce_sum(vcol[:, :], sqb[:, :], axis=AX))
                        W(v, "v", dve, dve.v)
                        dve.inc(v.tensor_scalar(vcol[:, :], vcol[:, :],
                                                1.0 / HID, LN_EPS, ALU.mult, ALU.add))
                    var_done = dve.v

                    @blk.scalar
                    def _(s):
                        W(s, "s", dve, var_done)
                        act.inc(s.activation(rstd[:, :], vcol[:, :], AF.Ln))
                        W(s, "s", act, act.v)
                        act.inc(s.activation(rstd[:, :], rstd[:, :], AF.Exp, scale=-0.5))
                    rstd_done = act.v

                    @blk.vector
                    def _(v):
                        W(v, "v", act, rstd_done)
                        dve.inc(v.tensor_scalar(ctr[:, :], ctr[:, :], rstd[:, 0:1],
                                                None, ALU.mult))
                    x1_done = dve.v

                    psT_free = 0
                    for hi in range(NHI):
                        fslot = hi % 2

                        @blk.tensor
                        def _(t):
                            W(t, "t", dve, x1_done)
                            W(t, "t", dve, psT_free)
                            pe.inc(t.transpose(psT[:, :], ctr[:, hi * 128:(hi + 1) * 128],
                                               identf[:, :]))
                        tr_mm = pe.v

                        @blk.vector
                        def _(v):
                            W(v, "v", pe, tr_mm)
                            W(v, "v", s_x1w[fslot], x1w_sp[fslot])
                            dve.inc(v.tensor_copy(x1f[fslot][:, :], psT[:, :]))
                        psT_free = dve.v

                        @blk.sync
                        def _(sy):
                            W(sy, "sy", dve, psT_free)
                            s_x1w[fslot].inc(sy.dma_start(
                                x1sp[hi][:, tt * 128:(tt + 1) * 128], x1f[fslot][:, :]), 16)
                        x1w_sp[fslot] = s_x1w[fslot].v
                    ln_ctr_free_pe = pe.v

            barrier()

        if "p4" in phases:
            with ExitStack() as ps:
                e2 = ps.enter_context
                acc = e2(nc.sbuf_tensor("acc", [128, NHI * TC], F32))
                x1b = e2(nc.sbuf_tensor("x1b", [128, NHI * TC], BF16))
                w1b = [e2(nc.sbuf_tensor(f"w1b{i}", [128, NHI * 768], BF16)) for i in range(2)]
                w2b = [e2(nc.sbuf_tensor(f"w2b{i}", [128, 6 * HID], BF16)) for i in range(2)]
                actb = e2(nc.sbuf_tensor("actb", [128, NI * TC], BF16))
                tsil = [e2(nc.sbuf_tensor(f"tsil{i}", [128, TC], BF16)) for i in range(2)]
                gwT = e2(nc.sbuf_tensor("gwT", [128, NHI * EXP], BF16))
                probs = e2(nc.sbuf_tensor("probs", [128, EXP], F32))
                wrk = e2(nc.sbuf_tensor("wrk4", [128, EXP], F32))
                msk = e2(nc.sbuf_tensor("msk", [128, EXP], F32))
                sel = e2(nc.sbuf_tensor("sel", [128, EXP], F32))
                cmb = e2(nc.sbuf_tensor("cmb", [128, EXP], F32))
                cmb_bf = e2(nc.sbuf_tensor("cmb_bf", [128, EXP], BF16))
                cmbT = e2(nc.sbuf_tensor("cmbT", [16, TC], BF16))
                cbrow = e2(nc.sbuf_tensor("cbrow", [1, TC], BF16))
                cbb = e2(nc.sbuf_tensor("cbb", [128, TC], F32))
                m1c = e2(nc.sbuf_tensor("m1c", [128, 1], F32))
                negm = e2(nc.sbuf_tensor("negm", [128, 1], F32))
                smc = e2(nc.sbuf_tensor("smc", [128, 1], F32))
                rrc = e2(nc.sbuf_tensor("rrc", [128, 1], F32))
                otm = e2(nc.sbuf_tensor("otm", [128, HID], F32))
                fsq = e2(nc.sbuf_tensor("fsq", [128, HID], F32))
                fmc = e2(nc.sbuf_tensor("fmc", [128, 1], F32))
                fvc = e2(nc.sbuf_tensor("fvc", [128, 1], F32))
                frs = e2(nc.sbuf_tensor("frs", [128, 1], F32))
                psA = e2(nc.psum_tensor("psA4", [128, 512], F32))
                psB = e2(nc.psum_tensor("psB4", [128, 512], F32))
                psC = e2(nc.psum_tensor("psC4", [128, 512], F32))
                psD = e2(nc.psum_tensor("psD4", [128, 512], F32))
                psE = e2(nc.psum_tensor("psE4", [128, 512], F32))
                psT = e2(nc.psum_tensor("psT4", [128, 128], F32))
                psTb = e2(nc.psum_tensor("psTb4", [128, 128], BF16))
                s_cmb = bld.sem("s_cmb")
                s_gwT = bld.sem("s_gwT")
                s_acc = bld.sem("s_acc")
                s_cbr = bld.sem("s_cbr")

                @blk.sync
                def _(sy):
                    W(sy, "sy", s_scr, ffn_scratch_qg)
                    for hi in range(NHI):
                        s_gwT.inc(sy.dma_start(gwT[:, hi * EXP:(hi + 1) * EXP],
                                               gws[:, hi * 128:(hi + 1) * 128],
                                               transpose=True), 16)
                gwT_qs = s_gwT.v

                def w1_src_moe(ee):
                    return lambda hi, r0, r1: w1s[ee, hi, r0:r1, :]

                def w2_src_moe(ee):
                    return lambda ii: w2s[ee, ii]

                def w1_src_sh(off):
                    return lambda hi, r0, r1: sh1s[hi, off + r0:off + r1, :]

                def w2_src_sh(off):
                    return lambda ii: sh2s[off + ii]

                units = [dict(w1f=w1_src_moe(ee), w2f=w2_src_moe(ee), cb=ee)
                         for ee in range(EXP)]
                units.append(dict(w1f=w1_src_sh(0), w2f=w2_src_sh(0), cb=None))
                units.append(dict(w1f=w1_src_sh(NI * 256), w2f=w2_src_sh(NI), cb=None))
                if p4_units is not None:
                    units = units[:p4_units]
                nunits = len(units)
                SETS = [(0, 3), (3, 6), (6, 9), (9, 11)]
                W2G = [(0, 6), (6, 11)]

                w1_fill = [0, 0]
                w1_free = [0, 0]
                w2_fill = [0, 0]
                w2_free = [0, 0]
                acc_free_pe = 0
                x1b_free_pe = 0
                psE_free = ("x", 0)
                psA_free_act = 0
                psB_free_dve = 0
                psCD_free = [("x", 0), ("x", 0)]
                acc_w_dve = [0] * NHI
                cb_free_dve = 0
                silu_free = [0, 0]
                actb_used_pe = 0
                out_qs_prev = 0
                psTb_free = 0
                psT_free2 = 0

                def issue_w1(sy, u, si, slot):
                    p0_, p1_ = SETS[si]
                    W(sy, "sy", pe, w1_free[slot])
                    for hi in range(NHI):
                        s_w1[slot].inc(sy.dma_start(
                            w1b[slot][:, hi * 768:hi * 768 + (p1_ - p0_) * 256],
                            units[u]["w1f"](hi, p0_ * 256, p1_ * 256), transpose=True), 16)
                    w1_fill[slot] = s_w1[slot].v

                def issue_w2(sy, u, gi, slot):
                    i0, i1_ = W2G[gi]
                    W(sy, "sy", pe, w2_free[slot])
                    for ii in range(i0, i1_):
                        s_w2[slot].inc(sy.dma_start(
                            w2b[slot][:, (ii - i0) * HID:(ii - i0 + 1) * HID],
                            units[u]["w2f"](ii), transpose=True), 16)
                    w2_fill[slot] = s_w2[slot].v

                for ch in range(p4_chunks):
                    c0 = ch * TC

                    @blk.scalar
                    def _(s):
                        W(s, "s", pe, acc_free_pe)
                        for hi in range(NHI):
                            s_acc.inc(s.dma_start(acc[:, hi * TC:(hi + 1) * TC],
                                                  x1sp[hi][:, c0:c0 + TC]), 16)
                    acci_qa = s_acc.v

                    @blk.scalar
                    def _(s):
                        W(s, "s", s_acc, acci_qa)
                        W(s, "s", pe, x1b_free_pe)
                        for hi in range(NHI):
                            act.inc(s.activation(x1b[:, hi * TC:(hi + 1) * TC],
                                                 acc[:, hi * TC:(hi + 1) * TC], AF.Copy))
                    x1b_done = act.v

                    for tt in range(4):
                        @blk.tensor
                        def _(t):
                            W(t, "t", act, x1b_done)
                            W(t, "t", s_gwT, gwT_qs)
                            eng, val = psE_free
                            if eng == "s":
                                W(t, "t", act, val)
                            elif eng == "v":
                                W(t, "t", dve, val)
                            ins = None
                            for hi in range(NHI):
                                ins = t.matmul(
                                    psE[:, 0:EXP],
                                    x1b[:, hi * TC + tt * 128:hi * TC + (tt + 1) * 128],
                                    gwT[:, hi * EXP:(hi + 1) * EXP],
                                    start=(hi == 0), stop=(hi == NHI - 1))
                            pe.inc(ins)
                        lg_mm = pe.v

                        @blk.vector
                        def _(v):
                            W(v, "v", pe, lg_mm)
                            dve.inc(v.reduce_max(m1c[:, :], psE[:, 0:EXP], axis=AX))
                            W(v, "v", dve, dve.v)
                            dve.inc(v.tensor_scalar(negm[:, :], m1c[:, :], -1.0,
                                                    None, ALU.mult))
                        negm_done = dve.v

                        @blk.scalar
                        def _(s):
                            W(s, "s", dve, negm_done)
                            act.inc(s.activation(probs[:, :], psE[:, 0:EXP], AF.Exp,
                                                 bias=negm[:, 0:1]))
                        probs_done = act.v
                        psE_free = ("s", probs_done)

                        @blk.vector
                        def _(v):
                            W(v, "v", act, probs_done)
                            dve.inc(v.reduce_sum(smc[:, :], probs[:, :], axis=AX))
                            W(v, "v", dve, dve.v)
                            dve.inc(v.reciprocal(rrc[:, :], smc[:, :]))
                            W(v, "v", dve, dve.v)
                            dve.inc(v.tensor_scalar(probs[:, :], probs[:, :], rrc[:, 0:1],
                                                    None, ALU.mult))
                            W(v, "v", dve, dve.v)
                            dve.inc(v.tensor_copy(wrk[:, :], probs[:, :]))
                            dve.inc(v.memset(cmb[:, :], 0.0))
                            for _k in range(6):
                                W(v, "v", dve, dve.v)
                                dve.inc(v.reduce_max(m1c[:, :], wrk[:, :], axis=AX))
                                W(v, "v", dve, dve.v)
                                dve.inc(v.tensor_scalar(msk[:, :], wrk[:, :], m1c[:, 0:1],
                                                        None, ALU.is_ge))
                                W(v, "v", dve, dve.v)
                                dve.inc(v.tensor_mul(sel[:, :], wrk[:, :], msk[:, :]))
                                W(v, "v", dve, dve.v)
                                dve.inc(v.tensor_add(cmb[:, :], cmb[:, :], sel[:, :]))
                                dve.inc(v.tensor_sub(wrk[:, :], wrk[:, :], sel[:, :]))
                            W(v, "v", dve, dve.v)
                            dve.inc(v.tensor_copy(cmb_bf[:, :], cmb[:, :]))
                        cmb_done = dve.v

                        @blk.tensor
                        def _(t):
                            W(t, "t", dve, cmb_done)
                            W(t, "t", dve, psTb_free)
                            pe.inc(t.transpose(psTb[0:16, :], cmb_bf[:, :], identb[:, :]))
                        trc_mm = pe.v

                        @blk.vector
                        def _(v):
                            W(v, "v", pe, trc_mm)
                            dve.inc(v.tensor_copy(cmbT[0:16, tt * 128:(tt + 1) * 128],
                                                  psTb[0:16, :]))
                        psTb_free = dve.v
                    cmbT_done = dve.v

                    @blk.sync
                    def _(sy):
                        W(sy, "sy", dve, cmbT_done)
                        W(sy, "sy", s_cbr, cb_last_qa if ch > 0 else 0)
                        s_cmb.inc(sy.dma_start(cmbd[:, :], cmbT[:, :]), 16)
                        if nunits > 0 and ch == 0:
                            issue_w1(sy, 0, 0, 0)
                            issue_w1(sy, 0, 1, 1)
                            issue_w2(sy, 0, 0, 0)
                    cmb_sp = s_cmb.v

                    cb_done_act = act.v

                    for u in range(nunits):
                        cbi = units[u]["cb"]
                        if cbi is not None:
                            @blk.scalar
                            def _(s):
                                W(s, "s", s_cmb, cmb_sp)
                                W(s, "s", act, cb_done_act)
                                s_cbr.inc(s.dma_start(cbrow[:, :], cmbd[cbi:cbi + 1, :]), 16)
                            cbrow_qa = s_cbr.v
                            cb_last_qa = s_cbr.v

                            @blk.tensor
                            def _(t):
                                W(t, "t", s_cbr, cbrow_qa)
                                eng, val = psE_free
                                if eng == "s":
                                    W(t, "t", act, val)
                                elif eng == "v":
                                    W(t, "t", dve, val)
                                pe.inc(t.matmul(psE[:, :], ones_row[:, :], cbrow[0:1, :],
                                                start=True, stop=True))
                            cb_mm = pe.v

                            @blk.scalar
                            def _(s):
                                W(s, "s", pe, cb_mm)
                                W(s, "s", dve, cb_free_dve)
                                act.inc(s.activation(cbb[:, :], psE[:, :], AF.Copy))
                            cb_done_act = act.v
                            psE_free = ("s", cb_done_act)

                        for si in range(4):
                            sslot = si % 2
                            p0_, p1_ = SETS[si]
                            for j in range(p0_, p1_):
                                jl = j - p0_

                                @blk.tensor
                                def _(t):
                                    if j == p0_:
                                        W(t, "t", s_w1[sslot], w1_fill[sslot])
                                    W(t, "t", act, psA_free_act)
                                    ins = None
                                    for hi in range(NHI):
                                        ins = t.matmul(
                                            psA[:, :],
                                            w1b[sslot][:, hi * 768 + jl * 256:hi * 768 + jl * 256 + 128],
                                            x1b[:, hi * TC:(hi + 1) * TC],
                                            start=(hi == 0), stop=(hi == NHI - 1))
                                    pe.inc(ins)
                                    W(t, "t", dve, psB_free_dve)
                                    ins = None
                                    for hi in range(NHI):
                                        ins = t.matmul(
                                            psB[:, :],
                                            w1b[sslot][:, hi * 768 + jl * 256 + 128:hi * 768 + (jl + 1) * 256],
                                            x1b[:, hi * TC:(hi + 1) * TC],
                                            start=(hi == 0), stop=(hi == NHI - 1))
                                    pe.inc(ins)
                                gu_mm = pe.v
                                w1_free[sslot] = gu_mm

                                tslot = j % 2

                                @blk.scalar
                                def _(s):
                                    W(s, "s", pe, gu_mm - 1)
                                    W(s, "s", dve, silu_free[tslot])
                                    act.inc(s.activation(tsil[tslot][:, :], psA[:, :],
                                                         AF.Silu))
                                psA_free_act = act.v
                                silu_done = act.v

                                @blk.vector
                                def _(v):
                                    W(v, "v", act, silu_done)
                                    W(v, "v", pe, gu_mm)
                                    W(v, "v", pe, actb_used_pe)
                                    if cbi is not None:
                                        W(v, "v", act, cb_done_act)
                                        dve.inc(v.tensor_mul(actb[:, j * TC:(j + 1) * TC],
                                                             tsil[tslot][:, :], psB[:, :]))
                                        W(v, "v", dve, dve.v)
                                        dve.inc(v.tensor_mul(actb[:, j * TC:(j + 1) * TC],
                                                             actb[:, j * TC:(j + 1) * TC],
                                                             cbb[:, :]))
                                    else:
                                        dve.inc(v.tensor_mul(actb[:, j * TC:(j + 1) * TC],
                                                             tsil[tslot][:, :], psB[:, :]))
                                psB_free_dve = dve.v
                                silu_free[tslot] = dve.v

                            @blk.sync
                            def _(sy):
                                if si + 2 <= 3:
                                    issue_w1(sy, u, si + 2, (si + 2) % 2)
                                elif u + 1 < nunits:
                                    nsi = si + 2 - 4
                                    issue_w1(sy, u + 1, nsi, nsi % 2)
                                elif ch + 1 < p4_chunks:
                                    nsi = si + 2 - 4
                                    issue_w1(sy, 0, nsi, nsi % 2)
                        act_done_dve = dve.v
                        cb_free_dve = dve.v

                        for gi in range(2):
                            gslot = gi % 2
                            i0, i1_ = W2G[gi]
                            for ht in range(NHI):
                                pw = psC if ht % 2 == 0 else psD

                                @blk.tensor
                                def _(t):
                                    if ht == 0:
                                        W(t, "t", s_w2[gslot], w2_fill[gslot])
                                        W(t, "t", dve, act_done_dve)
                                    eng, val = psCD_free[ht % 2]
                                    if eng == "v":
                                        W(t, "t", dve, val)
                                    ins = None
                                    for ii in range(i0, i1_):
                                        ins = t.matmul(
                                            pw[:, :],
                                            w2b[gslot][:, (ii - i0) * HID + ht * 128:(ii - i0) * HID + (ht + 1) * 128],
                                            actb[:, ii * TC:(ii + 1) * TC],
                                            start=(ii == i0), stop=(ii == i1_ - 1))
                                    pe.inc(ins)
                                w2_mm = pe.v
                                w2_free[gslot] = w2_mm
                                if gi == 1:
                                    actb_used_pe = w2_mm

                                @blk.vector
                                def _(v):
                                    W(v, "v", pe, w2_mm)
                                    W(v, "v", dve, acc_w_dve[ht])
                                    W(v, "v", s_acc, acci_qa)
                                    dve.inc(v.tensor_add(acc[:, ht * TC:(ht + 1) * TC],
                                                         acc[:, ht * TC:(ht + 1) * TC],
                                                         pw[:, :]))
                                acc_w_dve[ht] = dve.v
                                psCD_free[ht % 2] = ("v", dve.v)

                            @blk.sync
                            def _(sy):
                                if gi + 1 < 2:
                                    issue_w2(sy, u, gi + 1, 1)
                                elif u + 1 < nunits:
                                    issue_w2(sy, u + 1, 0, 0)
                                elif ch + 1 < p4_chunks:
                                    issue_w2(sy, 0, 0, 0)

                    all_acc_dve = dve.v
                    for tt in range(4):
                        for ht in range(NHI):
                            @blk.tensor
                            def _(t):
                                W(t, "t", dve, all_acc_dve)
                                W(t, "t", dve, psT_free2)
                                pe.inc(t.transpose(
                                    psT[:, :],
                                    acc[:, ht * TC + tt * 128:ht * TC + (tt + 1) * 128],
                                    identf[:, :]))
                            trm = pe.v

                            @blk.vector
                            def _(v):
                                W(v, "v", pe, trm)
                                if ht == 0:
                                    W(v, "v", dve, dve.v)
                                    W(v, "v", s_out, out_qs_prev)
                                dve.inc(v.tensor_copy(otm[:, ht * 128:(ht + 1) * 128],
                                                      psT[:, :]))
                            psT_free2 = dve.v
                        otm_done = dve.v

                        @blk.vector
                        def _(v):
                            W(v, "v", dve, otm_done)
                            dve.inc(v.reduce_sum(fmc[:, :], otm[:, :], axis=AX))
                            W(v, "v", dve, dve.v)
                            dve.inc(v.tensor_scalar(fmc[:, :], fmc[:, :], 1.0 / HID,
                                                    None, ALU.mult))
                            W(v, "v", dve, dve.v)
                            dve.inc(v.tensor_scalar(otm[:, :], otm[:, :], fmc[:, 0:1],
                                                    None, ALU.subtract))
                            W(v, "v", dve, dve.v)
                            dve.inc(v.tensor_mul(fsq[:, :], otm[:, :], otm[:, :]))
                            W(v, "v", dve, dve.v)
                            dve.inc(v.reduce_sum(fvc[:, :], fsq[:, :], axis=AX))
                            W(v, "v", dve, dve.v)
                            dve.inc(v.tensor_scalar(fvc[:, :], fvc[:, :], 1.0 / HID,
                                                    LN_EPS, ALU.mult, ALU.add))
                        fvar_done = dve.v

                        @blk.scalar
                        def _(s):
                            W(s, "s", dve, fvar_done)
                            act.inc(s.activation(frs[:, :], fvc[:, :], AF.Ln))
                            W(s, "s", act, act.v)
                            act.inc(s.activation(frs[:, :], frs[:, :], AF.Exp, scale=-0.5))
                        frs_done = act.v

                        @blk.vector
                        def _(v):
                            W(v, "v", act, frs_done)
                            dve.inc(v.tensor_scalar(otm[:, :], otm[:, :], frs[:, 0:1],
                                                    None, ALU.mult))
                        out_done = dve.v

                        @blk.sync
                        def _(sy):
                            W(sy, "sy", dve, out_done)
                            s_out.inc(sy.dma_start(
                                y[c0 + tt * 128:c0 + (tt + 1) * 128, :], otm[:, :]), 16)
                        out_qs_prev = s_out.v
                    acc_free_pe = pe.v
                    x1b_free_pe = pe.v

        @blk.sync
        def _(sy):
            for sm in bld.all_sems:
                W(sy, "sy", sm, sm.v)

    return nc


_IN_NAMES = [
    "L0_attn_hidden_states", "L0_attn_W_DKV", "L0_attn_b_DKV", "L0_attn_W_UK",
    "L0_attn_b_UK", "L0_attn_W_UV", "L0_attn_b_UV", "L0_attn_W_DQ", "L0_attn_b_DQ",
    "L0_attn_W_UQ", "L0_attn_b_UQ", "L0_attn_W_QR", "L0_attn_W_KR", "L0_attn_W_O",
    "L0_attn_b_O", "L0_ffn_gate_w", "L0_ffn_w1", "L0_ffn_w2", "L0_ffn_shared_w1",
    "L0_ffn_shared_w2",
]


def kernel(**inputs):
    nc = build()
    x = np.ascontiguousarray(np.asarray(inputs["L0_attn_hidden_states"], np.float32))
    weights = {k: np.ascontiguousarray(np.asarray(inputs[k], np.float32))
               for k in _IN_NAMES if k != "L0_attn_hidden_states"}
    in_maps = []
    for b in range(NB):
        m = dict(weights)
        m["L0_attn_hidden_states"] = x[b]
        in_maps.append(m)
    res = run_bass_kernel_spmd(nc, in_maps, core_ids=list(range(NB)))
    return np.stack([np.asarray(res.results[b]["y"], np.float32) for b in range(NB)], axis=0)


# version 04
# speedup vs baseline: 1.1032x; 1.1032x over previous
import math
import os
from contextlib import ExitStack

os.environ.setdefault("GAUGE_NTFF_IGNORE_DMA", "1")

import numpy as np
import ml_dtypes

import concourse.bass as bass
import concourse.mybir as mybir
from concourse.bass_utils import run_bass_kernel_spmd

F32 = mybir.dt.float32
BF16 = mybir.dt.bfloat16
AF = mybir.ActivationFunctionType
ALU = mybir.AluOpType
AX = mybir.AxisListType.X

S = 4096
HID = 2048
NHI = 16
NH = 16
DH = 128
KVC = 512
NKV = 4
QC = 1536
NQC = 12
RD = 64
HF = 32
EXP = 16
I1 = 1408
NI = 11
ISZ = 2816
NIS = 22
TB = 512
NTB = 8
QBW = 512
NQB = 8
TC = 512
NCH = 8
NT = 32
INVSQ = 1.0 / math.sqrt(128.0)
LN_EPS = 1e-5
NB = 8


class Sem:
    __slots__ = ("h", "v", "name")

    def __init__(self, h, name):
        self.h = h
        self.v = 0
        self.name = name

    def inc(self, instr, n=1):
        instr.then_inc(self.h, n)
        self.v += n
        return self.v


class Bld:
    def __init__(self, nc, st):
        self.nc = nc
        self.st = st
        self.hw = {}
        self.all_sems = []

    def sem(self, name):
        s = Sem(self.st.enter_context(self.nc.semaphore(name)), name)
        self.all_sems.append(s)
        return s

    def W(self, eng, engname, sem, val):
        if val <= 0:
            return
        key = (engname, sem.name)
        if self.hw.get(key, -1) >= val:
            return
        eng.wait_ge(sem.h, val)
        self.hw[key] = val


def rope_tables():
    inv_freq = 1.0 / 10000.0 ** (np.arange(0, RD, 2, dtype=np.float64) / RD)
    t = np.arange(S, dtype=np.float64)
    fr = t[None, :] * inv_freq[:, None]
    return (np.cos(fr).astype(ml_dtypes.bfloat16),
            np.sin(fr).astype(ml_dtypes.bfloat16))


def build(phases=("p1a", "p1b", "p2", "p3", "p4"), debug=False, p4_chunks=NCH, p4_units=None):
    nc = bass.Bass("TRN2", target_bir_lowering=False)
    dk = "ExternalOutput" if debug else "Internal"

    x_in = nc.dram_tensor("L0_attn_hidden_states", [S, HID], F32, kind="ExternalInput").ap()
    W_DKV = nc.dram_tensor("L0_attn_W_DKV", [HID, KVC], F32, kind="ExternalInput").ap()
    b_DKV = nc.dram_tensor("L0_attn_b_DKV", [KVC], F32, kind="ExternalInput").ap()
    W_UK = nc.dram_tensor("L0_attn_W_UK", [KVC, HID], F32, kind="ExternalInput").ap()
    b_UK = nc.dram_tensor("L0_attn_b_UK", [HID], F32, kind="ExternalInput").ap()
    W_UV = nc.dram_tensor("L0_attn_W_UV", [KVC, HID], F32, kind="ExternalInput").ap()
    b_UV = nc.dram_tensor("L0_attn_b_UV", [HID], F32, kind="ExternalInput").ap()
    W_DQ = nc.dram_tensor("L0_attn_W_DQ", [HID, QC], F32, kind="ExternalInput").ap()
    b_DQ = nc.dram_tensor("L0_attn_b_DQ", [QC], F32, kind="ExternalInput").ap()
    W_UQ = nc.dram_tensor("L0_attn_W_UQ", [QC, HID], F32, kind="ExternalInput").ap()
    b_UQ = nc.dram_tensor("L0_attn_b_UQ", [HID], F32, kind="ExternalInput").ap()
    W_QR = nc.dram_tensor("L0_attn_W_QR", [DH, RD], F32, kind="ExternalInput").ap()
    W_KR = nc.dram_tensor("L0_attn_W_KR", [DH, RD], F32, kind="ExternalInput").ap()
    W_O = nc.dram_tensor("L0_attn_W_O", [HID, HID], F32, kind="ExternalInput").ap()
    b_O = nc.dram_tensor("L0_attn_b_O", [HID], F32, kind="ExternalInput").ap()
    gate_w = nc.dram_tensor("L0_ffn_gate_w", [EXP, HID], F32, kind="ExternalInput").ap()
    w1 = nc.dram_tensor("L0_ffn_w1", [EXP, 2 * I1, HID], F32, kind="ExternalInput").ap()
    w2 = nc.dram_tensor("L0_ffn_w2", [EXP, HID, I1], F32, kind="ExternalInput").ap()
    sh_w1 = nc.dram_tensor("L0_ffn_shared_w1", [2 * ISZ, HID], F32, kind="ExternalInput").ap()
    sh_w2 = nc.dram_tensor("L0_ffn_shared_w2", [HID, ISZ], F32, kind="ExternalInput").ap()

    y = nc.dram_tensor("y", [S, HID], F32, kind="ExternalOutput").ap()

    xs = nc.dram_tensor("xs", [NHI, S, 128], BF16, kind="Internal").ap()
    ksp = nc.dram_tensor("ksp", [NH, 128, S], BF16, kind=dk).ap()
    qsp = nc.dram_tensor("qsp", [NH, 128, S], BF16, kind=dk).ap()
    vsp = nc.dram_tensor("vsp", [S, NH, 128], BF16, kind=dk).ap()
    csp = nc.dram_tensor("csp", [NH, 128, S], BF16, kind=dk).ap()
    x1sp = nc.dram_tensor("x1sp", [NHI, 128, S], F32, kind=dk).ap()
    w1s = nc.dram_tensor("w1s", [EXP, NHI, 2 * I1, 128], BF16, kind="Internal").ap()
    w2s = nc.dram_tensor("w2s", [EXP, NI, HID, 128], BF16, kind="Internal").ap()
    sh1s = nc.dram_tensor("sh1s", [NHI, 2 * ISZ, 128], BF16, kind="Internal").ap()
    sh2s = nc.dram_tensor("sh2s", [NIS, HID, 128], BF16, kind="Internal").ap()
    w1t = nc.dram_tensor("w1t", [EXP, 128, NHI, 2 * I1], BF16, kind="Internal").ap()
    w2t = nc.dram_tensor("w2t", [EXP, 128, NI, HID], BF16, kind="Internal").ap()
    sh1t = nc.dram_tensor("sh1t", [128, NHI, 2 * ISZ], BF16, kind="Internal").ap()
    sh2t = nc.dram_tensor("sh2t", [128, NIS, HID], BF16, kind="Internal").ap()
    gws = nc.dram_tensor("gws", [EXP, HID], BF16, kind="Internal").ap()
    cmbd = nc.dram_tensor("cmbd", [EXP, TC], BF16, kind=dk).ap()
    sdbg = nc.dram_tensor("sdbg", [NT, 128, 512], F32, kind=dk).ap() if debug else None
    pdbg = nc.dram_tensor("pdbg", [NT, 128, 512], BF16, kind=dk).ap() if debug else None

    cos_np, sin_np = rope_tables()
    cosT = nc.inline_tensor(np.ascontiguousarray(cos_np), "cosT").ap()
    sinT = nc.inline_tensor(np.ascontiguousarray(sin_np), "sinT").ap()
    identb_t = nc.inline_tensor(np.eye(128, dtype=ml_dtypes.bfloat16), "identb_t").ap()
    identf_t = nc.inline_tensor(np.eye(128, dtype=np.float32), "identf_t").ap()
    ones_row_t = nc.inline_tensor(np.ones((1, 128), dtype=ml_dtypes.bfloat16), "ones_row_t").ap()
    ones_rowf_t = nc.inline_tensor(np.ones((1, 128), dtype=np.float32), "ones_rowf_t").ap()
    ones_mat_t = nc.inline_tensor(np.ones((128, 128), dtype=ml_dtypes.bfloat16), "ones_mat_t").ap()
    onesw_t = nc.inline_tensor(np.ones((1, S), dtype=ml_dtypes.bfloat16), "onesw_t").ap()

    with ExitStack() as st:
        blk = st.enter_context(nc.Block())
        bld = Bld(nc, st)
        W = bld.W
        pe = bld.sem("pe")
        act = bld.sem("act")
        dve = bld.sem("dve")
        s_const = bld.sem("s_const")
        s_xs = bld.sem("s_xs")
        s_scr = bld.sem("s_scr")
        s_out = bld.sem("s_out")
        s_xfm = [bld.sem("s_xfm0"), bld.sem("s_xfm1")]
        s_kq = [bld.sem("s_kq0"), bld.sem("s_kq1")]
        s_w1 = [bld.sem("s_w10"), bld.sem("s_w11")]
        s_w2 = [bld.sem("s_w20"), bld.sem("s_w21")]

        e = st.enter_context
        identb = e(nc.sbuf_tensor("identb", [128, 128], BF16))
        identf = e(nc.sbuf_tensor("identf", [128, 128], F32))
        ones_row = e(nc.sbuf_tensor("ones_row", [1, 128], BF16))
        ones_rowf = e(nc.sbuf_tensor("ones_rowf", [1, 128], F32))
        ones_mat = e(nc.sbuf_tensor("ones_mat", [128, 128], BF16))
        onesw = e(nc.sbuf_tensor("onesw", [1, S], BF16))
        tbf = [e(nc.sbuf_tensor(f"tbf{i}", [128, ISZ], BF16)) for i in range(2)]
        s_tb = [bld.sem("s_tb0"), bld.sem("s_tb1")]
        s_tw = [bld.sem("s_tw0"), bld.sem("s_tw1")]

        @blk.sync
        def _(sy):
            s_const.inc(sy.dma_start(identb[:, :], identb_t[:, :]), 16)
            s_const.inc(sy.dma_start(identf[:, :], identf_t[:, :]), 16)
            s_const.inc(sy.dma_start(ones_row[:, :], ones_row_t[:, :]), 16)
            s_const.inc(sy.dma_start(ones_rowf[:, :], ones_rowf_t[:, :]), 16)
            s_const.inc(sy.dma_start(ones_mat[:, :], ones_mat_t[:, :]), 16)
            s_const.inc(sy.dma_start(onesw[:, :], onesw_t[:, :]), 16)
        consts_qs = s_const.v

        def barrier():
            tgt = [(sm, sm.v) for sm in bld.all_sems]

            def waits(engobj, ename):
                for sm, v_ in tgt:
                    W(engobj, ename, sm, v_)

            @blk.tensor
            def _(t):
                waits(t, "t")

            @blk.scalar
            def _(s):
                waits(s, "s")

            @blk.vector
            def _(v):
                waits(v, "v")

            @blk.gpsimd
            def _(g):
                waits(g, "g")

            @blk.sync
            def _(sy):
                waits(sy, "sy")

        @blk.gpsimd
        def _(g):
            for hi in range(NHI):
                for tcn in range(8):
                    s_xs.inc(g.dma_start(
                        xs[hi, tcn * 512:(tcn + 1) * 512, :],
                        x_in[tcn * 512:(tcn + 1) * 512, hi * 128:(hi + 1) * 128]), 16)
        xs_done_qg = s_xs.v

        def p1(is_q):
            NC_ = NQC if is_q else NKV
            Wd, Wu, Wr = (W_DQ, W_UQ, W_QR) if is_q else (W_DKV, W_UK, W_KR)
            bd_d, bu_d = (b_DQ, b_UQ) if is_q else (b_DKV, b_UK)
            sp = qsp if is_q else ksp
            tag = "q" if is_q else "k"
            with ExitStack() as ps:
                e2 = ps.enter_context
                cosb = e2(nc.sbuf_tensor(f"cosb{tag}", [32, S], BF16))
                sinb = e2(nc.sbuf_tensor(f"sinb{tag}", [32, S], BF16))
                xfm = [e2(nc.sbuf_tensor(f"xfm{tag}{i}", [128, NHI * TB], BF16)) for i in range(2)]
                psA = e2(nc.psum_tensor(f"psA{tag}", [128, 512], F32))
                psB = e2(nc.psum_tensor(f"psB{tag}", [128, 512], F32))
                psC = e2(nc.psum_tensor(f"psC{tag}", [128, 512], F32))
                psD = e2(nc.psum_tensor(f"psD{tag}", [128, 512], F32))
                psF = [e2(nc.psum_tensor(f"psF{tag}{i}", [128, 512], F32)) for i in range(2)]
                wd = e2(nc.sbuf_tensor(f"wd{tag}", [128, NHI * NC_ * 128], BF16))
                wu = e2(nc.sbuf_tensor(f"wu{tag}", [128, NC_ * HID], BF16))
                wr = e2(nc.sbuf_tensor(f"wr{tag}", [128, RD], BF16))
                bd = e2(nc.sbuf_tensor(f"bd{tag}", [128, NC_], F32))
                bu = e2(nc.sbuf_tensor(f"bu{tag}", [128, NH], F32))
                cc = e2(nc.sbuf_tensor(f"cc{tag}", [128, NC_ * TB], BF16))
                kfull = [e2(nc.sbuf_tensor(f"kf{tag}{i}", [128, TB], BF16)) for i in range(2)]
                kst = [e2(nc.sbuf_tensor(f"kst{tag}{i}", [128, TB], BF16)) for i in range(2)]
                rt1 = e2(nc.sbuf_tensor(f"rt1{tag}", [32, TB], F32))
                rt2 = e2(nc.sbuf_tensor(f"rt2{tag}", [32, TB], F32))
                if not is_q:
                    wuv = e2(nc.sbuf_tensor("wuv", [128, NKV * HID], BF16))
                    buvr = e2(nc.sbuf_tensor("buvr", [1, HID], F32))
                    buvb = e2(nc.sbuf_tensor("buvb", [128, HID], F32))
                    vwide = [e2(nc.sbuf_tensor(f"vw{i}", [128, HID], BF16)) for i in range(2)]

                s_kst = [bld.sem(f"s_kst{tag}0"), bld.sem(f"s_kst{tag}1")]
                s_ldw = bld.sem(f"s_ldw{tag}")
                s_bias = bld.sem(f"s_bias{tag}")
                s_tab = bld.sem(f"s_tab{tag}")
                s_vw = [bld.sem(f"s_vw{tag}0"), bld.sem(f"s_vw{tag}1")] if not is_q else None

                @blk.gpsimd
                def _(g):
                    for hi in range(NHI):
                        s_ldw.inc(g.dma_start(
                            wd[:, hi * NC_ * 128:(hi + 1) * NC_ * 128],
                            Wd[hi * 128:(hi + 1) * 128, :]), 16)
                    for ci in range(NC_):
                        s_ldw.inc(g.dma_start(wu[:, ci * HID:(ci + 1) * HID],
                                              Wu[ci * 128:(ci + 1) * 128, :]), 16)
                    s_ldw.inc(g.dma_start(wr[:, :], Wr[:, :]), 16)
                    if not is_q:
                        for ci in range(NKV):
                            s_ldw.inc(g.dma_start(wuv[:, ci * HID:(ci + 1) * HID],
                                                  W_UV[ci * 128:(ci + 1) * 128, :]), 16)
                ld_qg = s_ldw.v

                @blk.scalar
                def _(s):
                    for ci in range(NC_):
                        s_bias.inc(s.dma_start(bd[:, ci:ci + 1],
                                               bd_d[ci * 128:(ci + 1) * 128][:, None]), 16)
                    for h in range(NH):
                        s_bias.inc(s.dma_start(bu[:, h:h + 1],
                                               bu_d[h * 128:(h + 1) * 128][:, None]), 16)
                    if not is_q:
                        s_bias.inc(s.dma_start(buvr[:, :], b_UV[None, :]), 16)
                ld_qa = s_bias.v

                @blk.sync
                def _(sy):
                    s_tab.inc(sy.dma_start(cosb[:, :], cosT[:, :]), 16)
                    s_tab.inc(sy.dma_start(sinb[:, :], sinT[:, :]), 16)
                tab_qs = s_tab.v

                @blk.vector
                def _(v):
                    for i in range(2):
                        dve.inc(v.memset(kst[i][0:32, :], 0.0))
                        if not is_q:
                            W(v, "v", dve, dve.v)
                            dve.inc(v.memset(kst[i][0:1, :], 1.0))
                kst_init = dve.v

                buv_dve = 0
                if not is_q:
                    bb_dve = []
                    for i in range(4):
                        p_ = psA if i % 2 == 0 else psB

                        @blk.tensor
                        def _(t):
                            W(t, "t", s_bias, ld_qa)
                            W(t, "t", s_const, consts_qs)
                            if i >= 2:
                                W(t, "t", dve, bb_dve[i - 2])
                            pe.inc(t.matmul(p_[:, :], ones_rowf[:, :],
                                            buvr[0:1, i * 512:(i + 1) * 512],
                                            start=True, stop=True))
                        mmv = pe.v

                        @blk.vector
                        def _(v):
                            W(v, "v", pe, mmv)
                            dve.inc(v.tensor_copy(buvb[:, i * 512:(i + 1) * 512], p_[:, :]))
                            bb_dve.append(dve.v)
                    buv_dve = dve.v

                xfm_fill = [0, 0]

                def issue_xfm(sy, tb):
                    slot = tb % 2
                    for hi in range(NHI):
                        s_xfm[slot].inc(sy.dma_start(
                            xfm[slot][:, hi * TB:(hi + 1) * TB],
                            xs[hi, tb * TB:(tb + 1) * TB, :], transpose=True), 16)
                    xfm_fill[slot] = s_xfm[slot].v

                @blk.sync
                def _(sy):
                    W(sy, "sy", s_xs, xs_done_qg)
                    issue_xfm(sy, 0)

                psAB_free = [0, 0]
                psCD_free = [("x", 0), ("x", 0)]
                psF_free = [0, 0]
                kf_free = [(0, 0), (0, 0)]
                kst_sp = [0, 0]
                vw_sp = [0, 0]
                cc_done_prev = 0

                for tb in range(NTB):
                    slot = tb % 2

                    @blk.sync
                    def _(sy):
                        if tb + 1 < NTB:
                            W(sy, "sy", pe, cc_done_prev)
                            issue_xfm(sy, tb + 1)

                    cc_mm = [0] * NC_
                    cc_ev = [0] * NC_
                    for ci in range(NC_):
                        p_ = psA if ci % 2 == 0 else psB

                        @blk.tensor
                        def _(t):
                            W(t, "t", s_xfm[slot], xfm_fill[slot])
                            W(t, "t", s_ldw, ld_qg)
                            W(t, "t", dve, buv_dve)
                            W(t, "t", act, psAB_free[ci % 2])
                            ins = None
                            for hi in range(NHI):
                                ins = t.matmul(
                                    p_[:, :],
                                    wd[:, (hi * NC_ + ci) * 128:(hi * NC_ + ci + 1) * 128],
                                    xfm[slot][:, hi * TB:(hi + 1) * TB],
                                    start=(hi == 0), stop=(hi == NHI - 1))
                            cc_mm[ci] = pe.inc(ins)

                        @blk.scalar
                        def _(s):
                            W(s, "s", pe, cc_mm[ci])
                            W(s, "s", s_bias, ld_qa)
                            cc_ev[ci] = act.inc(s.activation(
                                cc[:, ci * TB:(ci + 1) * TB], p_[:, :],
                                AF.Identity, bias=bd[:, ci:ci + 1]))
                        psAB_free[ci % 2] = cc_ev[ci]
                    cc_done_prev = cc_mm[NC_ - 1]
                    cc_all_ev = cc_ev[NC_ - 1]

                    for h in range(NH):
                        hp = h % 2
                        p_ = psC if hp == 0 else psD

                        @blk.tensor
                        def _(t):
                            W(t, "t", act, cc_all_ev)
                            eng, val = psCD_free[hp]
                            if eng == "act":
                                W(t, "t", act, val)
                            elif eng == "dve":
                                W(t, "t", dve, val)
                            ins = None
                            for ci in range(NC_):
                                ins = t.matmul(
                                    p_[:, :],
                                    wu[:, ci * HID + h * 128:ci * HID + (h + 1) * 128],
                                    cc[:, ci * TB:(ci + 1) * TB],
                                    start=(ci == 0), stop=(ci == NC_ - 1))
                            pe.inc(ins)
                        k_mm = pe.v

                        @blk.scalar
                        def _(s):
                            W(s, "s", pe, k_mm)
                            W(s, "s", pe, kf_free[hp][0])
                            W(s, "s", dve, kf_free[hp][1])
                            act.inc(s.activation(kfull[hp][:, :], p_[:, :],
                                                 AF.Identity, bias=bu[:, h:h + 1]))
                        kf_ev = act.v
                        psCD_free[hp] = ("act", kf_ev)

                        pf = psF[hp]

                        @blk.tensor
                        def _(t):
                            W(t, "t", act, kf_ev)
                            W(t, "t", dve, psF_free[hp])
                            pe.inc(t.matmul(pf[0:RD, :], wr[:, :], kfull[hp][:, :],
                                            start=True, stop=True))
                        kr_mm = pe.v

                        @blk.vector
                        def _(v):
                            W(v, "v", pe, kr_mm)
                            W(v, "v", dve, dve.v)
                            W(v, "v", s_tab, tab_qs)
                            W(v, "v", s_kst[hp], kst_sp[hp])
                            W(v, "v", dve, kst_init)
                            dve.inc(v.tensor_mul(rt1[:, :], pf[0:HF, :],
                                                 cosb[:, tb * TB:(tb + 1) * TB]))
                            dve.inc(v.tensor_mul(rt2[:, :], pf[HF:RD, :],
                                                 sinb[:, tb * TB:(tb + 1) * TB]))
                            W(v, "v", dve, dve.v)
                            dve.inc(v.tensor_sub(kst[hp][32:64, :], rt1[:, :], rt2[:, :]))
                            W(v, "v", act, kf_ev)
                            dve.inc(v.tensor_copy(kst[hp][64:128, :], kfull[hp][64:128, :]))
                        rope_done = dve.v
                        psF_free[hp] = rope_done
                        kf_free[hp] = (kr_mm, rope_done)

                        @blk.sync
                        def _(sy):
                            W(sy, "sy", dve, rope_done)
                            s_kst[hp].inc(sy.dma_start(
                                sp[h][:, tb * TB:(tb + 1) * TB], kst[hp][:, :]), 16)
                        kst_sp[hp] = s_kst[hp].v

                    if not is_q:
                        for kt in range(4):
                            vslot = (tb * 4 + kt) % 2
                            for h in range(NH):
                                hp2 = (kt * NH + h) % 2
                                p_ = psC if hp2 == 0 else psD

                                @blk.tensor
                                def _(t):
                                    eng, val = psCD_free[hp2]
                                    if eng == "act":
                                        W(t, "t", act, val)
                                    elif eng == "dve":
                                        W(t, "t", dve, val)
                                    ins = None
                                    for ci in range(NKV):
                                        ins = t.matmul(
                                            p_[:, 0:128],
                                            cc[:, ci * TB + kt * 128:ci * TB + (kt + 1) * 128],
                                            wuv[:, ci * HID + h * 128:ci * HID + (h + 1) * 128],
                                            start=(ci == 0), stop=(ci == NKV - 1))
                                    pe.inc(ins)
                                v_mm = pe.v

                                @blk.vector
                                def _(v):
                                    W(v, "v", pe, v_mm)
                                    W(v, "v", dve, buv_dve)
                                    if h == 0:
                                        W(v, "v", s_vw[vslot], vw_sp[vslot])
                                    dve.inc(v.tensor_add(
                                        vwide[vslot][:, h * 128:(h + 1) * 128],
                                        p_[:, 0:128], buvb[:, h * 128:(h + 1) * 128]))
                                psCD_free[hp2] = ("dve", dve.v)
                            vw_ready = dve.v

                            @blk.sync
                            def _(sy):
                                W(sy, "sy", dve, vw_ready)
                                s_vw[vslot].inc(sy.dma_start(
                                    vsp[tb * TB + kt * 128:tb * TB + (kt + 1) * 128, :, :]
                                    .rearrange("t h d -> t (h d)"),
                                    vwide[vslot][:, :]), 16)
                            vw_sp[vslot] = s_vw[vslot].v

        if "p1a" in phases:
            p1(False)
            barrier()
        if "p1b" in phases:
            p1(True)
            barrier()

        ffn_scratch_qg = 0
        st1_jobs = []
        st1_cursor = [0]
        if "p4" in phases:
            scr_mark = {}

            @blk.gpsimd
            def _(g):
                s_scr.inc(g.dma_start(gws[:, :], gate_w[:, :]), 16)
                for ee in range(EXP):
                    for j in range(NI):
                        for part in range(2):
                            src_base = (0 if part == 0 else I1) + j * 128
                            for hg in range(2):
                                s_scr.inc(g.dma_start(
                                    w1s[ee, hg * 8:(hg + 1) * 8,
                                        j * 256 + part * 128:j * 256 + (part + 1) * 128, :],
                                    w1[ee, src_base:src_base + 128,
                                       hg * 1024:(hg + 1) * 1024]
                                    .rearrange("o (hi p) -> hi o p", p=128)), 16)
                    for ii in range(NI):
                        for hg in range(2):
                            s_scr.inc(g.dma_start(
                                w2s[ee, ii, hg * 1024:(hg + 1) * 1024, :],
                                w2[ee, hg * 1024:(hg + 1) * 1024, ii * 128:(ii + 1) * 128]), 16)
                    scr_mark[ee] = s_scr.v
                for j in range(NIS):
                    for part in range(2):
                        src_base = (0 if part == 0 else ISZ) + j * 128
                        for hg in range(2):
                            s_scr.inc(g.dma_start(
                                sh1s[hg * 8:(hg + 1) * 8,
                                     j * 256 + part * 128:j * 256 + (part + 1) * 128, :],
                                sh_w1[src_base:src_base + 128, hg * 1024:(hg + 1) * 1024]
                                .rearrange("o (hi p) -> hi o p", p=128)), 16)
                scr_mark["sh1"] = s_scr.v
                for ii in range(NIS):
                    for hg in range(2):
                        s_scr.inc(g.dma_start(
                            sh2s[ii, hg * 1024:(hg + 1) * 1024, :],
                            sh_w2[hg * 1024:(hg + 1) * 1024, ii * 128:(ii + 1) * 128]), 16)
                scr_mark["sh2"] = s_scr.v
            ffn_scratch_qg = s_scr.v

            for ee in range(EXP):
                for hi in range(NHI):
                    st1_jobs.append((w1s[ee, hi], 2 * I1, w1t[ee, :, hi, :],
                                     scr_mark[ee]))
                for ii in range(NI):
                    st1_jobs.append((w2s[ee, ii], HID, w2t[ee, :, ii, :],
                                     scr_mark[ee]))
            for hi in range(NHI):
                for hf in range(2):
                    st1_jobs.append((sh1s[hi, hf * ISZ:(hf + 1) * ISZ, :], ISZ,
                                     sh1t[:, hi, hf * ISZ:(hf + 1) * ISZ],
                                     scr_mark["sh1"]))
            for ii in range(NIS):
                st1_jobs.append((sh2s[ii], HID, sh2t[:, ii, :], scr_mark["sh2"]))

        def emit_stage1(n):
            i0 = st1_cursor[0]
            i1 = min(i0 + n, len(st1_jobs))
            if i0 >= i1:
                return
            st1_cursor[0] = i1

            @blk.sync
            def _(sy):
                for i in range(i0, i1):
                    src, wdt, dst, mark = st1_jobs[i]
                    slot = i % 2
                    W(sy, "sy", s_scr, mark)
                    if i >= 2:
                        sy.wait_ge(s_tw[slot].h, 16 * (i // 2))
                    s_tb[slot].inc(sy.dma_start(
                        tbf[slot][:, 0:wdt], src, transpose=True), 16)

            @blk.gpsimd
            def _(g):
                for i in range(i0, i1):
                    src, wdt, dst, mark = st1_jobs[i]
                    slot = i % 2
                    g.wait_ge(s_tb[slot].h, 16 * (i // 2 + 1))
                    s_tw[slot].inc(g.dma_start(dst, tbf[slot][:, 0:wdt]), 16)

        if "p2" in phases:
            NQH = NQB * NT
            GTOT = NH * NQH
            CMARGIN = 1.02
            with ExitStack() as ps:
                e2 = ps.enter_context
                kb = [e2(nc.sbuf_tensor(f"kb{i}", [128, S], BF16)) for i in range(2)]
                qbb = [e2(nc.sbuf_tensor(f"qbb{i}", [128, S], BF16)) for i in range(2)]
                vb = [e2(nc.sbuf_tensor(f"vb{i}", [128, NT * 128], BF16)) for i in range(2)]
                ssb = [e2(nc.sbuf_tensor(f"ssb{i}", [128, QBW], F32)) for i in range(8)]
                prb = [e2(nc.sbuf_tensor(f"prb{i}", [128, QBW], BF16)) for i in range(8)]
                ctxs = [e2(nc.sbuf_tensor(f"ctxs{i}", [128, QBW], BF16)) for i in range(2)]
                rs_sb = e2(nc.sbuf_tensor("rs_sb", [128, QBW], F32))
                se_sb = e2(nc.sbuf_tensor("se_sb", [128, QBW], F32))
                sc_sb = e2(nc.sbuf_tensor("sc_sb", [128, QBW], F32))
                sqs = e2(nc.sbuf_tensor("sqs", [128, 2 * S], BF16))
                nrm = e2(nc.sbuf_tensor("nrm", [1, 2 * S], F32))
                nsc = e2(nc.sbuf_tensor("nsc", [1, 8], F32))
                psS = [e2(nc.psum_tensor(f"psS{i}", [128, 512], F32)) for i in range(4)]
                psC = e2(nc.psum_tensor("psC2", [128, 512], F32))
                psE = e2(nc.psum_tensor("psE2", [128, 512], F32))
                psN = [e2(nc.psum_tensor(f"psN{i}", [1, 512], F32)) for i in range(2)]
                s_ctxw = [bld.sem("s_ctxw0"), bld.sem("s_ctxw1")]
                s_dbg = bld.sem("s_dbg") if debug else None
                dbg_wm = {}

                kq_fill = [0, 0]

                def load_head(sy, h):
                    slot = h % 2
                    s_kq[slot].inc(sy.dma_start(kb[slot][:, :], ksp[h][:, :]), 16)
                    s_kq[slot].inc(sy.dma_start(qbb[slot][:, :], qsp[h][:, :]), 16)
                    for kt in range(NT):
                        s_kq[slot].inc(sy.dma_start(
                            vb[slot][:, kt * 128:(kt + 1) * 128],
                            vsp[kt * 128:(kt + 1) * 128, h, :]), 16)
                    kq_fill[slot] = s_kq[slot].v

                st_mm = {}
                copy_done = {}
                exp_pred = {}
                pv_done = {}
                sq_done = {}
                c_done = {}
                row0_done = {}
                prep_norm_pe = {}
                prod_done = {}
                psN_free = [0, 0]
                state = {"psCE_free": 0}
                ctxs_sp = [0, 0]

                def prep_s1(h):
                    slot = h % 2

                    @blk.scalar
                    def _(s):
                        W(s, "s", s_kq[slot], kq_fill[slot])
                        if h >= 1:
                            W(s, "s", pe, prep_norm_pe[h - 1])
                        act.inc(s.activation(sqs[:, 0:S], kb[slot][:, :], AF.Square))
                        act.inc(s.activation(sqs[:, S:2 * S], qbb[slot][:, :], AF.Square))
                    sq_done[h] = act.v

                def prep_norm(h):
                    for i in range(16):
                        pn = psN[i % 2]

                        @blk.tensor
                        def _(t):
                            W(t, "t", act, sq_done[h])
                            W(t, "t", s_const, consts_qs)
                            W(t, "t", dve, psN_free[i % 2])
                            pe.inc(t.matmul(pn[0:1, :], ones_mat[:, 0:1],
                                            sqs[:, i * 512:(i + 1) * 512],
                                            start=True, stop=True))
                        nm = pe.v

                        @blk.vector
                        def _(v):
                            W(v, "v", pe, nm)
                            dve.inc(v.tensor_copy(nrm[0:1, i * 512:(i + 1) * 512],
                                                  pn[0:1, :]))
                        psN_free[i % 2] = dve.v
                    prep_norm_pe[h] = pe.v

                    @blk.vector
                    def _(v):
                        if h >= 1:
                            W(v, "v", act, c_done[h - 1])
                        dve.inc(v.reduce_max(nsc[0:1, 0:1], nrm[0:1, 0:S], axis=AX))
                        dve.inc(v.reduce_max(nsc[0:1, 1:2], nrm[0:1, S:2 * S], axis=AX))
                        W(v, "v", dve, dve.v)
                        dve.inc(v.tensor_mul(nsc[0:1, 2:3], nsc[0:1, 0:1],
                                             nsc[0:1, 1:2]))
                    prod_done[h] = dve.v

                def prep_s2(h):
                    @blk.scalar
                    def _(s):
                        W(s, "s", dve, prod_done[h])
                        act.inc(s.activation(nsc[0:1, 3:4], nsc[0:1, 2:3], AF.Ln))
                        W(s, "s", act, act.v)
                        act.inc(s.activation(nsc[0:1, 4:5], nsc[0:1, 3:4], AF.Exp,
                                             scale=0.5))
                    c_done[h] = act.v

                def prep_row0(h):
                    slot = h % 2

                    @blk.vector
                    def _(v):
                        W(v, "v", act, c_done[h])
                        W(v, "v", s_kq[slot], kq_fill[slot])
                        dve.inc(v.tensor_scalar(nsc[0:1, 5:6], nsc[0:1, 4:5], -CMARGIN,
                                                None, ALU.mult))
                        W(v, "v", dve, dve.v)
                        dve.inc(v.tensor_scalar(qbb[slot][0:1, :], onesw[0:1, :],
                                                nsc[0:1, 5:6], None, ALU.mult))
                    row0_done[h] = dve.v

                def emit_pvs(t, j0):
                    W(t, "t", act, exp_pred[j0 + 3])
                    for j in range(j0, j0 + 4):
                        jslot = (j // NQH) % 2
                        jkt = j % NT
                        if jkt == 0:
                            W(t, "t", dve, state["psCE_free"])
                        t.matmul(psC[:, :], vb[jslot][:, jkt * 128:(jkt + 1) * 128],
                                 prb[j % 8][:, :],
                                 start=(jkt == 0), stop=(jkt == NT - 1))
                        ins = t.matmul(psE[:, :], ones_mat[:, :], prb[j % 8][:, :],
                                       start=(jkt == 0), stop=(jkt == NT - 1))
                        if jkt == NT - 1:
                            pv_done[j] = pe.inc(ins)

                def emit_finalize(gqb):
                    cslot = gqb % 2
                    hh = gqb // NQB
                    q0f = (gqb % NQB) * QBW

                    @blk.vector
                    def _(v):
                        W(v, "v", pe, pv_done[gqb * NT + NT - 1])
                        W(v, "v", s_ctxw[cslot], ctxs_sp[cslot])
                        dve.inc(v.tensor_copy(se_sb[:, :], psE[:, :]))
                        dve.inc(v.tensor_copy(sc_sb[:, :], psC[:, :]))
                        W(v, "v", dve, dve.v)
                        dve.inc(v.reciprocal(rs_sb[:, :], se_sb[:, :]))
                        W(v, "v", dve, dve.v)
                        dve.inc(v.tensor_mul(ctxs[cslot][:, :], sc_sb[:, :],
                                             rs_sb[:, :]))
                    state["psCE_free"] = dve.v - 2
                    ctx_ready = dve.v

                    @blk.sync
                    def _(sy):
                        W(sy, "sy", dve, ctx_ready)
                        s_ctxw[cslot].inc(sy.dma_start(
                            csp[hh][:, q0f:q0f + QBW], ctxs[cslot][:, :]), 16)
                    ctxs_sp[cslot] = s_ctxw[cslot].v

                for h in range(NH):
                    slot = h % 2
                    if h == 0:
                        @blk.sync
                        def _(sy):
                            load_head(sy, 0)
                        prep_s1(0)
                        prep_norm(0)
                        prep_s2(0)
                        prep_row0(0)
                    for qb in range(NQB):
                        base = h * NQH + qb * NT
                        q0 = qb * QBW
                        a0 = act.v
                        for kt in range(NT):
                            exp_pred[base + kt] = a0 + 4 * (kt // 4 + 1)
                        for kq4 in range(8):
                            g0 = base + kq4 * 4
                            if kq4 == 2 and (h * NQB + qb) >= 1:
                                emit_finalize(h * NQB + qb - 1)

                            @blk.tensor
                            def _(t):
                                if g0 - 8 >= 0:
                                    emit_pvs(t, g0 - 8)
                                W(t, "t", dve, row0_done[h])
                                W(t, "t", s_kq[slot], kq_fill[slot])
                                if g0 - 1 >= 0:
                                    W(t, "t", dve, copy_done[g0 - 1])
                                ins = None
                                for x in range(4):
                                    gk = g0 + x
                                    kt_ = gk % NT
                                    ins = t.matmul(psS[gk % 4][:, :],
                                                   kb[slot][:, kt_ * 128:(kt_ + 1) * 128],
                                                   qbb[slot][:, q0:q0 + QBW],
                                                   start=True, stop=True)
                                pe.inc(ins)
                            smv = pe.v
                            st_mm[g0] = smv

                            @blk.vector
                            def _(v):
                                W(v, "v", pe, smv)
                                if debug and s_dbg.v:
                                    W(v, "v", s_dbg, s_dbg.v)
                                ins = None
                                for x in range(4):
                                    gk = g0 + x
                                    ins = v.tensor_copy(ssb[gk % 8][:, :],
                                                        psS[gk % 4][:, :])
                                dve.inc(ins)
                            for x in range(4):
                                copy_done[g0 + x] = dve.v

                            if debug and h == 0 and qb == 0:
                                @blk.gpsimd
                                def _(g):
                                    W(g, "g", dve, copy_done[g0])
                                    for x in range(4):
                                        gk = g0 + x
                                        s_dbg.inc(g.dma_start(sdbg[gk % NT],
                                                              ssb[gk % 8][:, :]), 16)
                                    W(g, "g", act, exp_pred[g0 + 3] - 4)
                                    if kq4 >= 1:
                                        for x in range(4):
                                            gk = g0 - 4 + x
                                            s_dbg.inc(g.dma_start(pdbg[gk % NT],
                                                                  prb[gk % 8][:, :]), 16)
                                dbg_wm[g0] = s_dbg.v

                            if qb == 0 and kq4 == 1 and h + 1 < NH:
                                @blk.sync
                                def _(sy):
                                    if h >= 1:
                                        W(sy, "sy", pe, pv_done[h * NQH - 1])
                                        W(sy, "sy", act, sq_done[h - 1])
                                        W(sy, "sy", dve, row0_done[h - 1])
                                    load_head(sy, h + 1)

                        @blk.scalar
                        def _(s):
                            for kq4 in range(8):
                                g0 = base + kq4 * 4
                                W(s, "s", dve, copy_done[g0 + 3])
                                if debug and g0 in dbg_wm:
                                    W(s, "s", s_dbg, dbg_wm[g0])
                                for x in range(4):
                                    gk = g0 + x
                                    act.inc(s.activation(prb[gk % 8][:, :],
                                                         ssb[gk % 8][:, :],
                                                         AF.Exp, scale=INVSQ))
                        assert act.v == a0 + NT
                        emit_stage1(4)

                        if h + 1 < NH:
                            if qb == 1:
                                prep_s1(h + 1)
                            elif qb == 2:
                                prep_norm(h + 1)
                            elif qb == 3:
                                prep_s2(h + 1)
                                prep_row0(h + 1)

                @blk.tensor
                def _(t):
                    emit_pvs(t, GTOT - 8)
                    emit_pvs(t, GTOT - 4)
                emit_finalize(NH * NQB - 1)

            barrier()

        if "p3" in phases:
            with ExitStack() as ps:
                e2 = ps.enter_context
                wo = e2(nc.sbuf_tensor("wo", [128, NHI * HID], BF16))
                bor = e2(nc.sbuf_tensor("bor", [1, HID], F32))
                bob = e2(nc.sbuf_tensor("bob", [128, HID], F32))
                ctxt = [e2(nc.sbuf_tensor(f"ctxt{i}", [128, NH * 128], BF16)) for i in range(2)]
                xres = [e2(nc.sbuf_tensor(f"xres{i}", [128, HID], F32)) for i in range(2)]
                xr = e2(nc.sbuf_tensor("xr", [128, HID], F32))
                ctr = e2(nc.sbuf_tensor("ctr", [128, HID], F32))
                sqb = e2(nc.sbuf_tensor("sqb", [128, HID], F32))
                mcol = e2(nc.sbuf_tensor("mcol", [128, 1], F32))
                vcol = e2(nc.sbuf_tensor("vcol", [128, 1], F32))
                rstd = e2(nc.sbuf_tensor("rstd", [128, 1], F32))
                x1f = [e2(nc.sbuf_tensor(f"x1f{i}", [128, 128], F32)) for i in range(2)]
                psA = e2(nc.psum_tensor("psA3", [128, 512], F32))
                psB = e2(nc.psum_tensor("psB3", [128, 512], F32))
                psT = e2(nc.psum_tensor("psT3", [128, 128], F32))
                s_ctxt = [bld.sem("s_ctxt_0"), bld.sem("s_ctxt_1")]
                s_x1w = [bld.sem("s_x1w_0"), bld.sem("s_x1w_1")]
                s_wo = bld.sem("s_wo")
                s_bo = bld.sem("s_bo")
                s_xres = [bld.sem("s_xres0"), bld.sem("s_xres1")]

                @blk.gpsimd
                def _(g):
                    for hi in range(NHI):
                        s_wo.inc(g.dma_start(wo[:, hi * HID:(hi + 1) * HID],
                                             W_O[hi * 128:(hi + 1) * 128, :]), 16)
                wo_qg = s_wo.v

                @blk.scalar
                def _(s):
                    s_bo.inc(s.dma_start(bor[:, :], b_O[None, :]), 16)
                bo_qa = s_bo.v

                bo_dve = []
                for i in range(4):
                    p_ = psA if i % 2 == 0 else psB

                    @blk.tensor
                    def _(t):
                        W(t, "t", s_bo, bo_qa)
                        W(t, "t", s_const, consts_qs)
                        if i >= 2:
                            W(t, "t", dve, bo_dve[i - 2])
                        pe.inc(t.matmul(p_[:, :], ones_rowf[:, :],
                                        bor[0:1, i * 512:(i + 1) * 512],
                                        start=True, stop=True))
                    mmv = pe.v

                    @blk.vector
                    def _(v):
                        W(v, "v", pe, mmv)
                        dve.inc(v.tensor_copy(bob[:, i * 512:(i + 1) * 512], p_[:, :]))
                        bo_dve.append(dve.v)
                bob_done = dve.v

                ctxt_fill = [0, 0]

                def load_ctxt(sy, tt):
                    slot = tt % 2
                    for h in range(NH):
                        s_ctxt[slot].inc(sy.dma_start(
                            ctxt[slot][:, h * 128:(h + 1) * 128],
                            csp[h][:, tt * 128:(tt + 1) * 128]), 16)
                    ctxt_fill[slot] = s_ctxt[slot].v

                @blk.sync
                def _(sy):
                    load_ctxt(sy, 0)

                xres_fill = [0, 0]

                @blk.scalar
                def _(s):
                    s_xres[0].inc(s.dma_start(xres[0][:, :], x_in[0:128, :]), 16)
                xres_fill[0] = s_xres[0].v

                psAB_free = [0, 0]
                ctxt_last_pe = [0, 0]
                xres_last_dve = [0, 0]
                x1w_sp = [0, 0]
                ln_ctr_free_pe = 0

                for tt in range(NT):
                    slot = tt % 2

                    @blk.sync
                    def _(sy):
                        if tt + 1 < NT:
                            W(sy, "sy", pe, ctxt_last_pe[(tt + 1) % 2])
                            load_ctxt(sy, tt + 1)

                    @blk.scalar
                    def _(s):
                        if tt + 1 < NT:
                            W(s, "s", dve, xres_last_dve[(tt + 1) % 2])
                            s_xres[(tt + 1) % 2].inc(s.dma_start(
                                xres[(tt + 1) % 2][:, :],
                                x_in[(tt + 1) * 128:(tt + 2) * 128, :]), 16)
                            xres_fill[(tt + 1) % 2] = s_xres[(tt + 1) % 2].v

                    for oc in range(4):
                        p_ = psA if oc % 2 == 0 else psB

                        @blk.tensor
                        def _(t):
                            W(t, "t", s_ctxt[slot], ctxt_fill[slot])
                            W(t, "t", s_wo, wo_qg)
                            W(t, "t", dve, bob_done)
                            W(t, "t", dve, psAB_free[oc % 2])
                            ins = None
                            for hh in range(NH):
                                ins = t.matmul(
                                    p_[:, :],
                                    ctxt[slot][:, hh * 128:(hh + 1) * 128],
                                    wo[:, hh * HID + oc * 512:hh * HID + (oc + 1) * 512],
                                    start=(hh == 0), stop=(hh == NH - 1))
                            pe.inc(ins)
                        mm_oc = pe.v
                        ctxt_last_pe[slot] = mm_oc

                        @blk.vector
                        def _(v):
                            W(v, "v", pe, mm_oc)
                            W(v, "v", dve, dve.v)
                            W(v, "v", dve, bob_done)
                            W(v, "v", s_xres[slot], xres_fill[slot])
                            dve.inc(v.tensor_add(xr[:, oc * 512:(oc + 1) * 512], p_[:, :],
                                                 bob[:, oc * 512:(oc + 1) * 512]))
                            W(v, "v", dve, dve.v)
                            dve.inc(v.tensor_add(xr[:, oc * 512:(oc + 1) * 512],
                                                 xr[:, oc * 512:(oc + 1) * 512],
                                                 xres[slot][:, oc * 512:(oc + 1) * 512]))
                        psAB_free[oc % 2] = dve.v
                    xres_last_dve[slot] = dve.v

                    @blk.vector
                    def _(v):
                        W(v, "v", dve, dve.v)
                        W(v, "v", pe, ln_ctr_free_pe)
                        dve.inc(v.reduce_sum(mcol[:, :], xr[:, :], axis=AX))
                        W(v, "v", dve, dve.v)
                        dve.inc(v.tensor_scalar(mcol[:, :], mcol[:, :],
                                                1.0 / HID, None, ALU.mult))
                        W(v, "v", dve, dve.v)
                        dve.inc(v.tensor_scalar(ctr[:, :], xr[:, :], mcol[:, 0:1],
                                                None, ALU.subtract))
                        W(v, "v", dve, dve.v)
                        dve.inc(v.tensor_mul(sqb[:, :], ctr[:, :], ctr[:, :]))
                        W(v, "v", dve, dve.v)
                        dve.inc(v.reduce_sum(vcol[:, :], sqb[:, :], axis=AX))
                        W(v, "v", dve, dve.v)
                        dve.inc(v.tensor_scalar(vcol[:, :], vcol[:, :],
                                                1.0 / HID, LN_EPS, ALU.mult, ALU.add))
                    var_done = dve.v

                    @blk.scalar
                    def _(s):
                        W(s, "s", dve, var_done)
                        act.inc(s.activation(rstd[:, :], vcol[:, :], AF.Ln))
                        W(s, "s", act, act.v)
                        act.inc(s.activation(rstd[:, :], rstd[:, :], AF.Exp, scale=-0.5))
                    rstd_done = act.v

                    @blk.vector
                    def _(v):
                        W(v, "v", act, rstd_done)
                        dve.inc(v.tensor_scalar(ctr[:, :], ctr[:, :], rstd[:, 0:1],
                                                None, ALU.mult))
                    x1_done = dve.v

                    psT_free = 0
                    for hi in range(NHI):
                        fslot = hi % 2

                        @blk.tensor
                        def _(t):
                            W(t, "t", dve, x1_done)
                            W(t, "t", dve, psT_free)
                            pe.inc(t.transpose(psT[:, :], ctr[:, hi * 128:(hi + 1) * 128],
                                               identf[:, :]))
                        tr_mm = pe.v

                        @blk.vector
                        def _(v):
                            W(v, "v", pe, tr_mm)
                            W(v, "v", s_x1w[fslot], x1w_sp[fslot])
                            dve.inc(v.tensor_copy(x1f[fslot][:, :], psT[:, :]))
                        psT_free = dve.v

                        @blk.sync
                        def _(sy):
                            W(sy, "sy", dve, psT_free)
                            s_x1w[fslot].inc(sy.dma_start(
                                x1sp[hi][:, tt * 128:(tt + 1) * 128], x1f[fslot][:, :]), 16)
                        x1w_sp[fslot] = s_x1w[fslot].v
                    ln_ctr_free_pe = pe.v

            barrier()

        emit_stage1(len(st1_jobs))
        if "p4" in phases:
            with ExitStack() as ps:
                e2 = ps.enter_context
                acc = e2(nc.sbuf_tensor("acc", [128, NHI * TC], F32))
                x1b = e2(nc.sbuf_tensor("x1b", [128, NHI * TC], BF16))
                w1b = [e2(nc.sbuf_tensor(f"w1b{i}", [128, NHI * 768], BF16)) for i in range(2)]
                w2b = [e2(nc.sbuf_tensor(f"w2b{i}", [128, 6 * HID], BF16)) for i in range(2)]
                actb = e2(nc.sbuf_tensor("actb", [128, NI * TC], BF16))
                tsil = [e2(nc.sbuf_tensor(f"tsil{i}", [128, TC], BF16)) for i in range(2)]
                gwT = e2(nc.sbuf_tensor("gwT", [128, NHI * EXP], BF16))
                probs = e2(nc.sbuf_tensor("probs", [128, EXP], F32))
                wrk = e2(nc.sbuf_tensor("wrk4", [128, EXP], F32))
                msk = e2(nc.sbuf_tensor("msk", [128, EXP], F32))
                sel = e2(nc.sbuf_tensor("sel", [128, EXP], F32))
                cmb = e2(nc.sbuf_tensor("cmb", [128, EXP], F32))
                cmb_bf = e2(nc.sbuf_tensor("cmb_bf", [128, EXP], BF16))
                cmbT = e2(nc.sbuf_tensor("cmbT", [16, TC], BF16))
                cbrow = e2(nc.sbuf_tensor("cbrow", [1, TC], BF16))
                cbb = e2(nc.sbuf_tensor("cbb", [128, TC], F32))
                m1c = e2(nc.sbuf_tensor("m1c", [128, 1], F32))
                negm = e2(nc.sbuf_tensor("negm", [128, 1], F32))
                smc = e2(nc.sbuf_tensor("smc", [128, 1], F32))
                rrc = e2(nc.sbuf_tensor("rrc", [128, 1], F32))
                otm = e2(nc.sbuf_tensor("otm", [128, HID], F32))
                fsq = e2(nc.sbuf_tensor("fsq", [128, HID], F32))
                fmc = e2(nc.sbuf_tensor("fmc", [128, 1], F32))
                fvc = e2(nc.sbuf_tensor("fvc", [128, 1], F32))
                frs = e2(nc.sbuf_tensor("frs", [128, 1], F32))
                psA = e2(nc.psum_tensor("psA4", [128, 512], F32))
                psB = e2(nc.psum_tensor("psB4", [128, 512], F32))
                psC = e2(nc.psum_tensor("psC4", [128, 512], F32))
                psD = e2(nc.psum_tensor("psD4", [128, 512], F32))
                psE = e2(nc.psum_tensor("psE4", [128, 512], F32))
                psT = e2(nc.psum_tensor("psT4", [128, 128], F32))
                psTb = e2(nc.psum_tensor("psTb4", [128, 128], BF16))
                s_cmb = bld.sem("s_cmb")
                s_gwT = bld.sem("s_gwT")
                s_acc = bld.sem("s_acc")
                s_cbr = bld.sem("s_cbr")

                @blk.sync
                def _(sy):
                    W(sy, "sy", s_scr, ffn_scratch_qg)
                    W(sy, "sy", s_tw[0], s_tw[0].v)
                    W(sy, "sy", s_tw[1], s_tw[1].v)
                    for hi in range(NHI):
                        s_gwT.inc(sy.dma_start(gwT[:, hi * EXP:(hi + 1) * EXP],
                                               gws[:, hi * 128:(hi + 1) * 128],
                                               transpose=True), 16)
                gwT_qs = s_gwT.v

                def w1_src_moe(ee):
                    return lambda r0, r1: w1t[ee, :, :, r0:r1]

                def w2_src_moe(ee):
                    return lambda i0, i1_: w2t[ee, :, i0:i1_, :]

                def w1_src_sh(off):
                    return lambda r0, r1: sh1t[:, :, off + r0:off + r1]

                def w2_src_sh(off):
                    return lambda i0, i1_: sh2t[:, off + i0:off + i1_, :]

                units = [dict(w1f=w1_src_moe(ee), w2f=w2_src_moe(ee), cb=ee)
                         for ee in range(EXP)]
                units.append(dict(w1f=w1_src_sh(0), w2f=w2_src_sh(0), cb=None))
                units.append(dict(w1f=w1_src_sh(NI * 256), w2f=w2_src_sh(NI), cb=None))
                if p4_units is not None:
                    units = units[:p4_units]
                nunits = len(units)
                SETS = [(0, 3), (3, 6), (6, 9), (9, 11)]
                W2G = [(0, 6), (6, 11)]

                w1_fill = [0, 0]
                w1_free = [0, 0]
                w2_fill = [0, 0]
                w2_free = [0, 0]
                acc_free_pe = 0
                x1b_free_pe = 0
                psE_free = ("x", 0)
                psA_free_act = 0
                psB_free_dve = 0
                psCD_free = [("x", 0), ("x", 0)]
                acc_w_dve = [0] * NHI
                cb_free_dve = 0
                silu_free = [0, 0]
                actb_used_pe = 0
                out_qs_prev = 0
                psTb_free = 0
                psT_free2 = 0

                def issue_w1(sy, u, si, slot):
                    p0_, p1_ = SETS[si]
                    W(sy, "sy", pe, w1_free[slot])
                    s_w1[slot].inc(sy.dma_start(
                        w1b[slot][:, 0:NHI * (p1_ - p0_) * 256],
                        units[u]["w1f"](p0_ * 256, p1_ * 256)), 16)
                    w1_fill[slot] = s_w1[slot].v

                def issue_w2(sy, u, gi, slot):
                    i0, i1_ = W2G[gi]
                    W(sy, "sy", pe, w2_free[slot])
                    s_w2[slot].inc(sy.dma_start(
                        w2b[slot][:, 0:(i1_ - i0) * HID],
                        units[u]["w2f"](i0, i1_)), 16)
                    w2_fill[slot] = s_w2[slot].v

                for ch in range(p4_chunks):
                    c0 = ch * TC

                    @blk.scalar
                    def _(s):
                        W(s, "s", pe, acc_free_pe)
                        for hi in range(NHI):
                            s_acc.inc(s.dma_start(acc[:, hi * TC:(hi + 1) * TC],
                                                  x1sp[hi][:, c0:c0 + TC]), 16)
                    acci_qa = s_acc.v

                    @blk.scalar
                    def _(s):
                        W(s, "s", s_acc, acci_qa)
                        W(s, "s", pe, x1b_free_pe)
                        for hi in range(NHI):
                            act.inc(s.activation(x1b[:, hi * TC:(hi + 1) * TC],
                                                 acc[:, hi * TC:(hi + 1) * TC], AF.Copy))
                    x1b_done = act.v

                    for tt in range(4):
                        @blk.tensor
                        def _(t):
                            W(t, "t", act, x1b_done)
                            W(t, "t", s_gwT, gwT_qs)
                            eng, val = psE_free
                            if eng == "s":
                                W(t, "t", act, val)
                            elif eng == "v":
                                W(t, "t", dve, val)
                            ins = None
                            for hi in range(NHI):
                                ins = t.matmul(
                                    psE[:, 0:EXP],
                                    x1b[:, hi * TC + tt * 128:hi * TC + (tt + 1) * 128],
                                    gwT[:, hi * EXP:(hi + 1) * EXP],
                                    start=(hi == 0), stop=(hi == NHI - 1))
                            pe.inc(ins)
                        lg_mm = pe.v

                        @blk.vector
                        def _(v):
                            W(v, "v", pe, lg_mm)
                            dve.inc(v.reduce_max(m1c[:, :], psE[:, 0:EXP], axis=AX))
                            W(v, "v", dve, dve.v)
                            dve.inc(v.tensor_scalar(negm[:, :], m1c[:, :], -1.0,
                                                    None, ALU.mult))
                        negm_done = dve.v

                        @blk.scalar
                        def _(s):
                            W(s, "s", dve, negm_done)
                            act.inc(s.activation(probs[:, :], psE[:, 0:EXP], AF.Exp,
                                                 bias=negm[:, 0:1]))
                        probs_done = act.v
                        psE_free = ("s", probs_done)

                        @blk.vector
                        def _(v):
                            W(v, "v", act, probs_done)
                            dve.inc(v.reduce_sum(smc[:, :], probs[:, :], axis=AX))
                            W(v, "v", dve, dve.v)
                            dve.inc(v.reciprocal(rrc[:, :], smc[:, :]))
                            W(v, "v", dve, dve.v)
                            dve.inc(v.tensor_scalar(probs[:, :], probs[:, :], rrc[:, 0:1],
                                                    None, ALU.mult))
                            W(v, "v", dve, dve.v)
                            dve.inc(v.tensor_copy(wrk[:, :], probs[:, :]))
                            dve.inc(v.memset(cmb[:, :], 0.0))
                            for _k in range(6):
                                W(v, "v", dve, dve.v)
                                dve.inc(v.reduce_max(m1c[:, :], wrk[:, :], axis=AX))
                                W(v, "v", dve, dve.v)
                                dve.inc(v.tensor_scalar(msk[:, :], wrk[:, :], m1c[:, 0:1],
                                                        None, ALU.is_ge))
                                W(v, "v", dve, dve.v)
                                dve.inc(v.tensor_mul(sel[:, :], wrk[:, :], msk[:, :]))
                                W(v, "v", dve, dve.v)
                                dve.inc(v.tensor_add(cmb[:, :], cmb[:, :], sel[:, :]))
                                dve.inc(v.tensor_sub(wrk[:, :], wrk[:, :], sel[:, :]))
                            W(v, "v", dve, dve.v)
                            dve.inc(v.tensor_copy(cmb_bf[:, :], cmb[:, :]))
                        cmb_done = dve.v

                        @blk.tensor
                        def _(t):
                            W(t, "t", dve, cmb_done)
                            W(t, "t", dve, psTb_free)
                            pe.inc(t.transpose(psTb[0:16, :], cmb_bf[:, :], identb[:, :]))
                        trc_mm = pe.v

                        @blk.vector
                        def _(v):
                            W(v, "v", pe, trc_mm)
                            dve.inc(v.tensor_copy(cmbT[0:16, tt * 128:(tt + 1) * 128],
                                                  psTb[0:16, :]))
                        psTb_free = dve.v
                    cmbT_done = dve.v

                    @blk.sync
                    def _(sy):
                        W(sy, "sy", dve, cmbT_done)
                        W(sy, "sy", s_cbr, cb_last_qa if ch > 0 else 0)
                        s_cmb.inc(sy.dma_start(cmbd[:, :], cmbT[:, :]), 16)
                        if nunits > 0 and ch == 0:
                            issue_w1(sy, 0, 0, 0)
                            issue_w1(sy, 0, 1, 1)
                            issue_w2(sy, 0, 0, 0)
                    cmb_sp = s_cmb.v

                    cb_done_act = act.v

                    for u in range(nunits):
                        cbi = units[u]["cb"]
                        if cbi is not None:
                            @blk.scalar
                            def _(s):
                                W(s, "s", s_cmb, cmb_sp)
                                W(s, "s", act, cb_done_act)
                                s_cbr.inc(s.dma_start(cbrow[:, :], cmbd[cbi:cbi + 1, :]), 16)
                            cbrow_qa = s_cbr.v
                            cb_last_qa = s_cbr.v

                            @blk.tensor
                            def _(t):
                                W(t, "t", s_cbr, cbrow_qa)
                                eng, val = psE_free
                                if eng == "s":
                                    W(t, "t", act, val)
                                elif eng == "v":
                                    W(t, "t", dve, val)
                                pe.inc(t.matmul(psE[:, :], ones_row[:, :], cbrow[0:1, :],
                                                start=True, stop=True))
                            cb_mm = pe.v

                            @blk.scalar
                            def _(s):
                                W(s, "s", pe, cb_mm)
                                W(s, "s", dve, cb_free_dve)
                                act.inc(s.activation(cbb[:, :], psE[:, :], AF.Copy))
                            cb_done_act = act.v
                            psE_free = ("s", cb_done_act)

                        for si in range(4):
                            sslot = si % 2
                            p0_, p1_ = SETS[si]
                            LBW = (p1_ - p0_) * 256
                            for j in range(p0_, p1_):
                                jl = j - p0_

                                @blk.tensor
                                def _(t):
                                    if j == p0_:
                                        W(t, "t", s_w1[sslot], w1_fill[sslot])
                                    W(t, "t", act, psA_free_act)
                                    ins = None
                                    for hi in range(NHI):
                                        ins = t.matmul(
                                            psA[:, :],
                                            w1b[sslot][:, hi * LBW + jl * 256:hi * LBW + jl * 256 + 128],
                                            x1b[:, hi * TC:(hi + 1) * TC],
                                            start=(hi == 0), stop=(hi == NHI - 1))
                                    pe.inc(ins)
                                    W(t, "t", dve, psB_free_dve)
                                    ins = None
                                    for hi in range(NHI):
                                        ins = t.matmul(
                                            psB[:, :],
                                            w1b[sslot][:, hi * LBW + jl * 256 + 128:hi * LBW + (jl + 1) * 256],
                                            x1b[:, hi * TC:(hi + 1) * TC],
                                            start=(hi == 0), stop=(hi == NHI - 1))
                                    pe.inc(ins)
                                gu_mm = pe.v
                                w1_free[sslot] = gu_mm

                                tslot = j % 2

                                @blk.scalar
                                def _(s):
                                    W(s, "s", pe, gu_mm - 1)
                                    W(s, "s", dve, silu_free[tslot])
                                    act.inc(s.activation(tsil[tslot][:, :], psA[:, :],
                                                         AF.Silu))
                                psA_free_act = act.v
                                silu_done = act.v

                                @blk.vector
                                def _(v):
                                    W(v, "v", act, silu_done)
                                    W(v, "v", pe, gu_mm)
                                    W(v, "v", pe, actb_used_pe)
                                    if cbi is not None:
                                        W(v, "v", act, cb_done_act)
                                        dve.inc(v.tensor_mul(actb[:, j * TC:(j + 1) * TC],
                                                             tsil[tslot][:, :], psB[:, :]))
                                        W(v, "v", dve, dve.v)
                                        dve.inc(v.tensor_mul(actb[:, j * TC:(j + 1) * TC],
                                                             actb[:, j * TC:(j + 1) * TC],
                                                             cbb[:, :]))
                                    else:
                                        dve.inc(v.tensor_mul(actb[:, j * TC:(j + 1) * TC],
                                                             tsil[tslot][:, :], psB[:, :]))
                                psB_free_dve = dve.v
                                silu_free[tslot] = dve.v

                            @blk.sync
                            def _(sy):
                                if si + 2 <= 3:
                                    issue_w1(sy, u, si + 2, (si + 2) % 2)
                                elif u + 1 < nunits:
                                    nsi = si + 2 - 4
                                    issue_w1(sy, u + 1, nsi, nsi % 2)
                                elif ch + 1 < p4_chunks:
                                    nsi = si + 2 - 4
                                    issue_w1(sy, 0, nsi, nsi % 2)
                        act_done_dve = dve.v
                        cb_free_dve = dve.v

                        for gi in range(2):
                            gslot = gi % 2
                            i0, i1_ = W2G[gi]
                            for ht in range(NHI):
                                pw = psC if ht % 2 == 0 else psD

                                @blk.tensor
                                def _(t):
                                    if ht == 0:
                                        W(t, "t", s_w2[gslot], w2_fill[gslot])
                                        W(t, "t", dve, act_done_dve)
                                    eng, val = psCD_free[ht % 2]
                                    if eng == "v":
                                        W(t, "t", dve, val)
                                    ins = None
                                    for ii in range(i0, i1_):
                                        ins = t.matmul(
                                            pw[:, :],
                                            w2b[gslot][:, (ii - i0) * HID + ht * 128:(ii - i0) * HID + (ht + 1) * 128],
                                            actb[:, ii * TC:(ii + 1) * TC],
                                            start=(ii == i0), stop=(ii == i1_ - 1))
                                    pe.inc(ins)
                                w2_mm = pe.v
                                w2_free[gslot] = w2_mm
                                if gi == 1:
                                    actb_used_pe = w2_mm

                                @blk.vector
                                def _(v):
                                    W(v, "v", pe, w2_mm)
                                    W(v, "v", dve, acc_w_dve[ht])
                                    W(v, "v", s_acc, acci_qa)
                                    dve.inc(v.tensor_add(acc[:, ht * TC:(ht + 1) * TC],
                                                         acc[:, ht * TC:(ht + 1) * TC],
                                                         pw[:, :]))
                                acc_w_dve[ht] = dve.v
                                psCD_free[ht % 2] = ("v", dve.v)

                            @blk.sync
                            def _(sy):
                                if gi + 1 < 2:
                                    issue_w2(sy, u, gi + 1, 1)
                                elif u + 1 < nunits:
                                    issue_w2(sy, u + 1, 0, 0)
                                elif ch + 1 < p4_chunks:
                                    issue_w2(sy, 0, 0, 0)

                    all_acc_dve = dve.v
                    for tt in range(4):
                        for ht in range(NHI):
                            @blk.tensor
                            def _(t):
                                W(t, "t", dve, all_acc_dve)
                                W(t, "t", dve, psT_free2)
                                pe.inc(t.transpose(
                                    psT[:, :],
                                    acc[:, ht * TC + tt * 128:ht * TC + (tt + 1) * 128],
                                    identf[:, :]))
                            trm = pe.v

                            @blk.vector
                            def _(v):
                                W(v, "v", pe, trm)
                                if ht == 0:
                                    W(v, "v", dve, dve.v)
                                    W(v, "v", s_out, out_qs_prev)
                                dve.inc(v.tensor_copy(otm[:, ht * 128:(ht + 1) * 128],
                                                      psT[:, :]))
                            psT_free2 = dve.v
                        otm_done = dve.v

                        @blk.vector
                        def _(v):
                            W(v, "v", dve, otm_done)
                            dve.inc(v.reduce_sum(fmc[:, :], otm[:, :], axis=AX))
                            W(v, "v", dve, dve.v)
                            dve.inc(v.tensor_scalar(fmc[:, :], fmc[:, :], 1.0 / HID,
                                                    None, ALU.mult))
                            W(v, "v", dve, dve.v)
                            dve.inc(v.tensor_scalar(otm[:, :], otm[:, :], fmc[:, 0:1],
                                                    None, ALU.subtract))
                            W(v, "v", dve, dve.v)
                            dve.inc(v.tensor_mul(fsq[:, :], otm[:, :], otm[:, :]))
                            W(v, "v", dve, dve.v)
                            dve.inc(v.reduce_sum(fvc[:, :], fsq[:, :], axis=AX))
                            W(v, "v", dve, dve.v)
                            dve.inc(v.tensor_scalar(fvc[:, :], fvc[:, :], 1.0 / HID,
                                                    LN_EPS, ALU.mult, ALU.add))
                        fvar_done = dve.v

                        @blk.scalar
                        def _(s):
                            W(s, "s", dve, fvar_done)
                            act.inc(s.activation(frs[:, :], fvc[:, :], AF.Ln))
                            W(s, "s", act, act.v)
                            act.inc(s.activation(frs[:, :], frs[:, :], AF.Exp, scale=-0.5))
                        frs_done = act.v

                        @blk.vector
                        def _(v):
                            W(v, "v", act, frs_done)
                            dve.inc(v.tensor_scalar(otm[:, :], otm[:, :], frs[:, 0:1],
                                                    None, ALU.mult))
                        out_done = dve.v

                        @blk.sync
                        def _(sy):
                            W(sy, "sy", dve, out_done)
                            s_out.inc(sy.dma_start(
                                y[c0 + tt * 128:c0 + (tt + 1) * 128, :], otm[:, :]), 16)
                        out_qs_prev = s_out.v
                    acc_free_pe = pe.v
                    x1b_free_pe = pe.v

        @blk.sync
        def _(sy):
            for sm in bld.all_sems:
                W(sy, "sy", sm, sm.v)

    return nc


_IN_NAMES = [
    "L0_attn_hidden_states", "L0_attn_W_DKV", "L0_attn_b_DKV", "L0_attn_W_UK",
    "L0_attn_b_UK", "L0_attn_W_UV", "L0_attn_b_UV", "L0_attn_W_DQ", "L0_attn_b_DQ",
    "L0_attn_W_UQ", "L0_attn_b_UQ", "L0_attn_W_QR", "L0_attn_W_KR", "L0_attn_W_O",
    "L0_attn_b_O", "L0_ffn_gate_w", "L0_ffn_w1", "L0_ffn_w2", "L0_ffn_shared_w1",
    "L0_ffn_shared_w2",
]


def kernel(**inputs):
    nc = build()
    x = np.ascontiguousarray(np.asarray(inputs["L0_attn_hidden_states"], np.float32))
    weights = {k: np.ascontiguousarray(np.asarray(inputs[k], np.float32))
               for k in _IN_NAMES if k != "L0_attn_hidden_states"}
    in_maps = []
    for b in range(NB):
        m = dict(weights)
        m["L0_attn_hidden_states"] = x[b]
        in_maps.append(m)
    res = run_bass_kernel_spmd(nc, in_maps, core_ids=list(range(NB)))
    return np.stack([np.asarray(res.results[b]["y"], np.float32) for b in range(NB)], axis=0)


# version 05
# speedup vs baseline: 1.2904x; 1.1697x over previous
import math
import os
from contextlib import ExitStack

os.environ.setdefault("GAUGE_NTFF_IGNORE_DMA", "1")

import numpy as np
import ml_dtypes

import concourse.bass as bass
import concourse.mybir as mybir
from concourse.bass_utils import run_bass_kernel_spmd

F32 = mybir.dt.float32
BF16 = mybir.dt.bfloat16
AF = mybir.ActivationFunctionType
ALU = mybir.AluOpType
AX = mybir.AxisListType.X

S = 4096
HID = 2048
NHI = 16
NH = 16
DH = 128
KVC = 512
NKV = 4
QC = 1536
NQC = 12
RD = 64
HF = 32
EXP = 16
I1 = 1408
NI = 11
ISZ = 2816
NIS = 22
TB = 512
NTB = 8
QBW = 512
NQB = 8
TC = 512
NCH = 8
NT = 32
INVSQ = 1.0 / math.sqrt(128.0)
LN_EPS = 1e-5
NB = 8


class Sem:
    __slots__ = ("h", "v", "name")

    def __init__(self, h, name):
        self.h = h
        self.v = 0
        self.name = name

    def inc(self, instr, n=1):
        instr.then_inc(self.h, n)
        self.v += n
        return self.v


class Bld:
    def __init__(self, nc, st):
        self.nc = nc
        self.st = st
        self.hw = {}
        self.all_sems = []

    def sem(self, name):
        s = Sem(self.st.enter_context(self.nc.semaphore(name)), name)
        self.all_sems.append(s)
        return s

    def W(self, eng, engname, sem, val):
        if val <= 0:
            return
        key = (engname, sem.name)
        if self.hw.get(key, -1) >= val:
            return
        eng.wait_ge(sem.h, val)
        self.hw[key] = val


def rope_tables():
    inv_freq = 1.0 / 10000.0 ** (np.arange(0, RD, 2, dtype=np.float64) / RD)
    t = np.arange(S, dtype=np.float64)
    fr = t[None, :] * inv_freq[:, None]
    return (np.cos(fr).astype(ml_dtypes.bfloat16),
            np.sin(fr).astype(ml_dtypes.bfloat16))


def build(phases=("p1a", "p1b", "p2", "p3", "p4"), debug=False, p4_chunks=NCH, p4_units=None):
    nc = bass.Bass("TRN2", target_bir_lowering=False)
    dk = "ExternalOutput" if debug else "Internal"

    x_in = nc.dram_tensor("L0_attn_hidden_states", [S, HID], F32, kind="ExternalInput").ap()
    W_DKV = nc.dram_tensor("L0_attn_W_DKV", [HID, KVC], F32, kind="ExternalInput").ap()
    b_DKV = nc.dram_tensor("L0_attn_b_DKV", [KVC], F32, kind="ExternalInput").ap()
    W_UK = nc.dram_tensor("L0_attn_W_UK", [KVC, HID], F32, kind="ExternalInput").ap()
    b_UK = nc.dram_tensor("L0_attn_b_UK", [HID], F32, kind="ExternalInput").ap()
    W_UV = nc.dram_tensor("L0_attn_W_UV", [KVC, HID], F32, kind="ExternalInput").ap()
    b_UV = nc.dram_tensor("L0_attn_b_UV", [HID], F32, kind="ExternalInput").ap()
    W_DQ = nc.dram_tensor("L0_attn_W_DQ", [HID, QC], F32, kind="ExternalInput").ap()
    b_DQ = nc.dram_tensor("L0_attn_b_DQ", [QC], F32, kind="ExternalInput").ap()
    W_UQ = nc.dram_tensor("L0_attn_W_UQ", [QC, HID], F32, kind="ExternalInput").ap()
    b_UQ = nc.dram_tensor("L0_attn_b_UQ", [HID], F32, kind="ExternalInput").ap()
    W_QR = nc.dram_tensor("L0_attn_W_QR", [DH, RD], F32, kind="ExternalInput").ap()
    W_KR = nc.dram_tensor("L0_attn_W_KR", [DH, RD], F32, kind="ExternalInput").ap()
    W_O = nc.dram_tensor("L0_attn_W_O", [HID, HID], F32, kind="ExternalInput").ap()
    b_O = nc.dram_tensor("L0_attn_b_O", [HID], F32, kind="ExternalInput").ap()
    gate_w = nc.dram_tensor("L0_ffn_gate_w", [EXP, HID], F32, kind="ExternalInput").ap()
    w1 = nc.dram_tensor("L0_ffn_w1", [EXP, 2 * I1, HID], F32, kind="ExternalInput").ap()
    w2 = nc.dram_tensor("L0_ffn_w2", [EXP, HID, I1], F32, kind="ExternalInput").ap()
    sh_w1 = nc.dram_tensor("L0_ffn_shared_w1", [2 * ISZ, HID], F32, kind="ExternalInput").ap()
    sh_w2 = nc.dram_tensor("L0_ffn_shared_w2", [HID, ISZ], F32, kind="ExternalInput").ap()

    y = nc.dram_tensor("y", [S, HID], F32, kind="ExternalOutput").ap()

    xs = nc.dram_tensor("xs", [NHI, S, 128], BF16, kind="Internal").ap()
    ksp = nc.dram_tensor("ksp", [NH, 128, S], BF16, kind=dk).ap()
    qsp = nc.dram_tensor("qsp", [NH, 128, S], BF16, kind=dk).ap()
    vsp = nc.dram_tensor("vsp", [S, NH, 128], BF16, kind=dk).ap()
    csp = nc.dram_tensor("csp", [NH, 128, S], BF16, kind=dk).ap()
    x1sp = nc.dram_tensor("x1sp", [NHI, 128, S], F32, kind=dk).ap()
    w1s = nc.dram_tensor("w1s", [EXP, NHI, 2 * I1, 128], BF16, kind="Internal").ap()
    w2s = nc.dram_tensor("w2s", [EXP, NI, HID, 128], BF16, kind="Internal").ap()
    sh1s = nc.dram_tensor("sh1s", [NHI, 2 * ISZ, 128], BF16, kind="Internal").ap()
    sh2s = nc.dram_tensor("sh2s", [NIS, HID, 128], BF16, kind="Internal").ap()
    w1t = nc.dram_tensor("w1t", [EXP, 128, NHI, 2 * I1], BF16, kind="Internal").ap()
    w2t = nc.dram_tensor("w2t", [EXP, 128, NI, HID], BF16, kind="Internal").ap()
    sh1t = nc.dram_tensor("sh1t", [128, NHI, 2 * ISZ], BF16, kind="Internal").ap()
    sh2t = nc.dram_tensor("sh2t", [128, NIS, HID], BF16, kind="Internal").ap()
    gws = nc.dram_tensor("gws", [EXP, HID], BF16, kind="Internal").ap()
    cmbd = nc.dram_tensor("cmbd", [EXP, TC], BF16, kind=dk).ap()
    sdbg = nc.dram_tensor("sdbg", [NT, 128, 512], F32, kind=dk).ap() if debug else None
    pdbg = nc.dram_tensor("pdbg", [NT, 128, 512], BF16, kind=dk).ap() if debug else None

    cos_np, sin_np = rope_tables()
    cosT = nc.inline_tensor(np.ascontiguousarray(cos_np), "cosT").ap()
    sinT = nc.inline_tensor(np.ascontiguousarray(sin_np), "sinT").ap()
    identb_t = nc.inline_tensor(np.eye(128, dtype=ml_dtypes.bfloat16), "identb_t").ap()
    identf_t = nc.inline_tensor(np.eye(128, dtype=np.float32), "identf_t").ap()
    ones_row_t = nc.inline_tensor(np.ones((1, 128), dtype=ml_dtypes.bfloat16), "ones_row_t").ap()
    ones_rowf_t = nc.inline_tensor(np.ones((1, 128), dtype=np.float32), "ones_rowf_t").ap()
    ones_mat_t = nc.inline_tensor(np.ones((128, 128), dtype=ml_dtypes.bfloat16), "ones_mat_t").ap()
    onesw_t = nc.inline_tensor(np.ones((1, S), dtype=ml_dtypes.bfloat16), "onesw_t").ap()

    with ExitStack() as st:
        blk = st.enter_context(nc.Block())
        bld = Bld(nc, st)
        W = bld.W
        pe = bld.sem("pe")
        act = bld.sem("act")
        dve = bld.sem("dve")
        s_const = bld.sem("s_const")
        s_xs = bld.sem("s_xs")
        s_scr = bld.sem("s_scr")
        s_out = bld.sem("s_out")
        s_xfm = [bld.sem("s_xfm0"), bld.sem("s_xfm1")]
        s_kq = [bld.sem("s_kq0"), bld.sem("s_kq1")]
        s_w1 = [bld.sem("s_w10"), bld.sem("s_w11")]
        s_w2 = [bld.sem("s_w20"), bld.sem("s_w21")]

        e = st.enter_context
        identb = e(nc.sbuf_tensor("identb", [128, 128], BF16))
        identf = e(nc.sbuf_tensor("identf", [128, 128], F32))
        ones_row = e(nc.sbuf_tensor("ones_row", [1, 128], BF16))
        ones_rowf = e(nc.sbuf_tensor("ones_rowf", [1, 128], F32))
        ones_mat = e(nc.sbuf_tensor("ones_mat", [128, 128], BF16))
        onesw = e(nc.sbuf_tensor("onesw", [1, S], BF16))
        tbf = [e(nc.sbuf_tensor(f"tbf{i}", [128, ISZ], BF16)) for i in range(2)]
        s_tb = [bld.sem("s_tb0"), bld.sem("s_tb1")]
        s_tw = [bld.sem("s_tw0"), bld.sem("s_tw1")]

        @blk.sync
        def _(sy):
            s_const.inc(sy.dma_start(identb[:, :], identb_t[:, :]), 16)
            s_const.inc(sy.dma_start(identf[:, :], identf_t[:, :]), 16)
            s_const.inc(sy.dma_start(ones_row[:, :], ones_row_t[:, :]), 16)
            s_const.inc(sy.dma_start(ones_rowf[:, :], ones_rowf_t[:, :]), 16)
            s_const.inc(sy.dma_start(ones_mat[:, :], ones_mat_t[:, :]), 16)
            s_const.inc(sy.dma_start(onesw[:, :], onesw_t[:, :]), 16)
        consts_qs = s_const.v

        barrier_exempt = {"s_scr", "s_tb0", "s_tb1", "s_tw0", "s_tw1"}

        def barrier():
            tgt = [(sm, sm.v) for sm in bld.all_sems
                   if sm.name not in barrier_exempt]

            def waits(engobj, ename):
                for sm, v_ in tgt:
                    W(engobj, ename, sm, v_)

            @blk.tensor
            def _(t):
                waits(t, "t")

            @blk.scalar
            def _(s):
                waits(s, "s")

            @blk.vector
            def _(v):
                waits(v, "v")

            @blk.gpsimd
            def _(g):
                waits(g, "g")

            @blk.sync
            def _(sy):
                waits(sy, "sy")

        @blk.gpsimd
        def _(g):
            for hi in range(NHI):
                for tcn in range(8):
                    s_xs.inc(g.dma_start(
                        xs[hi, tcn * 512:(tcn + 1) * 512, :],
                        x_in[tcn * 512:(tcn + 1) * 512, hi * 128:(hi + 1) * 128]), 16)
        xs_done_qg = s_xs.v

        ffn_scratch_qg = 0
        st0_jobs = []
        st0_cursor = [0]
        st1_jobs = []
        st1_cursor = [0]
        if "p4" in phases:
            st0_jobs.append((gws[:, :], gate_w[:, :]))
            scr_mark = {}
            for ee in range(EXP):
                for j in range(NI):
                    for part in range(2):
                        src_base = (0 if part == 0 else I1) + j * 128
                        for hg in range(2):
                            st0_jobs.append((
                                w1s[ee, hg * 8:(hg + 1) * 8,
                                    j * 256 + part * 128:j * 256 + (part + 1) * 128, :],
                                w1[ee, src_base:src_base + 128,
                                   hg * 1024:(hg + 1) * 1024]
                                .rearrange("o (hi p) -> hi o p", p=128)))
                for ii in range(NI):
                    for hg in range(2):
                        st0_jobs.append((
                            w2s[ee, ii, hg * 1024:(hg + 1) * 1024, :],
                            w2[ee, hg * 1024:(hg + 1) * 1024, ii * 128:(ii + 1) * 128]))
                scr_mark[ee] = 16 * len(st0_jobs)
            for j in range(NIS):
                for part in range(2):
                    src_base = (0 if part == 0 else ISZ) + j * 128
                    for hg in range(2):
                        st0_jobs.append((
                            sh1s[hg * 8:(hg + 1) * 8,
                                 j * 256 + part * 128:j * 256 + (part + 1) * 128, :],
                            sh_w1[src_base:src_base + 128, hg * 1024:(hg + 1) * 1024]
                            .rearrange("o (hi p) -> hi o p", p=128)))
            scr_mark["sh1"] = 16 * len(st0_jobs)
            for ii in range(NIS):
                for hg in range(2):
                    st0_jobs.append((
                        sh2s[ii, hg * 1024:(hg + 1) * 1024, :],
                        sh_w2[hg * 1024:(hg + 1) * 1024, ii * 128:(ii + 1) * 128]))
            scr_mark["sh2"] = 16 * len(st0_jobs)
            ffn_scratch_qg = 16 * len(st0_jobs)

            for ee in range(EXP):
                for hi in range(NHI):
                    st1_jobs.append((w1s[ee, hi], 2 * I1, w1t[ee, :, hi, :],
                                     scr_mark[ee]))
                for ii in range(NI):
                    st1_jobs.append((w2s[ee, ii], HID, w2t[ee, :, ii, :],
                                     scr_mark[ee]))
            for hi in range(NHI):
                for hf in range(2):
                    st1_jobs.append((sh1s[hi, hf * ISZ:(hf + 1) * ISZ, :], ISZ,
                                     sh1t[:, hi, hf * ISZ:(hf + 1) * ISZ],
                                     scr_mark["sh1"]))
            for ii in range(NIS):
                st1_jobs.append((sh2s[ii], HID, sh2t[:, ii, :], scr_mark["sh2"]))

        def emit_stage0(n):
            i0 = st0_cursor[0]
            i1 = min(i0 + n, len(st0_jobs))
            if i0 >= i1:
                return
            st0_cursor[0] = i1

            @blk.gpsimd
            def _(g):
                for i in range(i0, i1):
                    dst, srcp = st0_jobs[i]
                    s_scr.inc(g.dma_start(dst, srcp), 16)

        def emit_stage1(n):
            i0 = st1_cursor[0]
            i1 = min(i0 + n, len(st1_jobs))
            if i0 >= i1:
                return
            st1_cursor[0] = i1

            @blk.sync
            def _(sy):
                for i in range(i0, i1):
                    srcp, wdt, dst, mark = st1_jobs[i]
                    slot = i % 2
                    W(sy, "sy", s_scr, mark)
                    if i >= 2:
                        sy.wait_ge(s_tw[slot].h, 16 * (i // 2))
                    s_tb[slot].inc(sy.dma_start(
                        tbf[slot][:, 0:wdt], srcp, transpose=True), 16)

            @blk.gpsimd
            def _(g):
                for i in range(i0, i1):
                    srcp, wdt, dst, mark = st1_jobs[i]
                    slot = i % 2
                    g.wait_ge(s_tb[slot].h, 16 * (i // 2 + 1))
                    s_tw[slot].inc(g.dma_start(dst, tbf[slot][:, 0:wdt]), 16)

        def p1(is_q):
            NC_ = NQC if is_q else NKV
            Wd, Wu, Wr = (W_DQ, W_UQ, W_QR) if is_q else (W_DKV, W_UK, W_KR)
            bd_d, bu_d = (b_DQ, b_UQ) if is_q else (b_DKV, b_UK)
            sp = qsp if is_q else ksp
            tag = "q" if is_q else "k"
            with ExitStack() as ps:
                e2 = ps.enter_context
                cosb = e2(nc.sbuf_tensor(f"cosb{tag}", [32, S], BF16))
                sinb = e2(nc.sbuf_tensor(f"sinb{tag}", [32, S], BF16))
                xfm = [e2(nc.sbuf_tensor(f"xfm{tag}{i}", [128, NHI * TB], BF16)) for i in range(2)]
                psA = e2(nc.psum_tensor(f"psA{tag}", [128, 512], F32))
                psB = e2(nc.psum_tensor(f"psB{tag}", [128, 512], F32))
                psC = e2(nc.psum_tensor(f"psC{tag}", [128, 512], F32))
                psD = e2(nc.psum_tensor(f"psD{tag}", [128, 512], F32))
                psF = [e2(nc.psum_tensor(f"psF{tag}{i}", [128, 512], F32)) for i in range(2)]
                wd = e2(nc.sbuf_tensor(f"wd{tag}", [128, NHI * NC_ * 128], BF16))
                wu = e2(nc.sbuf_tensor(f"wu{tag}", [128, NC_ * HID], BF16))
                wr = e2(nc.sbuf_tensor(f"wr{tag}", [128, RD], BF16))
                bd = e2(nc.sbuf_tensor(f"bd{tag}", [128, NC_], F32))
                bu = e2(nc.sbuf_tensor(f"bu{tag}", [128, NH], F32))
                cc = e2(nc.sbuf_tensor(f"cc{tag}", [128, NC_ * TB], BF16))
                kfull = [e2(nc.sbuf_tensor(f"kf{tag}{i}", [128, TB], BF16)) for i in range(2)]
                kst = [e2(nc.sbuf_tensor(f"kst{tag}{i}", [128, TB], BF16)) for i in range(2)]
                rt1 = e2(nc.sbuf_tensor(f"rt1{tag}", [32, TB], F32))
                rt2 = e2(nc.sbuf_tensor(f"rt2{tag}", [32, TB], F32))
                if not is_q:
                    wuv = e2(nc.sbuf_tensor("wuv", [128, NKV * HID], BF16))
                    buvr = e2(nc.sbuf_tensor("buvr", [1, HID], F32))
                    buvb = e2(nc.sbuf_tensor("buvb", [128, HID], F32))
                    vwide = [e2(nc.sbuf_tensor(f"vw{i}", [128, HID], BF16)) for i in range(2)]

                s_kst = [bld.sem(f"s_kst{tag}0"), bld.sem(f"s_kst{tag}1")]
                s_ldw = bld.sem(f"s_ldw{tag}")
                s_bias = bld.sem(f"s_bias{tag}")
                s_tab = bld.sem(f"s_tab{tag}")
                s_vw = [bld.sem(f"s_vw{tag}0"), bld.sem(f"s_vw{tag}1")] if not is_q else None

                @blk.gpsimd
                def _(g):
                    for hi in range(NHI):
                        s_ldw.inc(g.dma_start(
                            wd[:, hi * NC_ * 128:(hi + 1) * NC_ * 128],
                            Wd[hi * 128:(hi + 1) * 128, :]), 16)
                    for ci in range(NC_):
                        s_ldw.inc(g.dma_start(wu[:, ci * HID:(ci + 1) * HID],
                                              Wu[ci * 128:(ci + 1) * 128, :]), 16)
                    s_ldw.inc(g.dma_start(wr[:, :], Wr[:, :]), 16)
                    if not is_q:
                        for ci in range(NKV):
                            s_ldw.inc(g.dma_start(wuv[:, ci * HID:(ci + 1) * HID],
                                                  W_UV[ci * 128:(ci + 1) * 128, :]), 16)
                ld_qg = s_ldw.v

                @blk.scalar
                def _(s):
                    for ci in range(NC_):
                        s_bias.inc(s.dma_start(bd[:, ci:ci + 1],
                                               bd_d[ci * 128:(ci + 1) * 128][:, None]), 16)
                    for h in range(NH):
                        s_bias.inc(s.dma_start(bu[:, h:h + 1],
                                               bu_d[h * 128:(h + 1) * 128][:, None]), 16)
                    if not is_q:
                        s_bias.inc(s.dma_start(buvr[:, :], b_UV[None, :]), 16)
                ld_qa = s_bias.v

                @blk.sync
                def _(sy):
                    s_tab.inc(sy.dma_start(cosb[:, :], cosT[:, :]), 16)
                    s_tab.inc(sy.dma_start(sinb[:, :], sinT[:, :]), 16)
                tab_qs = s_tab.v

                @blk.vector
                def _(v):
                    for i in range(2):
                        dve.inc(v.memset(kst[i][0:32, :], 0.0))
                        if not is_q:
                            W(v, "v", dve, dve.v)
                            dve.inc(v.memset(kst[i][0:1, :], 1.0))
                kst_init = dve.v

                buv_dve = 0
                if not is_q:
                    bb_dve = []
                    for i in range(4):
                        p_ = psA if i % 2 == 0 else psB

                        @blk.tensor
                        def _(t):
                            W(t, "t", s_bias, ld_qa)
                            W(t, "t", s_const, consts_qs)
                            if i >= 2:
                                W(t, "t", dve, bb_dve[i - 2])
                            pe.inc(t.matmul(p_[:, :], ones_rowf[:, :],
                                            buvr[0:1, i * 512:(i + 1) * 512],
                                            start=True, stop=True))
                        mmv = pe.v

                        @blk.vector
                        def _(v):
                            W(v, "v", pe, mmv)
                            dve.inc(v.tensor_copy(buvb[:, i * 512:(i + 1) * 512], p_[:, :]))
                            bb_dve.append(dve.v)
                    buv_dve = dve.v

                xfm_fill = [0, 0]

                def issue_xfm(sy, tb):
                    slot = tb % 2
                    for hi in range(NHI):
                        s_xfm[slot].inc(sy.dma_start(
                            xfm[slot][:, hi * TB:(hi + 1) * TB],
                            xs[hi, tb * TB:(tb + 1) * TB, :], transpose=True), 16)
                    xfm_fill[slot] = s_xfm[slot].v

                @blk.sync
                def _(sy):
                    W(sy, "sy", s_xs, xs_done_qg)
                    issue_xfm(sy, 0)

                psAB_free = [0, 0]
                psCD_free = [("x", 0), ("x", 0)]
                psF_free = [0, 0]
                kf_free = [(0, 0), (0, 0)]
                kst_sp = [0, 0]
                vw_sp = [0, 0]
                cc_done_prev = 0

                for tb in range(NTB):
                    slot = tb % 2
                    emit_stage0(40)

                    @blk.sync
                    def _(sy):
                        if tb + 1 < NTB:
                            W(sy, "sy", pe, cc_done_prev)
                            issue_xfm(sy, tb + 1)

                    cc_mm = [0] * NC_
                    cc_ev = [0] * NC_
                    for ci in range(NC_):
                        p_ = psA if ci % 2 == 0 else psB

                        @blk.tensor
                        def _(t):
                            W(t, "t", s_xfm[slot], xfm_fill[slot])
                            W(t, "t", s_ldw, ld_qg)
                            W(t, "t", dve, buv_dve)
                            W(t, "t", act, psAB_free[ci % 2])
                            ins = None
                            for hi in range(NHI):
                                ins = t.matmul(
                                    p_[:, :],
                                    wd[:, (hi * NC_ + ci) * 128:(hi * NC_ + ci + 1) * 128],
                                    xfm[slot][:, hi * TB:(hi + 1) * TB],
                                    start=(hi == 0), stop=(hi == NHI - 1))
                            cc_mm[ci] = pe.inc(ins)

                        @blk.scalar
                        def _(s):
                            W(s, "s", pe, cc_mm[ci])
                            W(s, "s", s_bias, ld_qa)
                            cc_ev[ci] = act.inc(s.activation(
                                cc[:, ci * TB:(ci + 1) * TB], p_[:, :],
                                AF.Identity, bias=bd[:, ci:ci + 1]))
                        psAB_free[ci % 2] = cc_ev[ci]
                    cc_done_prev = cc_mm[NC_ - 1]
                    cc_all_ev = cc_ev[NC_ - 1]

                    for h in range(NH):
                        hp = h % 2
                        p_ = psC if hp == 0 else psD

                        @blk.tensor
                        def _(t):
                            W(t, "t", act, cc_all_ev)
                            eng, val = psCD_free[hp]
                            if eng == "act":
                                W(t, "t", act, val)
                            elif eng == "dve":
                                W(t, "t", dve, val)
                            ins = None
                            for ci in range(NC_):
                                ins = t.matmul(
                                    p_[:, :],
                                    wu[:, ci * HID + h * 128:ci * HID + (h + 1) * 128],
                                    cc[:, ci * TB:(ci + 1) * TB],
                                    start=(ci == 0), stop=(ci == NC_ - 1))
                            pe.inc(ins)
                        k_mm = pe.v

                        @blk.scalar
                        def _(s):
                            W(s, "s", pe, k_mm)
                            W(s, "s", pe, kf_free[hp][0])
                            W(s, "s", dve, kf_free[hp][1])
                            act.inc(s.activation(kfull[hp][:, :], p_[:, :],
                                                 AF.Identity, bias=bu[:, h:h + 1]))
                        kf_ev = act.v
                        psCD_free[hp] = ("act", kf_ev)

                        pf = psF[hp]

                        @blk.tensor
                        def _(t):
                            W(t, "t", act, kf_ev)
                            W(t, "t", dve, psF_free[hp])
                            pe.inc(t.matmul(pf[0:RD, :], wr[:, :], kfull[hp][:, :],
                                            start=True, stop=True))
                        kr_mm = pe.v

                        @blk.vector
                        def _(v):
                            W(v, "v", pe, kr_mm)
                            W(v, "v", dve, dve.v)
                            W(v, "v", s_tab, tab_qs)
                            W(v, "v", s_kst[hp], kst_sp[hp])
                            W(v, "v", dve, kst_init)
                            dve.inc(v.tensor_mul(rt1[:, :], pf[0:HF, :],
                                                 cosb[:, tb * TB:(tb + 1) * TB]))
                            dve.inc(v.tensor_mul(rt2[:, :], pf[HF:RD, :],
                                                 sinb[:, tb * TB:(tb + 1) * TB]))
                            W(v, "v", dve, dve.v)
                            dve.inc(v.tensor_sub(kst[hp][32:64, :], rt1[:, :], rt2[:, :]))
                            W(v, "v", act, kf_ev)
                            dve.inc(v.tensor_copy(kst[hp][64:128, :], kfull[hp][64:128, :]))
                        rope_done = dve.v
                        psF_free[hp] = rope_done
                        kf_free[hp] = (kr_mm, rope_done)

                        @blk.sync
                        def _(sy):
                            W(sy, "sy", dve, rope_done)
                            s_kst[hp].inc(sy.dma_start(
                                sp[h][:, tb * TB:(tb + 1) * TB], kst[hp][:, :]), 16)
                        kst_sp[hp] = s_kst[hp].v

                    if not is_q:
                        for kt in range(4):
                            vslot = (tb * 4 + kt) % 2
                            for h in range(NH):
                                hp2 = (kt * NH + h) % 2
                                p_ = psC if hp2 == 0 else psD

                                @blk.tensor
                                def _(t):
                                    eng, val = psCD_free[hp2]
                                    if eng == "act":
                                        W(t, "t", act, val)
                                    elif eng == "dve":
                                        W(t, "t", dve, val)
                                    ins = None
                                    for ci in range(NKV):
                                        ins = t.matmul(
                                            p_[:, 0:128],
                                            cc[:, ci * TB + kt * 128:ci * TB + (kt + 1) * 128],
                                            wuv[:, ci * HID + h * 128:ci * HID + (h + 1) * 128],
                                            start=(ci == 0), stop=(ci == NKV - 1))
                                    pe.inc(ins)
                                v_mm = pe.v

                                @blk.vector
                                def _(v):
                                    W(v, "v", pe, v_mm)
                                    W(v, "v", dve, buv_dve)
                                    if h == 0:
                                        W(v, "v", s_vw[vslot], vw_sp[vslot])
                                    dve.inc(v.tensor_add(
                                        vwide[vslot][:, h * 128:(h + 1) * 128],
                                        p_[:, 0:128], buvb[:, h * 128:(h + 1) * 128]))
                                psCD_free[hp2] = ("dve", dve.v)
                            vw_ready = dve.v

                            @blk.sync
                            def _(sy):
                                W(sy, "sy", dve, vw_ready)
                                s_vw[vslot].inc(sy.dma_start(
                                    vsp[tb * TB + kt * 128:tb * TB + (kt + 1) * 128, :, :]
                                    .rearrange("t h d -> t (h d)"),
                                    vwide[vslot][:, :]), 16)
                            vw_sp[vslot] = s_vw[vslot].v

        if "p1a" in phases:
            p1(False)
            barrier()
        if "p1b" in phases:
            p1(True)
            barrier()

        if "p2" in phases:
            NQH = NQB * NT
            GTOT = NH * NQH
            CMARGIN = 1.02
            with ExitStack() as ps:
                e2 = ps.enter_context
                kb = [e2(nc.sbuf_tensor(f"kb{i}", [128, S], BF16)) for i in range(2)]
                qbb = [e2(nc.sbuf_tensor(f"qbb{i}", [128, S], BF16)) for i in range(2)]
                vb = [e2(nc.sbuf_tensor(f"vb{i}", [128, NT * 128], BF16)) for i in range(2)]
                ssb = [e2(nc.sbuf_tensor(f"ssb{i}", [128, QBW], F32)) for i in range(8)]
                prb = [e2(nc.sbuf_tensor(f"prb{i}", [128, QBW], BF16)) for i in range(8)]
                ctxs = [e2(nc.sbuf_tensor(f"ctxs{i}", [128, QBW], BF16)) for i in range(2)]
                rs_sb = e2(nc.sbuf_tensor("rs_sb", [128, QBW], F32))
                se_sb = e2(nc.sbuf_tensor("se_sb", [128, QBW], F32))
                sc_sb = e2(nc.sbuf_tensor("sc_sb", [128, QBW], F32))
                sqs = e2(nc.sbuf_tensor("sqs", [128, 2 * S], BF16))
                nrm = e2(nc.sbuf_tensor("nrm", [1, 2 * S], F32))
                nsc = e2(nc.sbuf_tensor("nsc", [1, 8], F32))
                psS = [e2(nc.psum_tensor(f"psS{i}", [128, 512], F32)) for i in range(4)]
                psC = e2(nc.psum_tensor("psC2", [128, 512], F32))
                psE = e2(nc.psum_tensor("psE2", [128, 512], F32))
                psN = [e2(nc.psum_tensor(f"psN{i}", [1, 512], F32)) for i in range(2)]
                s_ctxw = [bld.sem("s_ctxw0"), bld.sem("s_ctxw1")]
                s_dbg = bld.sem("s_dbg") if debug else None
                dbg_wm = {}

                kq_fill = [0, 0]

                def load_head(sy, h):
                    slot = h % 2
                    s_kq[slot].inc(sy.dma_start(kb[slot][:, :], ksp[h][:, :]), 16)
                    s_kq[slot].inc(sy.dma_start(qbb[slot][:, :], qsp[h][:, :]), 16)
                    for kt in range(NT):
                        s_kq[slot].inc(sy.dma_start(
                            vb[slot][:, kt * 128:(kt + 1) * 128],
                            vsp[kt * 128:(kt + 1) * 128, h, :]), 16)
                    kq_fill[slot] = s_kq[slot].v

                st_mm = {}
                copy_done = {}
                exp_pred = {}
                pv_done = {}
                sq_done = {}
                c_done = {}
                row0_done = {}
                prep_norm_pe = {}
                prod_done = {}
                psN_free = [0, 0]
                state = {"psCE_free": 0}
                ctxs_sp = [0, 0]

                def prep_s1(h):
                    slot = h % 2

                    @blk.scalar
                    def _(s):
                        W(s, "s", s_kq[slot], kq_fill[slot])
                        if h >= 1:
                            W(s, "s", pe, prep_norm_pe[h - 1])
                        act.inc(s.activation(sqs[:, 0:S], kb[slot][:, :], AF.Square))
                        act.inc(s.activation(sqs[:, S:2 * S], qbb[slot][:, :], AF.Square))
                    sq_done[h] = act.v

                def prep_norm(h):
                    for i in range(16):
                        pn = psN[i % 2]

                        @blk.tensor
                        def _(t):
                            W(t, "t", act, sq_done[h])
                            W(t, "t", s_const, consts_qs)
                            W(t, "t", dve, psN_free[i % 2])
                            pe.inc(t.matmul(pn[0:1, :], ones_mat[:, 0:1],
                                            sqs[:, i * 512:(i + 1) * 512],
                                            start=True, stop=True))
                        nm = pe.v

                        @blk.vector
                        def _(v):
                            W(v, "v", pe, nm)
                            dve.inc(v.tensor_copy(nrm[0:1, i * 512:(i + 1) * 512],
                                                  pn[0:1, :]))
                        psN_free[i % 2] = dve.v
                    prep_norm_pe[h] = pe.v

                    @blk.vector
                    def _(v):
                        if h >= 1:
                            W(v, "v", act, c_done[h - 1])
                        dve.inc(v.reduce_max(nsc[0:1, 0:1], nrm[0:1, 0:S], axis=AX))
                        dve.inc(v.reduce_max(nsc[0:1, 1:2], nrm[0:1, S:2 * S], axis=AX))
                        W(v, "v", dve, dve.v)
                        dve.inc(v.tensor_mul(nsc[0:1, 2:3], nsc[0:1, 0:1],
                                             nsc[0:1, 1:2]))
                    prod_done[h] = dve.v

                def prep_s2(h):
                    @blk.scalar
                    def _(s):
                        W(s, "s", dve, prod_done[h])
                        act.inc(s.activation(nsc[0:1, 3:4], nsc[0:1, 2:3], AF.Ln))
                        W(s, "s", act, act.v)
                        act.inc(s.activation(nsc[0:1, 4:5], nsc[0:1, 3:4], AF.Exp,
                                             scale=0.5))
                    c_done[h] = act.v

                def prep_row0(h):
                    slot = h % 2

                    @blk.vector
                    def _(v):
                        W(v, "v", act, c_done[h])
                        W(v, "v", s_kq[slot], kq_fill[slot])
                        dve.inc(v.tensor_scalar(nsc[0:1, 5:6], nsc[0:1, 4:5], -CMARGIN,
                                                None, ALU.mult))
                        W(v, "v", dve, dve.v)
                        dve.inc(v.tensor_scalar(qbb[slot][0:1, :], onesw[0:1, :],
                                                nsc[0:1, 5:6], None, ALU.mult))
                    row0_done[h] = dve.v

                def emit_pvs(t, j0):
                    W(t, "t", act, exp_pred[j0 + 3])
                    for j in range(j0, j0 + 4):
                        jslot = (j // NQH) % 2
                        jkt = j % NT
                        if jkt == 0:
                            W(t, "t", dve, state["psCE_free"])
                        t.matmul(psC[:, :], vb[jslot][:, jkt * 128:(jkt + 1) * 128],
                                 prb[j % 8][:, :],
                                 start=(jkt == 0), stop=(jkt == NT - 1))
                        ins = t.matmul(psE[:, :], ones_mat[:, :], prb[j % 8][:, :],
                                       start=(jkt == 0), stop=(jkt == NT - 1))
                        if jkt == NT - 1:
                            pv_done[j] = pe.inc(ins)

                def emit_finalize(gqb):
                    cslot = gqb % 2
                    hh = gqb // NQB
                    q0f = (gqb % NQB) * QBW

                    @blk.vector
                    def _(v):
                        W(v, "v", pe, pv_done[gqb * NT + NT - 1])
                        W(v, "v", s_ctxw[cslot], ctxs_sp[cslot])
                        dve.inc(v.tensor_copy(se_sb[:, :], psE[:, :]))
                        dve.inc(v.tensor_copy(sc_sb[:, :], psC[:, :]))
                        W(v, "v", dve, dve.v)
                        dve.inc(v.reciprocal(rs_sb[:, :], se_sb[:, :]))
                        W(v, "v", dve, dve.v)
                        dve.inc(v.tensor_mul(ctxs[cslot][:, :], sc_sb[:, :],
                                             rs_sb[:, :]))
                    state["psCE_free"] = dve.v - 2
                    ctx_ready = dve.v

                    @blk.sync
                    def _(sy):
                        W(sy, "sy", dve, ctx_ready)
                        s_ctxw[cslot].inc(sy.dma_start(
                            csp[hh][:, q0f:q0f + QBW], ctxs[cslot][:, :]), 16)
                    ctxs_sp[cslot] = s_ctxw[cslot].v

                for h in range(NH):
                    slot = h % 2
                    if h == 0:
                        @blk.sync
                        def _(sy):
                            load_head(sy, 0)
                        prep_s1(0)
                        prep_norm(0)
                        prep_s2(0)
                        prep_row0(0)
                    for qb in range(NQB):
                        base = h * NQH + qb * NT
                        q0 = qb * QBW
                        a0 = act.v
                        for kt in range(NT):
                            exp_pred[base + kt] = a0 + 4 * (kt // 4 + 1)
                        for kq4 in range(8):
                            g0 = base + kq4 * 4
                            if kq4 == 2 and (h * NQB + qb) >= 1:
                                emit_finalize(h * NQB + qb - 1)

                            @blk.tensor
                            def _(t):
                                if g0 - 8 >= 0:
                                    emit_pvs(t, g0 - 8)
                                W(t, "t", dve, row0_done[h])
                                W(t, "t", s_kq[slot], kq_fill[slot])
                                if g0 - 1 >= 0:
                                    W(t, "t", dve, copy_done[g0 - 1])
                                ins = None
                                for x in range(4):
                                    gk = g0 + x
                                    kt_ = gk % NT
                                    ins = t.matmul(psS[gk % 4][:, :],
                                                   kb[slot][:, kt_ * 128:(kt_ + 1) * 128],
                                                   qbb[slot][:, q0:q0 + QBW],
                                                   start=True, stop=True)
                                pe.inc(ins)
                            smv = pe.v
                            st_mm[g0] = smv

                            @blk.vector
                            def _(v):
                                W(v, "v", pe, smv)
                                if debug and s_dbg.v:
                                    W(v, "v", s_dbg, s_dbg.v)
                                ins = None
                                for x in range(4):
                                    gk = g0 + x
                                    ins = v.tensor_copy(ssb[gk % 8][:, :],
                                                        psS[gk % 4][:, :])
                                dve.inc(ins)
                            for x in range(4):
                                copy_done[g0 + x] = dve.v

                            if debug and h == 0 and qb == 0:
                                @blk.gpsimd
                                def _(g):
                                    W(g, "g", dve, copy_done[g0])
                                    for x in range(4):
                                        gk = g0 + x
                                        s_dbg.inc(g.dma_start(sdbg[gk % NT],
                                                              ssb[gk % 8][:, :]), 16)
                                    W(g, "g", act, exp_pred[g0 + 3] - 4)
                                    if kq4 >= 1:
                                        for x in range(4):
                                            gk = g0 - 4 + x
                                            s_dbg.inc(g.dma_start(pdbg[gk % NT],
                                                                  prb[gk % 8][:, :]), 16)
                                dbg_wm[g0] = s_dbg.v

                            if qb == 0 and kq4 == 1 and h + 1 < NH:
                                @blk.sync
                                def _(sy):
                                    if h >= 1:
                                        W(sy, "sy", pe, pv_done[h * NQH - 1])
                                        W(sy, "sy", act, sq_done[h - 1])
                                        W(sy, "sy", dve, row0_done[h - 1])
                                    load_head(sy, h + 1)

                        @blk.scalar
                        def _(s):
                            for kq4 in range(8):
                                g0 = base + kq4 * 4
                                W(s, "s", dve, copy_done[g0 + 3])
                                if debug and g0 in dbg_wm:
                                    W(s, "s", s_dbg, dbg_wm[g0])
                                for x in range(4):
                                    gk = g0 + x
                                    act.inc(s.activation(prb[gk % 8][:, :],
                                                         ssb[gk % 8][:, :],
                                                         AF.Exp, scale=INVSQ))
                        assert act.v == a0 + NT
                        emit_stage0(8)
                        emit_stage1(4)

                        if h + 1 < NH:
                            if qb == 1:
                                prep_s1(h + 1)
                            elif qb == 2:
                                prep_norm(h + 1)
                            elif qb == 3:
                                prep_s2(h + 1)
                                prep_row0(h + 1)

                @blk.tensor
                def _(t):
                    emit_pvs(t, GTOT - 8)
                    emit_pvs(t, GTOT - 4)
                emit_finalize(NH * NQB - 1)

            barrier()

        if "p3" in phases:
            with ExitStack() as ps:
                e2 = ps.enter_context
                wo = e2(nc.sbuf_tensor("wo", [128, NHI * HID], BF16))
                bor = e2(nc.sbuf_tensor("bor", [1, HID], F32))
                bob = e2(nc.sbuf_tensor("bob", [128, HID], F32))
                ctxt = [e2(nc.sbuf_tensor(f"ctxt{i}", [128, NH * 128], BF16)) for i in range(2)]
                xres = [e2(nc.sbuf_tensor(f"xres{i}", [128, HID], F32)) for i in range(2)]
                xr = e2(nc.sbuf_tensor("xr", [128, HID], F32))
                ctr = e2(nc.sbuf_tensor("ctr", [128, HID], F32))
                sqb = e2(nc.sbuf_tensor("sqb", [128, HID], F32))
                mcol = e2(nc.sbuf_tensor("mcol", [128, 1], F32))
                vcol = e2(nc.sbuf_tensor("vcol", [128, 1], F32))
                rstd = e2(nc.sbuf_tensor("rstd", [128, 1], F32))
                x1f = [e2(nc.sbuf_tensor(f"x1f{i}", [128, 128], F32)) for i in range(2)]
                psA = e2(nc.psum_tensor("psA3", [128, 512], F32))
                psB = e2(nc.psum_tensor("psB3", [128, 512], F32))
                psT = e2(nc.psum_tensor("psT3", [128, 128], F32))
                s_ctxt = [bld.sem("s_ctxt_0"), bld.sem("s_ctxt_1")]
                s_x1w = [bld.sem("s_x1w_0"), bld.sem("s_x1w_1")]
                s_wo = bld.sem("s_wo")
                s_bo = bld.sem("s_bo")
                s_xres = [bld.sem("s_xres0"), bld.sem("s_xres1")]

                @blk.gpsimd
                def _(g):
                    for hi in range(NHI):
                        s_wo.inc(g.dma_start(wo[:, hi * HID:(hi + 1) * HID],
                                             W_O[hi * 128:(hi + 1) * 128, :]), 16)
                wo_qg = s_wo.v

                @blk.scalar
                def _(s):
                    s_bo.inc(s.dma_start(bor[:, :], b_O[None, :]), 16)
                bo_qa = s_bo.v

                bo_dve = []
                for i in range(4):
                    p_ = psA if i % 2 == 0 else psB

                    @blk.tensor
                    def _(t):
                        W(t, "t", s_bo, bo_qa)
                        W(t, "t", s_const, consts_qs)
                        if i >= 2:
                            W(t, "t", dve, bo_dve[i - 2])
                        pe.inc(t.matmul(p_[:, :], ones_rowf[:, :],
                                        bor[0:1, i * 512:(i + 1) * 512],
                                        start=True, stop=True))
                    mmv = pe.v

                    @blk.vector
                    def _(v):
                        W(v, "v", pe, mmv)
                        dve.inc(v.tensor_copy(bob[:, i * 512:(i + 1) * 512], p_[:, :]))
                        bo_dve.append(dve.v)
                bob_done = dve.v

                ctxt_fill = [0, 0]

                def load_ctxt(sy, tt):
                    slot = tt % 2
                    for h in range(NH):
                        s_ctxt[slot].inc(sy.dma_start(
                            ctxt[slot][:, h * 128:(h + 1) * 128],
                            csp[h][:, tt * 128:(tt + 1) * 128]), 16)
                    ctxt_fill[slot] = s_ctxt[slot].v

                @blk.sync
                def _(sy):
                    load_ctxt(sy, 0)

                xres_fill = [0, 0]

                @blk.scalar
                def _(s):
                    s_xres[0].inc(s.dma_start(xres[0][:, :], x_in[0:128, :]), 16)
                xres_fill[0] = s_xres[0].v

                psAB_free = [0, 0]
                ctxt_last_pe = [0, 0]
                xres_last_dve = [0, 0]
                x1w_sp = [0, 0]
                ln_ctr_free_pe = 0

                for tt in range(NT):
                    slot = tt % 2

                    @blk.sync
                    def _(sy):
                        if tt + 1 < NT:
                            W(sy, "sy", pe, ctxt_last_pe[(tt + 1) % 2])
                            load_ctxt(sy, tt + 1)

                    @blk.scalar
                    def _(s):
                        if tt + 1 < NT:
                            W(s, "s", dve, xres_last_dve[(tt + 1) % 2])
                            s_xres[(tt + 1) % 2].inc(s.dma_start(
                                xres[(tt + 1) % 2][:, :],
                                x_in[(tt + 1) * 128:(tt + 2) * 128, :]), 16)
                            xres_fill[(tt + 1) % 2] = s_xres[(tt + 1) % 2].v

                    for oc in range(4):
                        p_ = psA if oc % 2 == 0 else psB

                        @blk.tensor
                        def _(t):
                            W(t, "t", s_ctxt[slot], ctxt_fill[slot])
                            W(t, "t", s_wo, wo_qg)
                            W(t, "t", dve, bob_done)
                            W(t, "t", dve, psAB_free[oc % 2])
                            ins = None
                            for hh in range(NH):
                                ins = t.matmul(
                                    p_[:, :],
                                    ctxt[slot][:, hh * 128:(hh + 1) * 128],
                                    wo[:, hh * HID + oc * 512:hh * HID + (oc + 1) * 512],
                                    start=(hh == 0), stop=(hh == NH - 1))
                            pe.inc(ins)
                        mm_oc = pe.v
                        ctxt_last_pe[slot] = mm_oc

                        @blk.vector
                        def _(v):
                            W(v, "v", pe, mm_oc)
                            W(v, "v", dve, dve.v)
                            W(v, "v", dve, bob_done)
                            W(v, "v", s_xres[slot], xres_fill[slot])
                            dve.inc(v.tensor_add(xr[:, oc * 512:(oc + 1) * 512], p_[:, :],
                                                 bob[:, oc * 512:(oc + 1) * 512]))
                            W(v, "v", dve, dve.v)
                            dve.inc(v.tensor_add(xr[:, oc * 512:(oc + 1) * 512],
                                                 xr[:, oc * 512:(oc + 1) * 512],
                                                 xres[slot][:, oc * 512:(oc + 1) * 512]))
                        psAB_free[oc % 2] = dve.v
                    xres_last_dve[slot] = dve.v

                    @blk.vector
                    def _(v):
                        W(v, "v", dve, dve.v)
                        W(v, "v", pe, ln_ctr_free_pe)
                        dve.inc(v.reduce_sum(mcol[:, :], xr[:, :], axis=AX))
                        W(v, "v", dve, dve.v)
                        dve.inc(v.tensor_scalar(mcol[:, :], mcol[:, :],
                                                1.0 / HID, None, ALU.mult))
                        W(v, "v", dve, dve.v)
                        dve.inc(v.tensor_scalar(ctr[:, :], xr[:, :], mcol[:, 0:1],
                                                None, ALU.subtract))
                        W(v, "v", dve, dve.v)
                        dve.inc(v.tensor_mul(sqb[:, :], ctr[:, :], ctr[:, :]))
                        W(v, "v", dve, dve.v)
                        dve.inc(v.reduce_sum(vcol[:, :], sqb[:, :], axis=AX))
                        W(v, "v", dve, dve.v)
                        dve.inc(v.tensor_scalar(vcol[:, :], vcol[:, :],
                                                1.0 / HID, LN_EPS, ALU.mult, ALU.add))
                    var_done = dve.v

                    @blk.scalar
                    def _(s):
                        W(s, "s", dve, var_done)
                        act.inc(s.activation(rstd[:, :], vcol[:, :], AF.Ln))
                        W(s, "s", act, act.v)
                        act.inc(s.activation(rstd[:, :], rstd[:, :], AF.Exp, scale=-0.5))
                    rstd_done = act.v

                    @blk.vector
                    def _(v):
                        W(v, "v", act, rstd_done)
                        dve.inc(v.tensor_scalar(ctr[:, :], ctr[:, :], rstd[:, 0:1],
                                                None, ALU.mult))
                    x1_done = dve.v

                    psT_free = 0
                    for hi in range(NHI):
                        fslot = hi % 2

                        @blk.tensor
                        def _(t):
                            W(t, "t", dve, x1_done)
                            W(t, "t", dve, psT_free)
                            pe.inc(t.transpose(psT[:, :], ctr[:, hi * 128:(hi + 1) * 128],
                                               identf[:, :]))
                        tr_mm = pe.v

                        @blk.vector
                        def _(v):
                            W(v, "v", pe, tr_mm)
                            W(v, "v", s_x1w[fslot], x1w_sp[fslot])
                            dve.inc(v.tensor_copy(x1f[fslot][:, :], psT[:, :]))
                        psT_free = dve.v

                        @blk.sync
                        def _(sy):
                            W(sy, "sy", dve, psT_free)
                            s_x1w[fslot].inc(sy.dma_start(
                                x1sp[hi][:, tt * 128:(tt + 1) * 128], x1f[fslot][:, :]), 16)
                        x1w_sp[fslot] = s_x1w[fslot].v
                    ln_ctr_free_pe = pe.v

            barrier()

        emit_stage0(len(st0_jobs))
        emit_stage1(len(st1_jobs))
        if "p4" in phases:
            with ExitStack() as ps:
                e2 = ps.enter_context
                acc = e2(nc.sbuf_tensor("acc", [128, NHI * TC], F32))
                x1b = e2(nc.sbuf_tensor("x1b", [128, NHI * TC], BF16))
                w1b = [e2(nc.sbuf_tensor(f"w1b{i}", [128, NHI * 768], BF16)) for i in range(2)]
                w2b = [e2(nc.sbuf_tensor(f"w2b{i}", [128, 6 * HID], BF16)) for i in range(2)]
                actb = e2(nc.sbuf_tensor("actb", [128, NI * TC], BF16))
                tsil = [e2(nc.sbuf_tensor(f"tsil{i}", [128, TC], BF16)) for i in range(2)]
                gwT = e2(nc.sbuf_tensor("gwT", [128, NHI * EXP], BF16))
                probs = e2(nc.sbuf_tensor("probs", [128, EXP], F32))
                wrk = e2(nc.sbuf_tensor("wrk4", [128, EXP], F32))
                msk = e2(nc.sbuf_tensor("msk", [128, EXP], F32))
                sel = e2(nc.sbuf_tensor("sel", [128, EXP], F32))
                cmb = e2(nc.sbuf_tensor("cmb", [128, EXP], F32))
                cmb_bf = e2(nc.sbuf_tensor("cmb_bf", [128, EXP], BF16))
                cmbT = e2(nc.sbuf_tensor("cmbT", [16, TC], BF16))
                cbrow = e2(nc.sbuf_tensor("cbrow", [1, TC], BF16))
                cbb = e2(nc.sbuf_tensor("cbb", [128, TC], F32))
                m1c = e2(nc.sbuf_tensor("m1c", [128, 1], F32))
                negm = e2(nc.sbuf_tensor("negm", [128, 1], F32))
                smc = e2(nc.sbuf_tensor("smc", [128, 1], F32))
                rrc = e2(nc.sbuf_tensor("rrc", [128, 1], F32))
                otm = e2(nc.sbuf_tensor("otm", [128, HID], F32))
                fsq = e2(nc.sbuf_tensor("fsq", [128, HID], F32))
                fmc = e2(nc.sbuf_tensor("fmc", [128, 1], F32))
                fvc = e2(nc.sbuf_tensor("fvc", [128, 1], F32))
                frs = e2(nc.sbuf_tensor("frs", [128, 1], F32))
                psA = e2(nc.psum_tensor("psA4", [128, 512], F32))
                psB = e2(nc.psum_tensor("psB4", [128, 512], F32))
                psC = e2(nc.psum_tensor("psC4", [128, 512], F32))
                psD = e2(nc.psum_tensor("psD4", [128, 512], F32))
                psE = e2(nc.psum_tensor("psE4", [128, 512], F32))
                psT = e2(nc.psum_tensor("psT4", [128, 128], F32))
                psTb = e2(nc.psum_tensor("psTb4", [128, 128], BF16))
                s_cmb = bld.sem("s_cmb")
                s_gwT = bld.sem("s_gwT")
                s_acc = bld.sem("s_acc")
                s_cbr = bld.sem("s_cbr")

                @blk.sync
                def _(sy):
                    W(sy, "sy", s_scr, ffn_scratch_qg)
                    W(sy, "sy", s_tw[0], s_tw[0].v)
                    W(sy, "sy", s_tw[1], s_tw[1].v)
                    for hi in range(NHI):
                        s_gwT.inc(sy.dma_start(gwT[:, hi * EXP:(hi + 1) * EXP],
                                               gws[:, hi * 128:(hi + 1) * 128],
                                               transpose=True), 16)
                gwT_qs = s_gwT.v

                def w1_src_moe(ee):
                    return lambda r0, r1: w1t[ee, :, :, r0:r1]

                def w2_src_moe(ee):
                    return lambda i0, i1_: w2t[ee, :, i0:i1_, :]

                def w1_src_sh(off):
                    return lambda r0, r1: sh1t[:, :, off + r0:off + r1]

                def w2_src_sh(off):
                    return lambda i0, i1_: sh2t[:, off + i0:off + i1_, :]

                units = [dict(w1f=w1_src_moe(ee), w2f=w2_src_moe(ee), cb=ee)
                         for ee in range(EXP)]
                units.append(dict(w1f=w1_src_sh(0), w2f=w2_src_sh(0), cb=None))
                units.append(dict(w1f=w1_src_sh(NI * 256), w2f=w2_src_sh(NI), cb=None))
                if p4_units is not None:
                    units = units[:p4_units]
                nunits = len(units)
                SETS = [(0, 3), (3, 6), (6, 9), (9, 11)]
                W2G = [(0, 6), (6, 11)]

                w1_fill = [0, 0]
                w1_free = [0, 0]
                w2_fill = [0, 0]
                w2_free = [0, 0]
                acc_free_pe = 0
                x1b_free_pe = 0
                psE_free = ("x", 0)
                psA_free_act = 0
                psB_free_dve = 0
                psCD_free = [("x", 0), ("x", 0)]
                acc_w_dve = [0] * NHI
                cb_free_dve = 0
                silu_free = [0, 0]
                actb_used_pe = 0
                out_qs_prev = 0
                psTb_free = 0
                psT_free2 = 0

                def issue_w1(sy, u, si, slot):
                    p0_, p1_ = SETS[si]
                    W(sy, "sy", pe, w1_free[slot])
                    s_w1[slot].inc(sy.dma_start(
                        w1b[slot][:, 0:NHI * (p1_ - p0_) * 256],
                        units[u]["w1f"](p0_ * 256, p1_ * 256)), 16)
                    w1_fill[slot] = s_w1[slot].v

                def issue_w2(sy, u, gi, slot):
                    i0, i1_ = W2G[gi]
                    W(sy, "sy", pe, w2_free[slot])
                    s_w2[slot].inc(sy.dma_start(
                        w2b[slot][:, 0:(i1_ - i0) * HID],
                        units[u]["w2f"](i0, i1_)), 16)
                    w2_fill[slot] = s_w2[slot].v

                for ch in range(p4_chunks):
                    c0 = ch * TC

                    @blk.scalar
                    def _(s):
                        W(s, "s", pe, acc_free_pe)
                        for hi in range(NHI):
                            s_acc.inc(s.dma_start(acc[:, hi * TC:(hi + 1) * TC],
                                                  x1sp[hi][:, c0:c0 + TC]), 16)
                    acci_qa = s_acc.v

                    @blk.scalar
                    def _(s):
                        W(s, "s", s_acc, acci_qa)
                        W(s, "s", pe, x1b_free_pe)
                        for hi in range(NHI):
                            act.inc(s.activation(x1b[:, hi * TC:(hi + 1) * TC],
                                                 acc[:, hi * TC:(hi + 1) * TC], AF.Copy))
                    x1b_done = act.v

                    for tt in range(4):
                        @blk.tensor
                        def _(t):
                            W(t, "t", act, x1b_done)
                            W(t, "t", s_gwT, gwT_qs)
                            eng, val = psE_free
                            if eng == "s":
                                W(t, "t", act, val)
                            elif eng == "v":
                                W(t, "t", dve, val)
                            ins = None
                            for hi in range(NHI):
                                ins = t.matmul(
                                    psE[:, 0:EXP],
                                    x1b[:, hi * TC + tt * 128:hi * TC + (tt + 1) * 128],
                                    gwT[:, hi * EXP:(hi + 1) * EXP],
                                    start=(hi == 0), stop=(hi == NHI - 1))
                            pe.inc(ins)
                        lg_mm = pe.v

                        @blk.vector
                        def _(v):
                            W(v, "v", pe, lg_mm)
                            dve.inc(v.reduce_max(m1c[:, :], psE[:, 0:EXP], axis=AX))
                            W(v, "v", dve, dve.v)
                            dve.inc(v.tensor_scalar(negm[:, :], m1c[:, :], -1.0,
                                                    None, ALU.mult))
                        negm_done = dve.v

                        @blk.scalar
                        def _(s):
                            W(s, "s", dve, negm_done)
                            act.inc(s.activation(probs[:, :], psE[:, 0:EXP], AF.Exp,
                                                 bias=negm[:, 0:1]))
                        probs_done = act.v
                        psE_free = ("s", probs_done)

                        @blk.vector
                        def _(v):
                            W(v, "v", act, probs_done)
                            dve.inc(v.reduce_sum(smc[:, :], probs[:, :], axis=AX))
                            W(v, "v", dve, dve.v)
                            dve.inc(v.reciprocal(rrc[:, :], smc[:, :]))
                            W(v, "v", dve, dve.v)
                            dve.inc(v.tensor_scalar(probs[:, :], probs[:, :], rrc[:, 0:1],
                                                    None, ALU.mult))
                            W(v, "v", dve, dve.v)
                            dve.inc(v.tensor_copy(wrk[:, :], probs[:, :]))
                            dve.inc(v.memset(cmb[:, :], 0.0))
                            for _k in range(6):
                                W(v, "v", dve, dve.v)
                                dve.inc(v.reduce_max(m1c[:, :], wrk[:, :], axis=AX))
                                W(v, "v", dve, dve.v)
                                dve.inc(v.tensor_scalar(msk[:, :], wrk[:, :], m1c[:, 0:1],
                                                        None, ALU.is_ge))
                                W(v, "v", dve, dve.v)
                                dve.inc(v.tensor_mul(sel[:, :], wrk[:, :], msk[:, :]))
                                W(v, "v", dve, dve.v)
                                dve.inc(v.tensor_add(cmb[:, :], cmb[:, :], sel[:, :]))
                                dve.inc(v.tensor_sub(wrk[:, :], wrk[:, :], sel[:, :]))
                            W(v, "v", dve, dve.v)
                            dve.inc(v.tensor_copy(cmb_bf[:, :], cmb[:, :]))
                        cmb_done = dve.v

                        @blk.tensor
                        def _(t):
                            W(t, "t", dve, cmb_done)
                            W(t, "t", dve, psTb_free)
                            pe.inc(t.transpose(psTb[0:16, :], cmb_bf[:, :], identb[:, :]))
                        trc_mm = pe.v

                        @blk.vector
                        def _(v):
                            W(v, "v", pe, trc_mm)
                            dve.inc(v.tensor_copy(cmbT[0:16, tt * 128:(tt + 1) * 128],
                                                  psTb[0:16, :]))
                        psTb_free = dve.v
                    cmbT_done = dve.v

                    @blk.sync
                    def _(sy):
                        W(sy, "sy", dve, cmbT_done)
                        W(sy, "sy", s_cbr, cb_last_qa if ch > 0 else 0)
                        s_cmb.inc(sy.dma_start(cmbd[:, :], cmbT[:, :]), 16)
                        if nunits > 0 and ch == 0:
                            issue_w1(sy, 0, 0, 0)
                            issue_w1(sy, 0, 1, 1)
                            issue_w2(sy, 0, 0, 0)
                    cmb_sp = s_cmb.v

                    cb_done_act = act.v

                    for u in range(nunits):
                        cbi = units[u]["cb"]
                        if cbi is not None:
                            @blk.scalar
                            def _(s):
                                W(s, "s", s_cmb, cmb_sp)
                                W(s, "s", act, cb_done_act)
                                s_cbr.inc(s.dma_start(cbrow[:, :], cmbd[cbi:cbi + 1, :]), 16)
                            cbrow_qa = s_cbr.v
                            cb_last_qa = s_cbr.v

                            @blk.tensor
                            def _(t):
                                W(t, "t", s_cbr, cbrow_qa)
                                eng, val = psE_free
                                if eng == "s":
                                    W(t, "t", act, val)
                                elif eng == "v":
                                    W(t, "t", dve, val)
                                pe.inc(t.matmul(psE[:, :], ones_row[:, :], cbrow[0:1, :],
                                                start=True, stop=True))
                            cb_mm = pe.v

                            @blk.scalar
                            def _(s):
                                W(s, "s", pe, cb_mm)
                                W(s, "s", dve, cb_free_dve)
                                act.inc(s.activation(cbb[:, :], psE[:, :], AF.Copy))
                            cb_done_act = act.v
                            psE_free = ("s", cb_done_act)

                        for si in range(4):
                            sslot = si % 2
                            p0_, p1_ = SETS[si]
                            LBW = (p1_ - p0_) * 256
                            for j in range(p0_, p1_):
                                jl = j - p0_

                                @blk.tensor
                                def _(t):
                                    if j == p0_:
                                        W(t, "t", s_w1[sslot], w1_fill[sslot])
                                    W(t, "t", act, psA_free_act)
                                    ins = None
                                    for hi in range(NHI):
                                        ins = t.matmul(
                                            psA[:, :],
                                            w1b[sslot][:, hi * LBW + jl * 256:hi * LBW + jl * 256 + 128],
                                            x1b[:, hi * TC:(hi + 1) * TC],
                                            start=(hi == 0), stop=(hi == NHI - 1))
                                    pe.inc(ins)
                                    W(t, "t", dve, psB_free_dve)
                                    ins = None
                                    for hi in range(NHI):
                                        ins = t.matmul(
                                            psB[:, :],
                                            w1b[sslot][:, hi * LBW + jl * 256 + 128:hi * LBW + (jl + 1) * 256],
                                            x1b[:, hi * TC:(hi + 1) * TC],
                                            start=(hi == 0), stop=(hi == NHI - 1))
                                    pe.inc(ins)
                                gu_mm = pe.v
                                w1_free[sslot] = gu_mm

                                tslot = j % 2

                                @blk.scalar
                                def _(s):
                                    W(s, "s", pe, gu_mm - 1)
                                    W(s, "s", dve, silu_free[tslot])
                                    act.inc(s.activation(tsil[tslot][:, :], psA[:, :],
                                                         AF.Silu))
                                psA_free_act = act.v
                                silu_done = act.v

                                @blk.vector
                                def _(v):
                                    W(v, "v", act, silu_done)
                                    W(v, "v", pe, gu_mm)
                                    W(v, "v", pe, actb_used_pe)
                                    if cbi is not None:
                                        W(v, "v", act, cb_done_act)
                                        dve.inc(v.tensor_mul(actb[:, j * TC:(j + 1) * TC],
                                                             tsil[tslot][:, :], psB[:, :]))
                                        W(v, "v", dve, dve.v)
                                        dve.inc(v.tensor_mul(actb[:, j * TC:(j + 1) * TC],
                                                             actb[:, j * TC:(j + 1) * TC],
                                                             cbb[:, :]))
                                    else:
                                        dve.inc(v.tensor_mul(actb[:, j * TC:(j + 1) * TC],
                                                             tsil[tslot][:, :], psB[:, :]))
                                psB_free_dve = dve.v
                                silu_free[tslot] = dve.v

                            @blk.sync
                            def _(sy):
                                if si + 2 <= 3:
                                    issue_w1(sy, u, si + 2, (si + 2) % 2)
                                elif u + 1 < nunits:
                                    nsi = si + 2 - 4
                                    issue_w1(sy, u + 1, nsi, nsi % 2)
                                elif ch + 1 < p4_chunks:
                                    nsi = si + 2 - 4
                                    issue_w1(sy, 0, nsi, nsi % 2)
                        act_done_dve = dve.v
                        cb_free_dve = dve.v

                        for gi in range(2):
                            gslot = gi % 2
                            i0, i1_ = W2G[gi]
                            for ht in range(NHI):
                                pw = psC if ht % 2 == 0 else psD

                                @blk.tensor
                                def _(t):
                                    if ht == 0:
                                        W(t, "t", s_w2[gslot], w2_fill[gslot])
                                        W(t, "t", dve, act_done_dve)
                                    eng, val = psCD_free[ht % 2]
                                    if eng == "v":
                                        W(t, "t", dve, val)
                                    ins = None
                                    for ii in range(i0, i1_):
                                        ins = t.matmul(
                                            pw[:, :],
                                            w2b[gslot][:, (ii - i0) * HID + ht * 128:(ii - i0) * HID + (ht + 1) * 128],
                                            actb[:, ii * TC:(ii + 1) * TC],
                                            start=(ii == i0), stop=(ii == i1_ - 1))
                                    pe.inc(ins)
                                w2_mm = pe.v
                                w2_free[gslot] = w2_mm
                                if gi == 1:
                                    actb_used_pe = w2_mm

                                @blk.vector
                                def _(v):
                                    W(v, "v", pe, w2_mm)
                                    W(v, "v", dve, acc_w_dve[ht])
                                    W(v, "v", s_acc, acci_qa)
                                    dve.inc(v.tensor_add(acc[:, ht * TC:(ht + 1) * TC],
                                                         acc[:, ht * TC:(ht + 1) * TC],
                                                         pw[:, :]))
                                acc_w_dve[ht] = dve.v
                                psCD_free[ht % 2] = ("v", dve.v)

                            @blk.sync
                            def _(sy):
                                if gi + 1 < 2:
                                    issue_w2(sy, u, gi + 1, 1)
                                elif u + 1 < nunits:
                                    issue_w2(sy, u + 1, 0, 0)
                                elif ch + 1 < p4_chunks:
                                    issue_w2(sy, 0, 0, 0)

                    all_acc_dve = dve.v
                    for tt in range(4):
                        for ht in range(NHI):
                            @blk.tensor
                            def _(t):
                                W(t, "t", dve, all_acc_dve)
                                W(t, "t", dve, psT_free2)
                                pe.inc(t.transpose(
                                    psT[:, :],
                                    acc[:, ht * TC + tt * 128:ht * TC + (tt + 1) * 128],
                                    identf[:, :]))
                            trm = pe.v

                            @blk.vector
                            def _(v):
                                W(v, "v", pe, trm)
                                if ht == 0:
                                    W(v, "v", dve, dve.v)
                                    W(v, "v", s_out, out_qs_prev)
                                dve.inc(v.tensor_copy(otm[:, ht * 128:(ht + 1) * 128],
                                                      psT[:, :]))
                            psT_free2 = dve.v
                        otm_done = dve.v

                        @blk.vector
                        def _(v):
                            W(v, "v", dve, otm_done)
                            dve.inc(v.reduce_sum(fmc[:, :], otm[:, :], axis=AX))
                            W(v, "v", dve, dve.v)
                            dve.inc(v.tensor_scalar(fmc[:, :], fmc[:, :], 1.0 / HID,
                                                    None, ALU.mult))
                            W(v, "v", dve, dve.v)
                            dve.inc(v.tensor_scalar(otm[:, :], otm[:, :], fmc[:, 0:1],
                                                    None, ALU.subtract))
                            W(v, "v", dve, dve.v)
                            dve.inc(v.tensor_mul(fsq[:, :], otm[:, :], otm[:, :]))
                            W(v, "v", dve, dve.v)
                            dve.inc(v.reduce_sum(fvc[:, :], fsq[:, :], axis=AX))
                            W(v, "v", dve, dve.v)
                            dve.inc(v.tensor_scalar(fvc[:, :], fvc[:, :], 1.0 / HID,
                                                    LN_EPS, ALU.mult, ALU.add))
                        fvar_done = dve.v

                        @blk.scalar
                        def _(s):
                            W(s, "s", dve, fvar_done)
                            act.inc(s.activation(frs[:, :], fvc[:, :], AF.Ln))
                            W(s, "s", act, act.v)
                            act.inc(s.activation(frs[:, :], frs[:, :], AF.Exp, scale=-0.5))
                        frs_done = act.v

                        @blk.vector
                        def _(v):
                            W(v, "v", act, frs_done)
                            dve.inc(v.tensor_scalar(otm[:, :], otm[:, :], frs[:, 0:1],
                                                    None, ALU.mult))
                        out_done = dve.v

                        @blk.sync
                        def _(sy):
                            W(sy, "sy", dve, out_done)
                            s_out.inc(sy.dma_start(
                                y[c0 + tt * 128:c0 + (tt + 1) * 128, :], otm[:, :]), 16)
                        out_qs_prev = s_out.v
                    acc_free_pe = pe.v
                    x1b_free_pe = pe.v

        @blk.sync
        def _(sy):
            for sm in bld.all_sems:
                W(sy, "sy", sm, sm.v)

    return nc


_IN_NAMES = [
    "L0_attn_hidden_states", "L0_attn_W_DKV", "L0_attn_b_DKV", "L0_attn_W_UK",
    "L0_attn_b_UK", "L0_attn_W_UV", "L0_attn_b_UV", "L0_attn_W_DQ", "L0_attn_b_DQ",
    "L0_attn_W_UQ", "L0_attn_b_UQ", "L0_attn_W_QR", "L0_attn_W_KR", "L0_attn_W_O",
    "L0_attn_b_O", "L0_ffn_gate_w", "L0_ffn_w1", "L0_ffn_w2", "L0_ffn_shared_w1",
    "L0_ffn_shared_w2",
]


def kernel(**inputs):
    nc = build()
    x = np.ascontiguousarray(np.asarray(inputs["L0_attn_hidden_states"], np.float32))
    weights = {k: np.ascontiguousarray(np.asarray(inputs[k], np.float32))
               for k in _IN_NAMES if k != "L0_attn_hidden_states"}
    in_maps = []
    for b in range(NB):
        m = dict(weights)
        m["L0_attn_hidden_states"] = x[b]
        in_maps.append(m)
    res = run_bass_kernel_spmd(nc, in_maps, core_ids=list(range(NB)))
    return np.stack([np.asarray(res.results[b]["y"], np.float32) for b in range(NB)], axis=0)
